# version 07
# baseline (speedup 1.0000x reference)
import numpy as np
import ml_dtypes
from contextlib import ExitStack
import concourse.bass as bass
import concourse.mybir as mybir
from concourse.bass_utils import run_bass_kernel_spmd

F32 = mybir.dt.float32
BF16 = mybir.dt.bfloat16
AF = mybir.ActivationFunctionType
ALU = mybir.AluOpType
AX = mybir.AxisListType

NCORES = 8
T = 2048
D = 1024
NT = T // 128
NC = D // 128
EPS_LN = 1e-5

ENGINES = ("sync", "gpsimd", "tensor", "vector", "scalar")
CAST_SEM = {"gpsimd": "pool", "vector": "dve", "scalar": "act"}


class Prog:

    def __init__(self, nc):
        self.nc = nc
        self.streams = {e: [] for e in ENGINES}
        self.handles = {}
        self.counts = {}

    def make_sem(self, ctx, name):
        self.handles[name] = ctx.enter_context(self.nc.semaphore(name))
        self.counts[name] = 0
        return name

    def op(self, eng, fn, inc=()):
        inc = tuple(inc)
        h = self.handles

        def run(engine, fn=fn, inc=inc, h=h):
            inst = fn(engine)
            for s, n in inc:
                inst = inst.then_inc(h[s], n)
        self.streams[eng].append(run)
        for s, n in inc:
            self.counts[s] += n

    def wait(self, eng, semname, val=None):
        v = self.counts[semname] if val is None else val
        h = self.handles
        self.streams[eng].append(lambda engine, s=semname, v=v, h=h: engine.wait_ge(h[s], v))

    def count(self, s):
        return self.counts[s]

    def replay(self, block):
        for name in ENGINES:
            ops = self.streams[name]
            getattr(block, name)(lambda eng, ops=ops: [f(eng) for f in ops])


class Buffers:

    def __init__(self, nc, ctx):
        sb = lambda name, shape, dt: ctx.enter_context(nc.sbuf_tensor(name, shape, dt))
        self.x = sb("xres", [128, NT, 1024], F32)
        self.actA = sb("actA", [128, 16384], BF16)
        self.actB = sb("actB", [128, 16384], BF16)
        self.actC = sb("actC", [128, 16384], BF16)
        self.wA = sb("wA", [128, 8192], BF16)
        self.wB = sb("wB", [128, 4096], BF16)
        self.stage = sb("stage", [128, 1024], F32)
        self.scr = sb("scr", [128, 2048], F32)
        self.tmb = sb("tmb", [128, 1024], BF16)
        self.small = sb("small", [128, 720], F32)
        self.smallb = sb("smallb", [128, 256], BF16)
        self.rowb = sb("rowb", [128, 2048], BF16)
        self.fc = sb("fc", [128, 8], F32)
        self.onesb = sb("onesbf", [128, 128], BF16)
        self.ident = sb("ident", [128, 128], BF16)
        self.maskTm = sb("maskTm", [128, 128], BF16)
        self.maskFm = sb("maskFm", [128, 128], BF16)
        self.pb = [ctx.enter_context(nc.psum_tensor(f"pb{i}", [128, 512], F32)) for i in range(7)]
        self.pt = ctx.enter_context(nc.psum_tensor("pt", [128, 1024], BF16))

    def fm(self, t):
        return t.ap().rearrange("p (c n) -> p c n", c=8)

    def tm(self, t):
        return t.ap().rearrange("p (t d) -> p t d", t=16)


class Emit:

    def __init__(self, nc, P: Prog, B: Buffers, ctx):
        self.nc, self.P, self.B = nc, P, B
        for s in ("ld", "wq", "pe", "dve", "act", "pool", "out", "cc0", "cc1", "wb"):
            P.make_sem(ctx, s)
        self._small_off = 0
        self.bank_ready = [None] * 7
        self.pt_ready = None
        self.tmb_ready = None
        self._stage_par = 0
        self._stage_busy = [None, None]
        self._ring_busy = [None, None, None, None]
        self._cast_rot = 0
        self.wcache = None
        self._wc_map = {}
        self._wc_next = 0

    def small_col(self, n):
        o = self._small_off
        assert o + n <= 720, "small pool overflow"
        self._small_off += n
        return o

    def small_reset(self, keep=0):
        self._small_off = keep

    def pe_wait_bank(self, i):
        if self.bank_ready[i] is not None:
            s, v = self.bank_ready[i]
            self.P.wait("tensor", s, v)
            self.bank_ready[i] = None

    def bank_consumed(self, i, sem):
        self.bank_ready[i] = (sem, self.P.count(sem))

    def fconst(self, value):
        return self.B.fc[:, self._fcmap[value]:self._fcmap[value] + 1]

    def init_consts(self, extra_consts=()):
        P, B = self.P, self.B
        self._fcmap = {}
        vals = [EPS_LN] + list(extra_consts)
        for i, v in enumerate(vals):
            self._fcmap[v] = i
            P.op("gpsimd", lambda e, v=v, i=i: e.memset(B.fc[:, i:i + 1], v))
        P.op("gpsimd", lambda e: e.memset(B.onesb[:, :], 1.0))
        P.op("gpsimd", lambda e: e.memset(B.ident[:, :], 0.0))
        P.op("gpsimd", lambda e: e.affine_select(
            out=B.ident[:, :], in_=B.ident[:, :], compare_op=ALU.not_equal,
            fill=1.0, base=0, pattern=[[-1, 128]], channel_multiplier=1))
        P.op("gpsimd", lambda e: e.memset(B.maskTm[:, :], 0.0))
        P.op("gpsimd", lambda e: e.affine_select(
            out=B.maskTm[:, :], in_=B.maskTm[:, :], compare_op=ALU.is_ge,
            fill=-1e10, base=0, pattern=[[-1, 128]], channel_multiplier=1))
        P.op("gpsimd", lambda e: e.memset(B.maskFm[:, :], 0.0))
        P.op("gpsimd", lambda e: e.affine_select(
            out=B.maskFm[:, :], in_=B.maskFm[:, :], compare_op=ALU.is_ge,
            fill=-1e10, base=0, pattern=[[1, 128]], channel_multiplier=-1),
            inc=[("pool", 1)])

    def load_x(self, x_in):
        P, B = self.P, self.B
        xin = x_in.ap().rearrange("(t p) d -> p t d", p=128)
        P.op("sync", lambda e: e.dma_start(out=B.x[:, :, :], in_=xin), inc=[("ld", 16)])
        for e in ("vector", "scalar", "tensor", "gpsimd"):
            P.wait(e, "ld")
        for e in ("vector", "scalar", "tensor"):
            P.wait(e, "pool")

    def store_x(self, x_out):
        P, B = self.P, self.B
        xout = x_out.ap().rearrange("(t p) d -> p t d", p=128)
        P.wait("gpsimd", "dve")
        P.op("gpsimd", lambda e: e.dma_start(out=xout, in_=B.x[:, :, :]), inc=[("out", 16)])
        P.wait("gpsimd", "out")

    def emit_ln(self, dst=None):
        P, B = self.P, self.B
        dst = dst if dst is not None else B.actA
        dfm = B.fm(dst)
        o = self.small_col(5)
        negmean = B.small[:, o:o + 1]
        rstd = B.small[:, o + 1:o + 2]
        nmr = B.small[:, o + 2:o + 3]
        lnv = B.small[:, o + 3:o + 4]
        junk = B.small[:, o + 4:o + 5]
        P.wait("scalar", "dve")
        P.wait("tensor", "dve")
        for t in range(NT):
            xt = B.x[:, t, :]
            if t > 0:
                P.wait("vector", "act")
            P.op("vector", lambda e, xt=xt: e.tensor_scalar(
                out=B.scr[:, 1024:2048], in0=xt, scalar1=-1.0 / D, scalar2=None,
                op0=ALU.mult, op1=ALU.add, accum_out=negmean), inc=[("dve", 1)])
            P.wait("scalar", "dve")
            P.op("scalar", lambda e, xt=xt: e.activation(
                out=B.scr[:, 0:1024], in_=xt, func=AF.Square, bias=negmean, scale=1.0,
                accum_out=lnv))
            P.op("scalar", lambda e: e.activation(out=lnv, in_=lnv, func=AF.Ln,
                                                  bias=self.fconst(EPS_LN), scale=1.0 / D))
            P.op("scalar", lambda e: e.activation(out=rstd, in_=lnv, func=AF.Exp,
                                                  scale=-0.5), inc=[("act", 1)])
            P.wait("vector", "act")
            P.op("vector", lambda e: e.tensor_mul(nmr, negmean, rstd), inc=[("dve", 1)])
            P.wait("scalar", "dve")
            if self.tmb_ready is not None:
                P.wait("scalar", *self.tmb_ready)
                self.tmb_ready = None
            P.op("scalar", lambda e, xt=xt: e.activation(
                out=B.tmb[:, :], in_=xt, func=AF.Identity, bias=nmr, scale=rstd),
                inc=[("act", 1)])
            P.wait("tensor", "act")
            if self.pt_ready is not None:
                P.wait("tensor", *self.pt_ready)
                self.pt_ready = None
            for f in range(NC):
                P.op("tensor", lambda e, f=f: e.transpose(
                    B.pt[:, f * 128:(f + 1) * 128], B.tmb[:, f * 128:(f + 1) * 128],
                    B.ident[:, :]), inc=[("pe", 1)] if f == NC - 1 else ())
            self.tmb_ready = ("pe", P.count("pe"))
            P.wait("vector", "pe")
            ptv = B.pt.ap().rearrange("p (c n) -> p c n", c=8)
            P.op("vector", lambda e, t=t, ptv=ptv, dfm=dfm: e.tensor_copy(
                out=dfm[:, :, t * 128:(t + 1) * 128], in_=ptv), inc=[("dve", 1)])
            self.pt_ready = ("dve", P.count("dve"))
        self.small_reset()

    def load_cast(self, hbm_ap, dst_ap, cols, eng=None, cache_key=None, war=None):
        P, B = self.P, self.B
        assert cols <= 512, "load_cast piece too wide"
        import os
        if os.environ.get("MK_NOCACHE"):
            cache_key = None
        if cache_key is not None and cache_key in self._wc_map:
            idx, ready = self._wc_map[cache_key]
            P.wait("sync", *ready)
            for wmk in (war or []):
                P.wait("sync", *wmk)
            src = self.wcache.ap()[idx][:, 0:cols]
            P.op("sync", lambda e, dst_ap=dst_ap, src=src: e.dma_start(out=dst_ap, in_=src),
                 inc=[("wq", 16)])
            return ("wq", P.count("wq"))
        if eng is None:
            eng = "gpsimd"
        par = self._stage_par
        self._stage_par = (self._stage_par + 1) % 2
        st = B.stage[:, par * 512: par * 512 + cols]
        last = self._stage_busy[par]
        if last is not None:
            P.wait("sync", last[0], last[1])
        P.op("sync", lambda e, st=st, hbm_ap=hbm_ap: e.dma_start(out=st, in_=hbm_ap),
             inc=[("wq", 16)])
        P.wait(eng, "wq")
        for wmk in (war or []):
            P.wait(eng, *wmk)
        sem = CAST_SEM[eng]
        P.op(eng, lambda e, dst_ap=dst_ap, st=st: e.tensor_copy(out=dst_ap, in_=st),
             inc=[(sem, 1)])
        self._stage_busy[par] = (sem, P.count(sem))
        done = (sem, P.count(sem))
        if cache_key is not None:
            idx = self._wc_next
            self._wc_next += 1
            dst = self.wcache.ap()[idx][:, 0:cols]
            P.wait("gpsimd", sem, P.count(sem))
            P.op("gpsimd", lambda e, dst=dst, dst_ap=dst_ap: e.dma_start(out=dst, in_=dst_ap),
                 inc=[("wb", 16)])
            self._wc_map[cache_key] = (idx, ("wb", P.count("wb")))
        return done

    def wait_casts(self, engine):
        self.P.wait(engine, "pool")
        self.P.wait(engine, "dve")
        self.P.wait(engine, "wq")

    def load_w_fm(self, hbm, dst_view, kchunks=8, cols=1024, eng=None, ckpre=None):
        h = hbm.ap().rearrange("(k p) d -> p k d", p=128)
        done = None
        for k in range(kchunks):
            for half in range(0, cols, 512):
                wdt = min(512, cols - half)
                ck = (ckpre, k, half) if ckpre is not None else None
                done = self.load_cast(h[:, k, half:half + wdt],
                                      dst_view[:, k, half:half + wdt], wdt, eng=eng,
                                      cache_key=ck)
        return done

    def load_vec_ap(self, hbm_ap, dst_ap):
        P = self.P
        P.wait("sync", "dve")
        P.wait("sync", "act")
        P.wait("sync", "pool")
        P.op("sync", lambda e: e.dma_start(out=dst_ap, in_=hbm_ap), inc=[("wq", 16)])
        return ("wq", P.count("wq"))

    def load_vec(self, hbm, dst_ap):
        P = self.P
        P.wait("sync", "dve")
        P.wait("sync", "act")
        P.wait("sync", "pool")
        P.op("sync", lambda e: e.dma_start(out=dst_ap, in_=hbm.ap()), inc=[("wq", 16)])
        return ("wq", P.count("wq"))

    def emit_l1_ffn(self, W):
        P, B = self.P, self.B
        xn = B.fm(B.actA)
        xk = B.fm(B.actB)
        kact = B.fm(B.actC)
        o = self.small_col(16)
        mur = B.small[:, o:o + 8]
        muk = B.small[:, o + 8:o + 16]
        mv = self.load_vec(W["mu2"], B.small[:, o:o + 16])
        P.wait("vector", *mv)
        wAv = B.wA.ap().rearrange("p (k d) -> p k d", k=8)

        def token_mix(dst, mu):
            for c in range(NC):
                src = xn[:, c, :]
                P.op("vector", lambda e, src=src: e.tensor_tensor(
                    out=B.scr[:, 0:2047], in0=src[:, 1:2048], in1=src[:, 0:2047],
                    op=ALU.subtract))
                P.op("vector", lambda e, dst=dst, c=c, src=src, mu=mu: e.scalar_tensor_tensor(
                    out=dst[:, c, 1:2048], in0=B.scr[:, 0:2047], scalar=mu[:, c:c + 1],
                    in1=src[:, 0:2047], op0=ALU.mult, op1=ALU.add))
                P.op("vector", lambda e, dst=dst, c=c, src=src, mu=mu: e.tensor_scalar_mul(
                    dst[:, c, 0:1], src[:, 0:1], mu[:, c:c + 1]), inc=[("dve", 1)])

        token_mix(xk, muk)
        wdone = self.load_w_fm(W["WkT"], wAv)
        P.wait("tensor", *wdone)
        P.wait("tensor", "dve")
        par = 0
        first_evac = True
        for m in range(NC):
            for n in range(4):
                bank = par
                par ^= 1
                self.pe_wait_bank(bank)
                ps = B.pb[bank]
                for k in range(NC):
                    P.op("tensor", lambda e, ps=ps, k=k, m=m, n=n: e.matmul(
                        ps[:, 0:512], wAv[:, k, m * 128:(m + 1) * 128],
                        xk[:, k, n * 512:(n + 1) * 512],
                        start=(k == 0), stop=(k == NC - 1)),
                        inc=[("pe", 1)] if k == NC - 1 else ())
                P.wait("vector", "pe")
                if not first_evac:
                    P.wait("vector", "act")
                first_evac = False
                P.op("vector", lambda e, ps=ps: e.tensor_scalar_max(
                    B.scr[:, 0:512], ps[:, 0:512], 0.0), inc=[("dve", 1)])
                self.bank_consumed(bank, "dve")
                P.wait("scalar", "dve")
                P.op("scalar", lambda e, m=m, n=n: e.activation(
                    out=kact[:, m, n * 512:(n + 1) * 512], in_=B.scr[:, 0:512],
                    func=AF.Square), inc=[("act", 1)])
        P.wait("vector", "pe")
        xr = xk
        token_mix(xr, mur)
        wBv = B.wB.ap().rearrange("p (k d) -> p k d", k=8)
        wAv2 = wAv
        for dchunk in range(2):
            hv = W["WvT"].ap().rearrange("(k p) d -> p k d", p=128)
            hr = W["WrT"].ap().rearrange("(k p) d -> p k d", p=128)
            P.wait("gpsimd", "pe")
            dv = None
            for k in range(NC):
                dv = self.load_cast(hv[:, k, dchunk * 512:(dchunk + 1) * 512],
                                    wAv[:, k, 0:512], 512)
                dv = self.load_cast(hr[:, k, dchunk * 512:(dchunk + 1) * 512],
                                    wBv[:, k, 0:512], 512)
            P.wait("tensor", *dv)
            P.wait("tensor", "act")
            P.wait("tensor", "dve")
            for t in range(NT):
                bv = 0 if t % 2 == 0 else 1
                br = 2 if t % 2 == 0 else 3
                self.pe_wait_bank(bv)
                self.pe_wait_bank(br)
                psv, psr = B.pb[bv], B.pb[br]
                for k in range(NC):
                    P.op("tensor", lambda e, psv=psv, k=k, t=t: e.matmul(
                        psv[:, 0:512], kact[:, k, t * 128:(t + 1) * 128], wAv[:, k, 0:512],
                        start=(k == 0), stop=(k == NC - 1)))
                for k in range(NC):
                    P.op("tensor", lambda e, psr=psr, k=k, t=t: e.matmul(
                        psr[:, 0:512], xr[:, k, t * 128:(t + 1) * 128], wBv[:, k, 0:512],
                        start=(k == 0), stop=(k == NC - 1)),
                        inc=[("pe", 1)] if k == NC - 1 else ())
                P.wait("scalar", "pe")
                P.wait("scalar", "dve")
                P.op("scalar", lambda e, psr=psr: e.activation(
                    out=B.scr[:, 1024:1536], in_=psr[:, 0:512], func=AF.Sigmoid),
                    inc=[("act", 1)])
                self.bank_consumed(br, "act")
                P.wait("vector", "act")
                P.op("vector", lambda e, psv=psv: e.tensor_tensor(
                    out=B.scr[:, 0:512], in0=psv[:, 0:512], in1=B.scr[:, 1024:1536],
                    op=ALU.mult), inc=[("dve", 1)])
                self.bank_consumed(bv, "dve")
                xsl = B.x[:, t, dchunk * 512:(dchunk + 1) * 512]
                P.op("vector", lambda e, xsl=xsl: e.scalar_tensor_tensor(
                    out=xsl, in0=B.scr[:, 0:512], scalar=0.5, in1=xsl,
                    op0=ALU.mult, op1=ALU.add), inc=[("dve", 1)])
        self.small_reset()



def prep_l1_ffn(inp):
    t = lambda a: np.ascontiguousarray(np.asarray(a).T.astype(np.float32))
    vec = lambda v: np.ascontiguousarray(np.asarray(v, np.float32).reshape(8, 128).T)
    return {
        "l1f_WrT": t(inp["L1_ffn_W_r"]),
        "l1f_WkT": t(inp["L1_ffn_W_k"]),
        "l1f_WvT": t(inp["L1_ffn_W_v"]),
        "l1f_mu2": np.ascontiguousarray(np.concatenate(
            [vec(inp["L1_ffn_mu_r"]), vec(inp["L1_ffn_mu_k"])], 1)),
    }


def build_test_l1(weight_names):
    nc = bass.Bass("TRN2", target_bir_lowering=False)
    x_in = nc.dram_tensor("x", [T, D], F32, kind="ExternalInput")
    x_out = nc.dram_tensor("xout", [T, D], F32, kind="ExternalOutput")
    W = {n: nc.dram_tensor(f"l1f_{n}", s, F32, kind="ExternalInput")
         for n, s in [("WrT", [D, D]), ("WkT", [D, D]), ("WvT", [D, D]),
                      ("mu2", [128, 16])]}
    with ExitStack() as ctx:
        B = Buffers(nc, ctx)
        P = Prog(nc)
        E = Emit(nc, P, B, ctx)
        E.wcache = nc.dram_tensor("wcache", [1024, 128, 512], BF16, kind="Internal")
        E.init_consts()
        E.load_x(x_in)
        E.emit_ln()
        E.emit_l1_ffn(W)
        E.store_x(x_out)
        dbg1 = nc.dram_tensor("dbg1", [128, 2048], F32, kind="ExternalOutput")
        dbg2 = nc.dram_tensor("dbg2", [128, 16384], BF16, kind="ExternalOutput")
        P.op("gpsimd", lambda e: e.dma_start(out=dbg1.ap(), in_=B.scr[:, :]), inc=[("out", 16)])
        P.op("gpsimd", lambda e: e.dma_start(out=dbg2.ap(), in_=B.actA[:, :]), inc=[("out", 16)])
        P.wait("gpsimd", "out")
        block = ctx.enter_context(nc.Block())
        P.replay(block)
    return nc



def _emit_l3(self):
    pass


class EmitL3:
    pass


def emit_l3_build_M(E, W, M_dram):
    P, B = E.P, E.B
    PI = float(np.pi)
    aAv = B.fm(B.actA)
    aBv = B.fm(B.actB)
    aCv = B.fm(B.actC)
    for k in range(NC):
        ha = W["A"].ap().rearrange("(k p) d -> p k d", p=128)
        hb = W["Bi"].ap().rearrange("(k p) d -> p k d", p=128)
        for hf in range(2):
            E.load_cast(ha[:, k, hf * 512:(hf + 1) * 512],
                        aAv[:, k, hf * 512:(hf + 1) * 512], 512)
            wdone = E.load_cast(hb[:, k, hf * 512:(hf + 1) * 512],
                                aAv[:, k, 1024 + hf * 512:1024 + (hf + 1) * 512], 512)
    CRND = 12582912.0
    o = E.small_col(1)
    icol = B.small[:, o:o + 1]
    s0 = B.scr[:, 0:1024]
    s1 = B.scr[:, 1024:2048]
    for k in range(NC):
        P.wait("gpsimd", "act")
        P.op("gpsimd", lambda e: e.iota(s0, [[1, 1024]], base=0,
                                        channel_multiplier=0,
                                        allow_small_or_imprecise_dtypes=True))
        P.op("gpsimd", lambda e, k=k: e.iota(icol, [[1, 1]], base=k * 128,
                                             channel_multiplier=1,
                                             allow_small_or_imprecise_dtypes=True),
             inc=[("pool", 1)])
        P.wait("vector", "pool")
        P.op("vector", lambda e: e.tensor_scalar(
            out=s1, in0=s0, scalar1=icol, scalar2=2.0 ** -10,
            op0=ALU.mult, op1=ALU.mult))
        P.op("vector", lambda e: e.tensor_scalar(
            out=s0, in0=s1, scalar1=CRND, scalar2=CRND, op0=ALU.add, op1=ALU.subtract))
        P.op("vector", lambda e: e.tensor_tensor(
            out=s0, in0=s1, in1=s0, op=ALU.subtract), inc=[("dve", 1)])
        P.wait("scalar", "dve")
        P.op("scalar", lambda e, k=k: e.activation(
            out=aBv[:, k, 1024:2048], in_=s0, func=AF.Sin,
            scale=2 * float(np.pi)), inc=[("act", 1)])
        P.wait("vector", "act")
        P.op("vector", lambda e: e.tensor_scalar(out=s0, in0=s1, scalar1=0.25, scalar2=None,
                                                 op0=ALU.add))
        P.op("vector", lambda e: e.tensor_scalar(
            out=s1, in0=s0, scalar1=CRND, scalar2=CRND, op0=ALU.add, op1=ALU.subtract))
        P.op("vector", lambda e: e.tensor_tensor(
            out=s0, in0=s0, in1=s1, op=ALU.subtract), inc=[("dve", 1)])
        P.wait("scalar", "dve")
        P.op("scalar", lambda e, k=k: e.activation(
            out=aBv[:, k, 0:1024], in_=s0, func=AF.Sin,
            scale=2 * float(np.pi)), inc=[("act", 1)])
    E.small_reset()
    P.wait("tensor", *wdone)
    P.wait("tensor", "act")
    P.wait("tensor", "dve")
    for m in range(NC):
        for n in range(2):
            nsl = slice(n * 512, (n + 1) * 512)
            b0, b1 = (0, 1) if (m * 2 + n) % 2 == 0 else (2, 3)
            E.pe_wait_bank(b0)
            E.pe_wait_bank(b1)
            p0, p1 = B.pb[b0], B.pb[b1]
            for k in range(NC):
                P.op("tensor", lambda e, p0=p0, k=k, m=m, nsl=nsl: e.matmul(
                    p0[:, 0:512], aAv[:, k, m * 128:(m + 1) * 128],
                    aBv[:, k, nsl], start=(k == 0), stop=False))
            for k in range(NC):
                P.op("tensor", lambda e, p0=p0, k=k, m=m, nsl=nsl: e.matmul(
                    p0[:, 0:512], aAv[:, k, 1024 + m * 128:1024 + (m + 1) * 128],
                    aBv[:, k, 1024 + nsl.start:1024 + nsl.stop],
                    start=False, stop=(k == NC - 1)), inc=[("pe", 1)] if k == NC - 1 else ())
            for k in range(NC):
                P.op("tensor", lambda e, p1=p1, k=k, m=m, nsl=nsl: e.matmul(
                    p1[:, 0:512], aAv[:, k, 1024 + m * 128:1024 + (m + 1) * 128],
                    aBv[:, k, nsl], start=(k == 0), stop=(k == NC - 1)),
                    inc=[("pe", 1)] if k == NC - 1 else ())
            P.wait("scalar", "pe")
            P.op("scalar", lambda e, p0=p0, m=m, nsl=nsl: e.activation(
                out=aCv[:, m, nsl], in_=p0[:, 0:512], func=AF.Copy), inc=[("act", 1)])
            E.bank_consumed(b0, "act")
            b2 = 4 if (m * 2 + n) % 2 == 0 else 5
            E.pe_wait_bank(b2)
            p2 = B.pb[b2]
            for k in range(NC):
                P.op("tensor", lambda e, p2=p2, k=k, m=m, nsl=nsl: e.matmul(
                    p2[:, 0:512], aAv[:, k, m * 128:(m + 1) * 128],
                    aBv[:, k, 1024 + nsl.start:1024 + nsl.stop],
                    start=(k == 0), stop=(k == NC - 1)), inc=[("pe", 1)] if k == NC - 1 else ())
            P.wait("vector", "pe")
            P.wait("vector", "act")
            P.op("vector", lambda e, p1=p1: e.tensor_copy(out=B.scr[:, 0:512], in_=p1[:, 0:512]))
            P.op("vector", lambda e, p2=p2, m=m, nsl=nsl: e.tensor_tensor(
                out=aCv[:, m, 1024 + nsl.start:1024 + nsl.stop], in0=B.scr[:, 0:512],
                in1=p2[:, 0:512], op=ALU.subtract), inc=[("dve", 1)])
            E.bank_consumed(b1, "dve")
            E.bank_consumed(b2, "dve")
    P.wait("tensor", "act")
    P.wait("tensor", "dve")
    Mv = M_dram.ap()
    tmb_busy = [None, None]
    for m in range(NC):
        for n in range(2):
            nsl = slice(n * 512, (n + 1) * 512)
            b0, b1 = (0, 1) if (m * 2 + n) % 2 == 0 else (2, 3)
            E.pe_wait_bank(b0)
            E.pe_wait_bank(b1)
            p0, p1 = B.pb[b0], B.pb[b1]
            for k in range(NC):
                P.op("tensor", lambda e, p0=p0, k=k, m=m, nsl=nsl: e.matmul(
                    p0[:, 0:512], aCv[:, k, m * 128:(m + 1) * 128],
                    aBv[:, k, nsl], start=(k == 0), stop=(k == NC - 1)),
                    inc=[("pe", 1)] if k == NC - 1 else ())
            for k in range(NC):
                P.op("tensor", lambda e, p1=p1, k=k, m=m, nsl=nsl: e.matmul(
                    p1[:, 0:512], aCv[:, k, 1024 + m * 128:1024 + (m + 1) * 128],
                    aBv[:, k, 1024 + nsl.start:1024 + nsl.stop],
                    start=(k == 0), stop=(k == NC - 1)), inc=[("pe", 1)] if k == NC - 1 else ())
            P.wait("scalar", "pe")
            P.wait("scalar", "dve")
            par = (m * 2 + n) % 2
            P.op("scalar", lambda e, p0=p0: e.activation(
                out=B.scr[:, 1024:1536], in_=p0[:, 0:512], func=AF.Copy, scale=1.0 / 1024),
                inc=[("act", 1)])
            E.bank_consumed(b0, "act")
            P.wait("vector", "act")
            if tmb_busy[par] is not None:
                P.wait("vector", *tmb_busy[par])
            P.op("vector", lambda e, p1=p1, par=par: e.scalar_tensor_tensor(
                out=B.tmb[:, par * 512:(par + 1) * 512], in0=p1[:, 0:512],
                scalar=-1.0 / 1024, in1=B.scr[:, 1024:1536], op0=ALU.mult, op1=ALU.add),
                inc=[("dve", 1)])
            E.bank_consumed(b1, "dve")
            P.wait("gpsimd", "dve")
            P.op("gpsimd", lambda e, m=m, nsl=nsl, par=par: e.dma_start(
                out=Mv[m, :, nsl], in_=B.tmb[:, par * 512:(par + 1) * 512]),
                inc=[("out", 16)])
            tmb_busy[par] = ("out", P.count("out"))
    P.wait("gpsimd", "out")
    E.tmb_ready = None


def emit_l3_apply(E, W, M_dram, first_call):
    P, B = E.P, E.B
    xn = B.fm(B.actA)
    wAv = B.wA.ap().rearrange("p (k d) -> p k d", k=8)
    for sl in range(2):
        if E._stage_busy[sl] is not None:
            P.wait("sync", *E._stage_busy[sl])
            E._stage_busy[sl] = None
    P.op("sync", lambda e: e.dma_start(out=B.stage[0:1, 0:512], in_=W["bias"].ap()[:, 0:512]),
         inc=[("wq", 16)])
    P.op("sync", lambda e: e.dma_start(out=B.stage[0:1, 512:1024], in_=W["bias"].ap()[:, 512:1024]),
         inc=[("wq", 16)])
    P.wait("gpsimd", "wq")
    if E.tmb_ready is not None:
        P.wait("gpsimd", *E.tmb_ready)
        E.tmb_ready = None
    P.op("gpsimd", lambda e: e.tensor_copy(out=B.tmb[0:1, 0:512], in_=B.stage[0:1, 0:512]))
    P.op("gpsimd", lambda e: e.tensor_copy(out=B.tmb[0:1, 512:1024], in_=B.stage[0:1, 512:1024]),
         inc=[("pool", 1)])
    E._stage_busy[0] = ("pool", P.count("pool"))
    E._stage_busy[1] = ("pool", P.count("pool"))
    P.wait("tensor", "pool")
    P.wait("tensor", "dve")
    E.pe_wait_bank(4)
    E.pe_wait_bank(5)
    for h in range(2):
        P.op("tensor", lambda e, h=h: e.matmul(
            B.pb[4 + h][:, 0:512], B.onesb[0:1, :], B.tmb[0:1, h * 512:(h + 1) * 512],
            start=True, stop=True), inc=[("pe", 1)])
    P.wait("vector", "pe")
    for h in range(2):
        P.op("vector", lambda e, h=h: e.tensor_copy(
            out=B.scr[:, 1024 + h * 512:1024 + (h + 1) * 512], in_=B.pb[4 + h][:, 0:512]),
            inc=[("dve", 1)])
        E.bank_consumed(4 + h, "dve")
    Mv = M_dram.ap()
    for dchunk in range(2):
        P.wait("sync", "pe")
        dsl = slice(dchunk * 512, (dchunk + 1) * 512)
        mm = Mv[:, :, dsl].rearrange("k p n -> p k n")
        P.op("sync", lambda e, mm=mm: e.dma_start(out=wAv[:, :, 0:512], in_=mm),
             inc=[("wq", 16)])
        P.wait("tensor", "wq")
        for t in range(NT):
            bank = t % 2
            E.pe_wait_bank(bank)
            ps = B.pb[bank]
            for k in range(NC):
                P.op("tensor", lambda e, ps=ps, k=k, t=t: e.matmul(
                    ps[:, 0:512], xn[:, k, t * 128:(t + 1) * 128], wAv[:, k, 0:512],
                    start=(k == 0), stop=(k == NC - 1)), inc=[("pe", 1)] if k == NC - 1 else ())
            P.wait("vector", "pe")
            P.op("vector", lambda e, ps=ps, dchunk=dchunk: e.tensor_tensor(
                out=B.scr[:, 0:512], in0=ps[:, 0:512],
                in1=B.scr[:, 1024 + dchunk * 512:1024 + (dchunk + 1) * 512], op=ALU.add),
                inc=[("dve", 1)])
            E.bank_consumed(bank, "dve")
            xsl = B.x[:, t, dsl]
            P.op("vector", lambda e, xsl=xsl: e.scalar_tensor_tensor(
                out=xsl, in0=B.scr[:, 0:512], scalar=0.5, in1=xsl,
                op0=ALU.mult, op1=ALU.add), inc=[("dve", 1)])


def prep_l3_ffn(inp):
    return {
        "l3f_A": np.ascontiguousarray(np.asarray(inp["L3_ffn_w_freq_real"], np.float32)),
        "l3f_Bi": np.ascontiguousarray(np.asarray(inp["L3_ffn_w_freq_imag"], np.float32)),
        "l3f_bias": np.ascontiguousarray(np.asarray(inp["L3_ffn_bias"], np.float32).reshape(1, 1024)),
    }


def build_test_l3(_):
    nc = bass.Bass("TRN2", target_bir_lowering=False)
    x_in = nc.dram_tensor("x", [T, D], F32, kind="ExternalInput")
    x_out = nc.dram_tensor("xout", [T, D], F32, kind="ExternalOutput")
    W = {"A": nc.dram_tensor("l3f_A", [D, D], F32, kind="ExternalInput"),
         "Bi": nc.dram_tensor("l3f_Bi", [D, D], F32, kind="ExternalInput"),
         "bias": nc.dram_tensor("l3f_bias", [1, D], F32, kind="ExternalInput")}
    M_dram = nc.dram_tensor("l3f_M", [8, 128, 1024], BF16, kind="ExternalOutput")
    with ExitStack() as ctx:
        B = Buffers(nc, ctx)
        P = Prog(nc)
        E = Emit(nc, P, B, ctx)
        E.wcache = nc.dram_tensor("wcache", [1024, 128, 512], BF16, kind="Internal")
        E.init_consts(extra_consts=[-float(np.pi)])
        E.load_x(x_in)
        emit_l3_build_M(E, W, M_dram)
        E.emit_ln()
        emit_l3_apply(E, W, M_dram, True)
        E.store_x(x_out)
        dbg1 = nc.dram_tensor("dbg1", [128, 2048], F32, kind="ExternalOutput")
        dbg2 = nc.dram_tensor("dbg2", [128, 16384], BF16, kind="ExternalOutput")
        P.op("gpsimd", lambda e: e.dma_start(out=dbg1.ap(), in_=B.scr[:, :]), inc=[("out", 16)])
        P.op("gpsimd", lambda e: e.dma_start(out=dbg2.ap(), in_=B.actA[:, :]), inc=[("out", 16)])
        P.wait("gpsimd", "out")
        block = ctx.enter_context(nc.Block())
        P.replay(block)
    return nc


def emit_bias_bc(E, hbm_row, dst_off, banks=(2, 3)):
    P, B = E.P, E.B
    for sl in range(2):
        if E._stage_busy[sl] is not None:
            P.wait("sync", *E._stage_busy[sl])
            E._stage_busy[sl] = None
    P.op("sync", lambda e: e.dma_start(out=B.stage[0:1, 0:512], in_=hbm_row.ap()[:, 0:512]),
         inc=[("wq", 16)])
    P.op("sync", lambda e: e.dma_start(out=B.stage[0:1, 512:1024], in_=hbm_row.ap()[:, 512:1024]),
         inc=[("wq", 16)])
    P.wait("gpsimd", "wq")
    P.wait("gpsimd", "pe")
    P.op("gpsimd", lambda e: e.tensor_copy(out=B.rowb[0:1, 0:512], in_=B.stage[0:1, 0:512]))
    P.op("gpsimd", lambda e: e.tensor_copy(out=B.rowb[0:1, 512:1024], in_=B.stage[0:1, 512:1024]),
         inc=[("pool", 1)])
    E._stage_busy[0] = ("pool", P.count("pool"))
    E._stage_busy[1] = ("pool", P.count("pool"))
    P.wait("tensor", "pool")
    for i, bk in enumerate(banks):
        E.pe_wait_bank(bk)
        P.op("tensor", lambda e, i=i, bk=bk: e.matmul(
            B.pb[bk][:, 0:512], B.onesb[0:1, :], B.rowb[0:1, i * 512:(i + 1) * 512],
            start=True, stop=True), inc=[("pe", 1)])
    P.wait("vector", "pe")
    for i, bk in enumerate(banks):
        P.op("vector", lambda e, i=i, bk=bk: e.tensor_copy(
            out=B.scr[:, dst_off + i * 512:dst_off + (i + 1) * 512], in_=B.pb[bk][:, 0:512]),
            inc=[("dve", 1)])
        E.bank_consumed(bk, "dve")


def emit_attn(E, cfg, W, stats_dram):
    P, B = E.P, E.B
    causal, l3 = cfg["causal"], cfg["l3"]
    qscale = cfg["qscale"]
    has_b = cfg["has_qkv_bias"]
    xn = B.fm(B.actA)
    vtm = B.tm(B.actB)
    wAf = B.wA.ap()
    qaug = [wAf[:, 0:2048], wAf[:, 2048:4096]]
    kaug = [wAf[:, 4096:6144], wAf[:, 6144:8192]]
    wBf = B.wB.ap()
    qsl = [wBf[:, 0:512].rearrange("p (k j) -> p k j", k=8),
           wBf[:, 1024:1536].rearrange("p (k j) -> p k j", k=8)]
    ksl = [wBf[:, 512:1024].rearrange("p (k j) -> p k j", k=8),
           wBf[:, 1536:2048].rearrange("p (k j) -> p k j", k=8)]
    ptile = [wBf[:, 2048:2560], wBf[:, 2560:3072]]
    sqb = wBf[:, 3072:3584]
    NH = 16

    o = E.small_col(52)
    bq8 = B.small[:, o:o + 8]
    bk0 = B.small[:, o + 8:o + 16]
    nm4 = B.small[:, o + 16:o + 20]
    nth = B.small[:, o + 20:o + 36]
    hi32 = B.small[:, o + 36:o + 52]
    o15 = E.small_col(4)
    nm4b = B.small[:, o15:o15 + 4]
    o16 = E.small_col(32)
    nthq = B.small[:, o16:o16 + 16]
    lh16 = B.small[:, o16 + 16:o16 + 32]
    o2 = E.small_col(12 + 32)
    qscrow = B.small[0:1, o2:o2 + 16]
    kscrow = B.small[0:1, o2 + 16:o2 + 32]

    bias_ready = None
    if has_b:
        d1 = E.load_vec(W["bqk"], B.small[:, o:o + 16])
        bias_ready = ("wq-scalar", d1)
    if l3:
        P.wait("sync", "dve")
        P.wait("sync", "act")
        P.op("sync", lambda e: e.dma_start(out=qscrow, in_=W["qsc"].ap()), inc=[("wq", 16)])
        P.op("sync", lambda e: e.dma_start(out=kscrow, in_=W["ksc"].ap()), inc=[("wq", 16)])
        P.wait("vector", "wq")

    if cfg["has_v_bias"]:
        emit_bias_bc(E, W["bv"], 0)
    wAv_big = wAf.rearrange("p (k d) -> p k d", k=8)
    hv = W["Wv"].ap().rearrange("(k p) d -> p k d", p=128)
    for dchunk in range(2):
        dv = None
        for k in range(NC):
            dv = E.load_cast(hv[:, k, dchunk * 512:(dchunk + 1) * 512],
                             wAv_big[:, k, 0:512] if dchunk == 0 else wAv_big[:, k, 512:1024],
                             512)
        P.wait("tensor", *dv)
        for t in range(NT):
            bank = t % 2
            E.pe_wait_bank(bank)
            ps = B.pb[bank]
            wcol = 0 if dchunk == 0 else 512
            for k in range(NC):
                P.op("tensor", lambda e, ps=ps, k=k, t=t, wcol=wcol: e.matmul(
                    ps[:, 0:512], xn[:, k, t * 128:(t + 1) * 128],
                    wAv_big[:, k, wcol:wcol + 512],
                    start=(k == 0), stop=(k == NC - 1)), inc=[("pe", 1)] if k == NC - 1 else ())
            dst = vtm[:, t, dchunk * 512:(dchunk + 1) * 512]
            if l3:
                P.wait("scalar", "pe")
                P.op("scalar", lambda e, ps=ps, dst=dst: e.activation(
                    out=dst, in_=ps[:, 0:512], func=AF.Silu), inc=[("act", 1)])
                E.bank_consumed(bank, "act")
            elif cfg["has_v_bias"]:
                P.wait("vector", "pe")
                P.op("vector", lambda e, ps=ps, dst=dst, dchunk=dchunk: e.tensor_tensor(
                    out=dst, in0=ps[:, 0:512], in1=B.scr[:, dchunk * 512:(dchunk + 1) * 512],
                    op=ALU.add), inc=[("dve", 1)])
                E.bank_consumed(bank, "dve")
            else:
                P.wait("vector", "pe")
                P.op("vector", lambda e, ps=ps, dst=dst: e.tensor_copy(
                    out=dst, in_=ps[:, 0:512]), inc=[("dve", 1)])
                E.bank_consumed(bank, "dve")

    v_done = ("dve", P.count("dve")) if not l3 else ("act", P.count("act"))
    P.wait("gpsimd", "pe")
    for par in range(2):
        P.op("gpsimd", lambda e, par=par: e.memset(kaug[par][64:66, :], 1.0),
             inc=[("pool", 1)])
    P.wait("tensor", "pool")

    aug_busy = [None, None]
    pt_busy = [None, None]
    slice_busy = [None, None]
    qk_done = [None] * NH
    stats_dma = [None] * NH
    Wqh, Wkh = W["Wqh"].ap(), W["Wkh"].ap()

    qk_ld = [None] * NH

    def qk_loads(h):
        par = h % 2
        if slice_busy[par] is not None:
            P.wait("gpsimd", *slice_busy[par])
        done = None
        for which in ("q", "k"):
            hb = (Wqh if which == "q" else Wkh)[h].rearrange("p kk j -> p (kk j)")
            dstsl = (qsl if which == "q" else ksl)[par]
            done = E.load_cast(hb, dstsl.rearrange("p k j -> p (k j)"), 512)
        qk_ld[h] = done

    def qk_proj(h):
        par = h % 2
        P.wait("tensor", *qk_ld[h])
        if aug_busy[par] is not None:
            P.wait("tensor", *aug_busy[par])
        if bias_ready is not None:
            P.wait("scalar", *bias_ready[1])
        for which in ("q", "k"):
            dst = (qaug if which == "q" else kaug)[par]
            wsl = (qsl if which == "q" else ksl)[par]
            bias = (bq8 if which == "q" else bk0) if has_b else None
            scale = 1.0
            func = AF.Silu if l3 else AF.Identity
            for n in range(4):
                bank = n % 2
                E.pe_wait_bank(bank)
                ps = B.pb[bank]
                for k in range(NC):
                    P.op("tensor", lambda e, ps=ps, k=k, n=n, wsl=wsl: e.matmul(
                        ps[0:64, 0:512], wsl[:, k, :], xn[:, k, n * 512:(n + 1) * 512],
                        start=(k == 0), stop=(k == NC - 1)),
                        inc=[("pe", 1)] if k == NC - 1 else ())
                P.wait("scalar", "pe")
                kw = dict(func=func, scale=scale)
                if bias is not None:
                    kw["bias"] = bias[(h % 2) * 64:(h % 2) * 64 + 64, h // 2:h // 2 + 1]
                P.op("scalar", lambda e, ps=ps, dst=dst, n=n, kw=kw: e.activation(
                    out=dst[0:64, n * 512:(n + 1) * 512], in_=ps[0:64, 0:512], **kw),
                    inc=[("act", 1)])
                E.bank_consumed(bank, "act")
        slice_busy[par] = ("pe", P.count("pe"))
        if l3:
            for which in ("q", "k"):
                dst = (qaug if which == "q" else kaug)[par]
                scrw = qscrow if which == "q" else kscrow
                for n in range(4):
                    P.wait("scalar", "dve")
                    P.wait("scalar", "pe")
                    P.op("scalar", lambda e, dst=dst, n=n: e.activation(
                        out=sqb[0:64, 0:512], in_=dst[0:64, n * 512:(n + 1) * 512],
                        func=AF.Square), inc=[("act", 1)])
                    bank = 4 + n % 2
                    E.pe_wait_bank(bank)
                    P.wait("tensor", "act")
                    P.op("tensor", lambda e, bank=bank: e.matmul(
                        B.pb[bank][0:1, 0:512], B.onesb[0:64, 0:1], sqb[0:64, 0:512],
                        start=True, stop=True), inc=[("pe", 1)])
                    P.wait("vector", "pe")
                    P.op("vector", lambda e, bank=bank, n=n: e.tensor_copy(
                        out=B.scr[0:1, n * 512:(n + 1) * 512], in_=B.pb[bank][0:1, 0:512]),
                        inc=[("dve", 1)])
                    E.bank_consumed(bank, "dve")
                P.wait("scalar", "dve")
                P.wait("vector", "pe")
                for half in (0, 1):
                    r2h = B.scr[0:1, half * 1024:half * 1024 + 1024]
                    rbh = B.rowb[0:1, half * 1024:half * 1024 + 1024]
                    P.op("scalar", lambda e, r2h=r2h: e.activation(
                        out=r2h, in_=r2h, func=AF.Ln))
                    P.op("scalar", lambda e, r2h=r2h: e.activation(
                        out=r2h, in_=r2h, func=AF.Exp, scale=0.5), inc=[("act", 1)])
                    P.wait("vector", "act")
                    P.op("vector", lambda e, r2h=r2h, rbh=rbh, scrw=scrw, h=h:
                         e.tensor_scalar(out=rbh, in0=r2h, scalar1=scrw[0:1, h:h + 1],
                                         scalar2=None, op0=ALU.mult))
                    P.op("vector", lambda e, r2h=r2h: e.tensor_scalar(
                        out=r2h, in0=r2h, scalar1=1e-6, scalar2=None, op0=ALU.add))
                    P.op("vector", lambda e, r2h=r2h: e.reciprocal(r2h, r2h))
                    P.op("vector", lambda e, r2h=r2h, rbh=rbh: e.tensor_tensor(
                        out=rbh, in0=rbh, in1=r2h, op=ALU.mult), inc=[("dve", 1)])
                for n in range(4):
                    bank = 4 + n % 2
                    E.pe_wait_bank(bank)
                    P.wait("tensor", "dve")
                    P.op("tensor", lambda e, bank=bank, n=n: e.matmul(
                        B.pb[bank][0:64, 0:512], B.onesb[0:1, 0:64],
                        B.rowb[0:1, n * 512:(n + 1) * 512], start=True, stop=True),
                        inc=[("pe", 1)])
                    P.wait("vector", "pe")
                    P.op("vector", lambda e, dst=dst, bank=bank, n=n: e.tensor_tensor(
                        out=dst[0:64, n * 512:(n + 1) * 512],
                        in0=dst[0:64, n * 512:(n + 1) * 512],
                        in1=B.pb[bank][0:64, 0:512], op=ALU.mult), inc=[("dve", 1)])
                    E.bank_consumed(bank, "dve")
            qk_done[h] = ("dve", P.count("dve"))
        else:
            qk_done[h] = ("act", P.count("act"))

    def pass_a(h):
        par = h % 2
        P.wait("tensor", *qk_done[h])
        for i in range(NT):
            kl = (i + 1) * 128 if causal else T
            nch = (kl + 511) // 512
            for c in range(nch):
                w = min(512, kl - c * 512)
                E.pe_wait_bank(c)
                P.op("tensor", lambda e, c=c, par=par, i=i, w=w: e.matmul(
                    B.pb[c][:, 0:w], qaug[par][0:64, i * 128:(i + 1) * 128],
                    kaug[par][0:64, c * 512:c * 512 + w], start=True, stop=True),
                    inc=[("pe", 1)])
            P.wait("vector", "pe")
            if causal:
                c = (i * 128) // 512
                off = i * 128 - c * 512
                P.op("vector", lambda e, c=c, off=off: e.tensor_tensor(
                    out=B.pb[c][:, off:off + 128], in0=B.pb[c][:, off:off + 128],
                    in1=B.maskTm[:, :], op=ALU.add))
            for c in range(nch):
                w = min(512, kl - c * 512)
                P.op("vector", lambda e, c=c, w=w: e.reduce_max(
                    nm4[:, c:c + 1], B.pb[c][:, 0:w], axis=AX.X, negate=True),
                    inc=[("dve", 1)])
            P.wait("gpsimd", "dve")
            P.op("gpsimd", lambda e, nch=nch: e.tensor_copy(out=nm4b[:, 0:nch], in_=nm4[:, 0:nch]),
                 inc=[("pool", 1)])
            P.wait("vector", "pool")
            P.op("vector", lambda e, i=i, nch=nch: e.tensor_reduce(
                nth[:, i:i + 1], nm4b[:, 0:nch], axis=AX.X, op=ALU.min),
                inc=[("dve", 1)])
            P.wait("gpsimd", "dve")
            P.op("gpsimd", lambda e, i=i: e.tensor_scalar(
                out=nthq[:, i:i + 1], in0=nth[:, i:i + 1], scalar1=qscale, scalar2=None,
                op0=ALU.mult), inc=[("pool", 1)])
            P.wait("scalar", "pool")
            for c in range(nch):
                w = min(512, kl - c * 512)
                P.op("scalar", lambda e, c=c, w=w, i=i: e.activation(
                    out=B.tmb[:, 0:w], in_=B.pb[c][:, 0:w], func=AF.Exp,
                    bias=nthq[:, i:i + 1], scale=qscale, accum_out=nm4[:, c:c + 1]),
                    inc=[("act", 1)])
                E.bank_consumed(c, "act")
            P.wait("vector", "act")
            P.op("vector", lambda e, i=i, nch=nch: e.reduce_sum(
                lh16[:, i:i + 1], nm4[:, 0:nch], axis=AX.X), inc=[("dve", 1)])
        P.wait("scalar", "dve")
        P.op("scalar", lambda e: e.activation(out=lh16[:, :], in_=lh16[:, :], func=AF.Ln),
             inc=[("act", 1)])
        P.wait("vector", "act")
        P.op("vector", lambda e: e.scalar_tensor_tensor(
            out=nth[:, :], in0=lh16[:, :], scalar=-1.0 / qscale, in1=nth[:, :],
            op0=ALU.mult, op1=ALU.add), inc=[("dve", 1)])

    nthl = B.smallb[:, 0:32].rearrange("p (t s) -> p t s", s=2)
    flipst = B.smallb[:, 32:160]

    def stat_flip(h):
        par = h % 2
        P.wait("gpsimd", "dve")
        P.op("gpsimd", lambda e: e.tensor_copy(out=B.scr[:, 96:112], in_=nth[:, :]),
             inc=[("pool", 1)])
        P.wait("vector", "pool")
        P.op("vector", lambda e: e.tensor_copy(out=nthl[:, :, 0:1], in_=B.scr[:, 96:112]),
             inc=[("dve", 1)])
        P.wait("gpsimd", "dve")
        P.op("gpsimd", lambda e: e.tensor_copy(out=hi32, in_=nthl[:, :, 0:1]),
             inc=[("pool", 1)])
        P.wait("vector", "pool")
        P.op("vector", lambda e: e.tensor_tensor(out=nthl[:, :, 1:2],
                                                 in0=B.scr[:, 96:112],
                                                 in1=hi32, op=ALU.subtract), inc=[("dve", 1)])
        P.wait("tensor", "dve")
        if E.pt_ready is not None:
            P.wait("tensor", *E.pt_ready)
            E.pt_ready = None
        P.op("tensor", lambda e: e.transpose(B.pt[0:32, 0:128], B.smallb[:, 0:32],
                                             B.ident[:, :]), inc=[("pe", 1)])
        P.wait("vector", "pe")
        P.wait("vector", "out")
        P.op("vector", lambda e: e.tensor_copy(out=flipst[0:32, :], in_=B.pt[0:32, 0:128]),
             inc=[("dve", 1)])
        E.pt_ready = ("dve", P.count("dve"))
        P.wait("gpsimd", "dve")
        P.op("gpsimd", lambda e, h=h: e.dma_start(out=stats_dram.ap()[h], in_=flipst[0:32, :]),
             inc=[("out", 16)])
        P.wait("gpsimd", "out")
        if aug_busy[par] is not None:
            P.wait("gpsimd", *aug_busy[par])
        src = stats_dram.ap()[h].rearrange("(t s) q -> s t q", s=2)
        dstap = qaug[par][64:66, :].rearrange("p (t q) -> p t q", t=16)
        P.op("gpsimd", lambda e, dstap=dstap, src=src: e.dma_start(
            out=dstap, in_=src), inc=[("out", 16)])
        stats_dma[h] = P.count("out")

    def pass_b(h):
        par = h % 2
        P.wait("tensor", *v_done)
        P.wait("tensor", "out", stats_dma[h])
        sidx = 0
        jlast = lambda c: min(NT - 1, 4 * c + 3) if causal else NT - 1
        for j in range(NT):
            c0 = (j * 128) // 512 if causal else 0
            for c in range(c0, 4):
                qs = max(j * 128, c * 512) if causal else c * 512
                w = (c + 1) * 512 - qs
                slot = sidx % 2
                sidx += 1
                bank = 4 + slot
                E.pe_wait_bank(bank)
                P.op("tensor", lambda e, bank=bank, par=par, j=j, qs=qs, w=w: e.matmul(
                    B.pb[bank][:, 0:w], kaug[par][0:66, j * 128:(j + 1) * 128],
                    qaug[par][0:66, qs:qs + w], start=True, stop=True), inc=[("pe", 1)])
                dsem = "pe"
                if causal and qs == j * 128:
                    P.wait("vector", "pe")
                    P.op("vector", lambda e, bank=bank: e.tensor_tensor(
                        out=B.pb[bank][:, 0:128], in0=B.pb[bank][:, 0:128],
                        in1=B.maskFm[:, :], op=ALU.add), inc=[("dve", 1)])
                    dsem = "dve"
                P.wait("scalar", dsem)
                if pt_busy[slot] is not None:
                    P.wait("scalar", *pt_busy[slot])
                P.op("scalar", lambda e, bank=bank, slot=slot, w=w: e.activation(
                    out=ptile[slot][:, 0:w], in_=B.pb[bank][:, 0:w], func=AF.Exp,
                    scale=qscale), inc=[("act", 1)])
                E.bank_consumed(bank, "act")
                P.wait("tensor", "act")
                pvb = c
                if j == 0:
                    E.pe_wait_bank(pvb)
                st = (j == 0)
                sp = (j == jlast(c))
                P.op("tensor", lambda e, pvb=pvb, j=j, h=h, slot=slot, qs=qs, c=c, w=w,
                     st=st, sp=sp: e.matmul(
                        B.pb[pvb][0:64, qs - c * 512:qs - c * 512 + w],
                        vtm[:, j, h * 64:(h + 1) * 64], ptile[slot][:, 0:w],
                        start=st, stop=sp), inc=[("pe", 1)])
                pt_busy[slot] = ("pe", P.count("pe"))
        aoc = B.actC.ap().rearrange("p (c n) -> p c n", c=8)
        rb = (h % 2) * 64
        P.wait("scalar", "pe")
        for c in range(4):
            P.op("scalar", lambda e, c=c, h=h, rb=rb: e.copy(
                out=aoc[rb:rb + 64, h // 2, c * 512:(c + 1) * 512],
                in_=B.pb[c][0:64, 0:512]), inc=[("act", 1)])
            E.bank_consumed(c, "act")
        aug_busy[par] = ("pe", P.count("pe"))

    if cfg.get("v_only"):
        E.small_reset()
        return
    qk_loads(0)
    qk_proj(0)
    qk_loads(1)
    for h in range(NH):
        pass_a(h)
        stat_flip(h)
        if h + 2 < NH:
            qk_loads(h + 2)
        if h + 1 < NH:
            qk_proj(h + 1)
        pass_b(h)

    if cfg["has_o_bias"]:
        emit_bias_bc(E, W["bo"], 1024)
    aofm = B.fm(B.actC)
    ho = W["Wo"].ap().rearrange("(k p) d -> p k d", p=128)
    P.wait("tensor", "dve")
    P.wait("gpsimd", "pe")
    for dchunk in range(2):
        dv = None
        for k in range(NC):
            dv = E.load_cast(ho[:, k, dchunk * 512:(dchunk + 1) * 512],
                             wAv_big[:, k, (dchunk % 2) * 512:(dchunk % 2) * 512 + 512], 512)
        P.wait("tensor", *dv)
        wcol = (dchunk % 2) * 512
        for t in range(NT):
            bank = t % 2
            E.pe_wait_bank(bank)
            ps = B.pb[bank]
            for k in range(NC):
                P.op("tensor", lambda e, ps=ps, k=k, t=t, wcol=wcol: e.matmul(
                    ps[:, 0:512], aofm[:, k, t * 128:(t + 1) * 128],
                    wAv_big[:, k, wcol:wcol + 512],
                    start=(k == 0), stop=(k == NC - 1)), inc=[("pe", 1)] if k == NC - 1 else ())
            P.wait("vector", "pe")
            xsl = B.x[:, t, dchunk * 512:(dchunk + 1) * 512]
            if cfg["has_o_bias"]:
                P.op("vector", lambda e, ps=ps, dchunk=dchunk: e.tensor_tensor(
                    out=B.scr[:, 0:512], in0=ps[:, 0:512],
                    in1=B.scr[:, 1024 + dchunk * 512:1024 + (dchunk + 1) * 512], op=ALU.add))
                P.op("vector", lambda e, xsl=xsl: e.tensor_tensor(
                    out=xsl, in0=xsl, in1=B.scr[:, 0:512], op=ALU.add), inc=[("dve", 1)])
            else:
                P.op("vector", lambda e, ps=ps, xsl=xsl: e.tensor_tensor(
                    out=xsl, in0=xsl, in1=ps[:, 0:512], op=ALU.add), inc=[("dve", 1)])
            E.bank_consumed(bank, "dve")
    E.small_reset()


def prep_attn(inp, layer):
    f32 = lambda a: np.asarray(a, np.float32)
    if layer == 0:
        Wq, Wk, Wv, Wo = (f32(inp["L0_attn_W_q"]), f32(inp["L0_attn_W_k"]),
                          f32(inp["L0_attn_W_v"]), f32(inp["L0_attn_W_o"]))
        bq, bk, bv, bo = (f32(inp["L0_attn_b_q"]), f32(inp["L0_attn_b_k"]),
                          f32(inp["L0_attn_b_v"]), f32(inp["L0_attn_b_o"]))
    elif layer == 1:
        w_in = f32(inp["L1_attn_w_in"])
        Wq, Wk, Wv = w_in[0:1024].T, w_in[1024:2048].T, w_in[2048:3072].T
        Wo = f32(inp["L1_attn_w_out"]).T
        b_in = f32(inp["L1_attn_b_in"])
        bq, bk, bv = b_in[0:1024], b_in[1024:2048], b_in[2048:3072]
        bo = f32(inp["L1_attn_b_out"])
    elif layer == 2:
        Wq, Wk, Wv = f32(inp["L2_attn_Wq"]).T, f32(inp["L2_attn_Wk"]).T, f32(inp["L2_attn_Wv"]).T
        Wo = f32(inp["L2_attn_Wo"]).T
        bq, bk, bv, bo = (f32(inp["L2_attn_bq"]), f32(inp["L2_attn_bk"]),
                          f32(inp["L2_attn_bv"]), f32(inp["L2_attn_bo"]))
    else:
        Wq, Wk, Wv = f32(inp["L3_attn_Wq"]).T, f32(inp["L3_attn_Wk"]).T, f32(inp["L3_attn_Wv"]).T
        Wo = f32(inp["L3_attn_Wo"]).T
        bq = bk = bv = bo = None
    pre = f"a{layer}_"
    d = {}
    slices = lambda Wx: np.ascontiguousarray(
        Wx.reshape(8, 128, 16, 64).transpose(2, 1, 0, 3))
    d[pre + "Wqh"] = slices(Wq)
    d[pre + "Wkh"] = slices(Wk)
    d[pre + "Wv"] = np.ascontiguousarray(Wv)
    d[pre + "Wo"] = np.ascontiguousarray(Wo)
    if bq is not None:
        vec = lambda v: v.reshape(8, 128).T
        d[pre + "bqk"] = np.ascontiguousarray(np.concatenate([vec(bq), vec(bk)], 1))
        d[pre + "bv"] = np.ascontiguousarray(bv.reshape(1, 1024))
        d[pre + "bo"] = np.ascontiguousarray(bo.reshape(1, 1024))
    if layer == 3:
        d[pre + "qsc"] = np.ascontiguousarray(f32(inp["L3_attn_q_scale"]).reshape(1, 16))
        d[pre + "ksc"] = np.ascontiguousarray(f32(inp["L3_attn_k_scale"]).reshape(1, 16))
    return d


ATTN_CFGS = {
    0: dict(causal=True, l3=False, qscale=0.125, has_qkv_bias=True, has_v_bias=True, has_o_bias=True),
    1: dict(causal=True, l3=False, qscale=0.125, has_qkv_bias=True, has_v_bias=True, has_o_bias=True),
    2: dict(causal=False, l3=False, qscale=0.125, has_qkv_bias=True, has_v_bias=True, has_o_bias=True),
    3: dict(causal=True, l3=True, qscale=1.0, has_qkv_bias=False, has_v_bias=False, has_o_bias=False),
}


def declare_attn_w(nc, layer):
    pre = f"a{layer}_"
    W = {"Wqh": nc.dram_tensor(pre + "Wqh", [16, 128, 8, 64], F32, kind="ExternalInput"),
         "Wkh": nc.dram_tensor(pre + "Wkh", [16, 128, 8, 64], F32, kind="ExternalInput"),
         "Wv": nc.dram_tensor(pre + "Wv", [D, D], F32, kind="ExternalInput"),
         "Wo": nc.dram_tensor(pre + "Wo", [D, D], F32, kind="ExternalInput")}
    if ATTN_CFGS[layer]["has_qkv_bias"]:
        W["bqk"] = nc.dram_tensor(pre + "bqk", [128, 16], F32, kind="ExternalInput")
        W["bv"] = nc.dram_tensor(pre + "bv", [1, D], F32, kind="ExternalInput")
        W["bo"] = nc.dram_tensor(pre + "bo", [1, D], F32, kind="ExternalInput")
    if layer == 3:
        W["qsc"] = nc.dram_tensor(pre + "qsc", [1, 16], F32, kind="ExternalInput")
        W["ksc"] = nc.dram_tensor(pre + "ksc", [1, 16], F32, kind="ExternalInput")
    return W


def build_test_attn(layer):
    nc = bass.Bass("TRN2", target_bir_lowering=False)
    x_in = nc.dram_tensor("x", [T, D], F32, kind="ExternalInput")
    x_out = nc.dram_tensor("xout", [T, D], F32, kind="ExternalOutput")
    W = declare_attn_w(nc, layer)
    st = nc.dram_tensor("attn_st", [16, 32, 128], BF16, kind="ExternalOutput")
    with ExitStack() as ctx:
        B = Buffers(nc, ctx)
        P = Prog(nc)
        E = Emit(nc, P, B, ctx)
        E.wcache = nc.dram_tensor("wcache", [1024, 128, 512], BF16, kind="Internal")
        E.init_consts()
        E.load_x(x_in)
        E.emit_ln()
        emit_attn(E, ATTN_CFGS[layer], W, st)
        E.store_x(x_out)
        dbg2 = nc.dram_tensor("dbg2", [128, 16384], BF16, kind="ExternalOutput")
        dbg3 = nc.dram_tensor("dbg3", [128, 16384], BF16, kind="ExternalOutput")
        dbg4 = nc.dram_tensor("dbg4", [128, 720], F32, kind="ExternalOutput")
        P.op("gpsimd", lambda e: e.dma_start(out=dbg2.ap(), in_=B.actC[:, :]), inc=[("out", 16)])
        P.op("gpsimd", lambda e: e.dma_start(out=dbg3.ap(), in_=B.actB[:, :]), inc=[("out", 16)])
        P.op("gpsimd", lambda e: e.dma_start(out=dbg4.ap(), in_=B.small[:, :]), inc=[("out", 16)])
        P.wait("gpsimd", "out")
        block = ctx.enter_context(nc.Block())
        P.replay(block)
    return nc



def emit_moe_gating_softmax(E, W, probs_off, has_bias):
    P, B = E.P, E.B
    xn = B.fm(B.actA)
    wBf = B.wB.ap()
    gwv = wBf[:, 3584:3648].rearrange("p (k e) -> p k e", k=8)
    dv = E.load_cast(W["gw"].ap(), wBf[:, 3584:3648], 64)
    gbrow = B.smallb[0:1, 160:168]
    if has_bias:
        if E._stage_busy[0] is not None:
            P.wait("sync", *E._stage_busy[0])
            E._stage_busy[0] = None
        P.wait("sync", "dve")
        P.wait("sync", "act")
        P.op("sync", lambda e: e.dma_start(out=B.stage[0:1, 0:8], in_=W["gb"].ap()),
             inc=[("wq", 16)])
        P.wait("gpsimd", "wq")
        P.op("gpsimd", lambda e: e.tensor_copy(out=gbrow, in_=B.stage[0:1, 0:8]),
             inc=[("pool", 1)])
        E._stage_busy[0] = ("pool", P.count("pool"))
        P.wait("tensor", "pool")
    o = E.small_col(36)
    nm16 = B.small[:, o:o + 16]
    s16 = B.small[:, o + 16:o + 32]
    P.wait("tensor", *dv)
    for t in range(NT):
        bank = 4 + t % 2
        E.pe_wait_bank(bank)
        ps = B.pb[bank]
        for k in range(NC):
            P.op("tensor", lambda e, ps=ps, t=t, k=k: e.matmul(
                ps[:, 0:8], xn[:, k, t * 128:(t + 1) * 128], gwv[:, k, :],
                start=(k == 0), stop=(k == NC - 1) if not has_bias else False),
                inc=[("pe", 1)] if (k == NC - 1 and not has_bias) else ())
        if has_bias:
            P.op("tensor", lambda e, ps=ps, t=t: e.matmul(
                ps[:, 0:8], B.onesb[0:1, 0:128], gbrow,
                start=False, stop=True), inc=[("pe", 1)])
        P.wait("vector", "pe")
        P.op("vector", lambda e, ps=ps, t=t: e.reduce_max(
            nm16[:, t:t + 1], ps[:, 0:8], axis=AX.X, negate=True), inc=[("dve", 1)])
        P.wait("scalar", "dve")
        pr = B.small[:, probs_off + t * 8:probs_off + t * 8 + 8]
        P.op("scalar", lambda e, ps=ps, t=t, pr=pr: e.activation(
            out=pr, in_=ps[:, 0:8], func=AF.Exp, bias=nm16[:, t:t + 1],
            accum_out=s16[:, t:t + 1]), inc=[("act", 1)])
        E.bank_consumed(bank, "act")
    P.wait("vector", "act")
    P.op("vector", lambda e: e.reciprocal(nm16[:, 0:16], s16[:, 0:16]), inc=[("dve", 1)])
    P.wait("gpsimd", "dve")
    P.op("gpsimd", lambda e: e.tensor_copy(out=s16[:, 0:16], in_=nm16[:, 0:16]),
         inc=[("pool", 1)])
    P.wait("vector", "pool")
    for t in range(NT):
        pr = B.small[:, probs_off + t * 8:probs_off + t * 8 + 8]
        P.op("vector", lambda e, pr=pr, t=t: e.tensor_scalar(
            out=pr, in0=pr, scalar1=s16[:, t:t + 1], scalar2=None, op0=ALU.mult),
            inc=[("dve", 1)] if t == NT - 1 else ())
    return o


def emit_top2(E, probs_off, h1_off, p2_off, o_stats):
    P, B = E.P, E.B
    nm16 = B.small[:, o_stats:o_stats + 16]
    s16 = B.small[:, o_stats + 16:o_stats + 32]
    for t in range(NT):
        pr = B.small[:, probs_off + t * 8:probs_off + t * 8 + 8]
        P.op("vector", lambda e, pr=pr, t=t: e.reduce_max(
            nm16[:, t:t + 1], pr, axis=AX.X), inc=[("dve", 1)] if t == NT - 1 else ())
    P.wait("gpsimd", "dve")
    P.op("gpsimd", lambda e: e.tensor_copy(out=s16[:, 0:16], in_=nm16[:, 0:16]),
         inc=[("pool", 1)])
    P.wait("vector", "pool")
    for t in range(NT):
        pr = B.small[:, probs_off + t * 8:probs_off + t * 8 + 8]
        h1 = B.small[:, h1_off + t * 8:h1_off + t * 8 + 8]
        P.op("vector", lambda e, pr=pr, h1=h1, t=t: e.tensor_scalar(
            out=h1, in0=pr, scalar1=s16[:, t:t + 1], scalar2=None, op0=ALU.is_ge))
    for t in range(NT):
        pr = B.small[:, probs_off + t * 8:probs_off + t * 8 + 8]
        h1 = B.small[:, h1_off + t * 8:h1_off + t * 8 + 8]
        p2 = B.small[:, p2_off + t * 8:p2_off + t * 8 + 8]
        P.op("vector", lambda e, pr=pr, h1=h1, p2=p2: e.scalar_tensor_tensor(
            out=p2, in0=h1, scalar=-2.0, in1=pr, op0=ALU.mult, op1=ALU.add))
    for t in range(NT):
        p2 = B.small[:, p2_off + t * 8:p2_off + t * 8 + 8]
        P.op("vector", lambda e, p2=p2, t=t: e.reduce_max(
            s16[:, t:t + 1], p2, axis=AX.X), inc=[("dve", 1)] if t == NT - 1 else ())
    P.wait("gpsimd", "dve")
    P.op("gpsimd", lambda e: e.tensor_copy(out=nm16[:, 0:16], in_=s16[:, 0:16]),
         inc=[("pool", 1)])
    P.wait("vector", "pool")
    for t in range(NT):
        p2 = B.small[:, p2_off + t * 8:p2_off + t * 8 + 8]
        P.op("vector", lambda e, p2=p2, t=t: e.tensor_scalar(
            out=p2, in0=p2, scalar1=nm16[:, t:t + 1], scalar2=None, op0=ALU.is_ge),
            inc=[("dve", 1)] if t == NT - 1 else ())


def emit_l2_ffn(E, W):
    P, B = E.P, E.B
    xn = B.fm(B.actA)
    h1f = B.fm(B.actB)
    h2f = B.fm(B.actC)
    wAv = B.wA.ap().rearrange("p (k d) -> p k d", k=8)
    wBf = B.wB.ap()
    ring = [wBf[:, i * 1024:(i + 1) * 1024].rearrange("p (k j) -> p k j", k=8)
            for i in range(4)]
    probs_off = E.small_col(128)
    h1_off = E.small_col(128)
    h2_off = E.small_col(128)
    cw_off = E.small_col(128)
    o_stats = emit_moe_gating_softmax(E, W, probs_off, has_bias=True)
    emit_top2(E, probs_off, h1_off, h2_off, o_stats)
    for t in range(NT):
        pr = B.small[:, probs_off + t * 8:probs_off + t * 8 + 8]
        h1 = B.small[:, h1_off + t * 8:h1_off + t * 8 + 8]
        h2 = B.small[:, h2_off + t * 8:h2_off + t * 8 + 8]
        cw = B.small[:, cw_off + t * 8:cw_off + t * 8 + 8]
        P.op("vector", lambda e, pr=pr, h1=h1, cw=cw, t=t: e.tensor_scalar(
            out=cw, in0=h1, scalar1=pr[:, 0:1], scalar2=0.5, op0=ALU.mult, op1=ALU.mult))
    for t in range(NT):
        pr = B.small[:, probs_off + t * 8:probs_off + t * 8 + 8]
        h2 = B.small[:, h2_off + t * 8:h2_off + t * 8 + 8]
        cw = B.small[:, cw_off + t * 8:cw_off + t * 8 + 8]
        P.op("vector", lambda e, pr=pr, h2=h2, cw=cw, t=t: e.tensor_scalar(
            out=B.small[:, h1_off + t * 8:h1_off + t * 8 + 8], in0=h2,
            scalar1=pr[:, 1:2], scalar2=0.5, op0=ALU.mult, op1=ALU.mult))
    for t in range(NT):
        cw = B.small[:, cw_off + t * 8:cw_off + t * 8 + 8]
        t1 = B.small[:, h1_off + t * 8:h1_off + t * 8 + 8]
        P.op("vector", lambda e, cw=cw, t1=t1: e.tensor_tensor(
            out=cw, in0=cw, in1=t1, op=ALU.add), inc=[("dve", 1)] if t == NT - 1 else ())
    cw_ready = ("dve", P.count("dve"))
    lvl = W.get("_lvl", 4)
    if lvl <= 1:
        E.small_reset()
        return

    w1h, w2h, w3h = W["w1"].ap(), W["w2"].ap(), W["w3"].ap()
    b12h, b3h = W["b12"].ap(), W["b3"].ap()
    o_b = E.small_col(16)
    b1c = B.small[:, o_b:o_b + 8]
    b2c = B.small[:, o_b + 8:o_b + 16]
    for e_i in range(8):
        db = E.load_vec_ap(b12h[e_i], B.small[:, o_b:o_b + 16])
        for sl in range(2):
            if E._stage_busy[sl] is not None:
                P.wait("sync", *E._stage_busy[sl])
                E._stage_busy[sl] = None
        P.wait("sync", "pe")
        P.op("sync", lambda e, e_i=e_i: e.dma_start(out=B.stage[0:1, 0:512],
                                                    in_=b3h[e_i][:, 0:512]), inc=[("wq", 16)])
        P.op("sync", lambda e, e_i=e_i: e.dma_start(out=B.stage[0:1, 512:1024],
                                                    in_=b3h[e_i][:, 512:1024]), inc=[("wq", 16)])
        P.wait("gpsimd", "wq")
        P.op("gpsimd", lambda e: e.tensor_copy(out=B.rowb[0:1, 0:512], in_=B.stage[0:1, 0:512]))
        P.op("gpsimd", lambda e: e.tensor_copy(out=B.rowb[0:1, 512:1024],
                                               in_=B.stage[0:1, 512:1024]), inc=[("pool", 1)])
        E._stage_busy[0] = ("pool", P.count("pool"))
        E._stage_busy[1] = ("pool", P.count("pool"))
        b3_ready = ("pool", P.count("pool"))
        P.wait("gpsimd", "pe")
        P.wait("gpsimd", "wb")
        P.wait("sync", "pe")
        P.wait("sync", "wb")
        dv3 = None
        for k in range(NC):
            for hf in range(2):
                dv3 = E.load_cast(w3h[e_i].rearrange("(k p) o -> p k o", p=128)[:, k, hf * 512:(hf + 1) * 512],
                                  wAv[:, k, hf * 512:(hf + 1) * 512], 512,
                                  cache_key=("l2", "w3", e_i, k, hf))
        P.wait("scalar", *db)
        for half in range(2):
            tok0 = half * 1024
            stages = ((xn, h1f, b1c, w1h),) if lvl == 2 else ((xn, h1f, b1c, w1h), (h1f, h2f, b2c, w2h))
            for isrc, (src, dstf, bcol, wh) in enumerate(stages):
                for m in range(8):
                    sl = ring[m % 4]
                    war = E._ring_busy[m % 4] or []
                    E._ring_busy[m % 4] = None
                    hbmsl = wh[e_i, m].rearrange("p k j -> p (k j)")
                    slf = sl.rearrange("p k j -> p (k j)")
                    wn = "w1" if isrc == 0 else "w2"
                    dvr = E.load_cast(hbmsl[:, 0:512], slf[:, 0:512], 512,
                                      cache_key=("l2", wn, e_i, m, 0), war=war)
                    dvr = E.load_cast(hbmsl[:, 512:1024], slf[:, 512:1024], 512,
                                      cache_key=("l2", wn, e_i, m, 1), war=war)
                    wbm = ("wb", P.count("wb"))
                    E.wait_casts("tensor")
                    if isrc == 1 and m == 0:
                        P.wait("tensor", "act")
                    for n in range(2):
                        bank = n % 2
                        E.pe_wait_bank(bank)
                        ps = B.pb[bank]
                        for k in range(NC):
                            P.op("tensor", lambda e, ps=ps, src=src, sl=sl, k=k, m=m, n=n,
                                 tok0=tok0: e.matmul(
                                    ps[:, 0:512], sl[:, k, :],
                                    src[:, k, tok0 + n * 512:tok0 + (n + 1) * 512],
                                    start=(k == 0), stop=(k == NC - 1)),
                                inc=[("pe", 1)] if k == NC - 1 else ())
                        E._ring_busy[m % 4] = [("pe", P.count("pe")), wbm]
                        P.wait("scalar", "pe")
                        P.op("scalar", lambda e, ps=ps, dstf=dstf, m=m, n=n, tok0=tok0,
                             bcol=bcol: e.activation(
                                out=dstf[:, m, tok0 + n * 512:tok0 + (n + 1) * 512],
                                in_=ps[:, 0:512], func=AF.Relu, bias=bcol[:, m:m + 1],
                                scale=1.0), inc=[("act", 1)])
                        E.bank_consumed(bank, "act")
            if lvl <= 3:
                continue
            E.wait_casts("tensor")
            P.wait("tensor", *b3_ready)
            P.wait("tensor", "act")
            if half == 0 and e_i == 0:
                P.wait("vector", *cw_ready)
            for dchunk in range(2):
                for tt in range(8):
                    t = half * 8 + tt
                    bank = 2 + tt % 2
                    E.pe_wait_bank(bank)
                    ps = B.pb[bank]
                    for k in range(NC):
                        P.op("tensor", lambda e, ps=ps, k=k, t=t, dchunk=dchunk: e.matmul(
                            ps[:, 0:512], h2f[:, k, t * 128:(t + 1) * 128],
                            wAv[:, k, dchunk * 512:(dchunk + 1) * 512],
                            start=(k == 0), stop=False))
                    P.op("tensor", lambda e, ps=ps, t=t, dchunk=dchunk: e.matmul(
                        ps[:, 0:512], B.onesb[0:1, 0:128],
                        B.rowb[0:1, dchunk * 512:(dchunk + 1) * 512],
                        start=False, stop=True), inc=[("pe", 1)])
                    P.wait("vector", "pe")
                    cwc = B.small[:, cw_off + t * 8 + e_i:cw_off + t * 8 + e_i + 1]
                    xsl = B.x[:, t, dchunk * 512:(dchunk + 1) * 512]
                    P.op("vector", lambda e, ps=ps, cwc=cwc, xsl=xsl: e.scalar_tensor_tensor(
                        out=xsl, in0=ps[:, 0:512], scalar=cwc, in1=xsl,
                        op0=ALU.mult, op1=ALU.add), inc=[("dve", 1)])
                    E.bank_consumed(bank, "dve")
    E.small_reset()


def prep_l2_ffn(inp):
    f32 = lambda a: np.asarray(a, np.float32)
    w1, w2, w3 = f32(inp["L2_ffn_w1"]), f32(inp["L2_ffn_w2"]), f32(inp["L2_ffn_w3"])
    b1, b2, b3 = f32(inp["L2_ffn_b1"]), f32(inp["L2_ffn_b2"]), f32(inp["L2_ffn_b3"])
    gw, gb = f32(inp["L2_ffn_gate_w"]), f32(inp["L2_ffn_gate_b"])
    mslice = lambda w: np.ascontiguousarray(
        w.reshape(8, 8, 128, 8, 128).transpose(0, 3, 2, 1, 4))
    cols = lambda b: b.reshape(8, 8, 128).transpose(0, 2, 1)
    return {
        "l2f_w1": mslice(w1),
        "l2f_w2": mslice(w2),
        "l2f_w3": np.ascontiguousarray(w3),
        "l2f_b12": np.ascontiguousarray(np.concatenate([cols(b1), cols(b2)], 2)),
        "l2f_b3": np.ascontiguousarray(b3.reshape(8, 1, 1024)),
        "l2f_gw": np.ascontiguousarray(
            gw.T.reshape(8, 128, 8).transpose(1, 0, 2).reshape(128, 64)),
        "l2f_gb": np.ascontiguousarray(gb.reshape(1, 8)),
    }


def declare_l2f_w(nc):
    return {
        "w1": nc.dram_tensor("l2f_w1", [8, 8, 128, 8, 128], F32, kind="ExternalInput"),
        "w2": nc.dram_tensor("l2f_w2", [8, 8, 128, 8, 128], F32, kind="ExternalInput"),
        "w3": nc.dram_tensor("l2f_w3", [8, 1024, 1024], F32, kind="ExternalInput"),
        "b12": nc.dram_tensor("l2f_b12", [8, 128, 16], F32, kind="ExternalInput"),
        "b3": nc.dram_tensor("l2f_b3", [8, 1, 1024], F32, kind="ExternalInput"),
        "gw": nc.dram_tensor("l2f_gw", [128, 64], F32, kind="ExternalInput"),
        "gb": nc.dram_tensor("l2f_gb", [1, 8], F32, kind="ExternalInput"),
    }


def build_test_l2f(_):
    nc = bass.Bass("TRN2", target_bir_lowering=False)
    x_in = nc.dram_tensor("x", [T, D], F32, kind="ExternalInput")
    x_out = nc.dram_tensor("xout", [T, D], F32, kind="ExternalOutput")
    W = declare_l2f_w(nc)
    with ExitStack() as ctx:
        B = Buffers(nc, ctx)
        P = Prog(nc)
        E = Emit(nc, P, B, ctx)
        E.wcache = nc.dram_tensor("wcache", [1024, 128, 512], BF16, kind="Internal")
        E.init_consts()
        E.load_x(x_in)
        E.emit_ln()
        emit_l2_ffn(E, W)
        E.store_x(x_out)
        block = ctx.enter_context(nc.Block())
        P.replay(block)
    return nc


def emit_l0_ffn(E, W, n_cores=8, cc_idx=0):
    P, B = E.P, E.B
    xn = B.fm(B.actA)
    hmf = B.fm(B.actB)
    wAv = B.wA.ap().rearrange("p (k d) -> p k d", k=8)
    wBf = B.wB.ap()
    ring = [wBf[:, i * 1024:(i + 1) * 1024].rearrange("p (k j) -> p k j", k=8)
            for i in range(4)]
    probs_off = E.small_col(128)
    h1_off = E.small_col(128)
    h2_off = E.small_col(128)
    cw_off = E.small_col(128)
    o_stats = emit_moe_gating_softmax(E, W, probs_off, has_bias=False)
    emit_top2(E, probs_off, h1_off, h2_off, o_stats)
    m2c = B.small[:, o_stats:o_stats + 16]
    o2 = E.small_col(64)
    m1c = B.small[:, o2:o2 + 16]
    sden = B.small[:, o2 + 16:o2 + 32]
    w1c = B.small[:, o2 + 32:o2 + 48]
    w2c = B.small[:, o2 + 48:o2 + 64]
    o3 = E.small_col(64)
    cnt = B.small[:, o3:o3 + 1]
    gcc = B.small[:, o3 + 1:o3 + 2]
    gcr = B.small[0:1, o3 + 2:o3 + 10]
    maskc = B.small[:, o3 + 10:o3 + 11]
    omm = B.small[:, o3 + 11:o3 + 12]
    bestc = B.small[:, o3 + 12:o3 + 13]
    bestb = B.small[:, o3 + 13:o3 + 14]
    c1c = B.small[:, o3 + 14:o3 + 15]
    condc = B.small[:, o3 + 15:o3 + 16]
    omc = B.small[:, o3 + 16:o3 + 17]
    simc = B.small[:, o3 + 17:o3 + 33]
    msim = B.small[:, o3 + 33:o3 + 49]
    for t in range(NT):
        pr = B.small[:, probs_off + t * 8:probs_off + t * 8 + 8]
        P.op("vector", lambda e, pr=pr, t=t: e.reduce_max(m1c[:, t:t + 1], pr, axis=AX.X),
             inc=[("dve", 1)] if t == NT - 1 else ())
    P.op("vector", lambda e: e.tensor_copy(out=B.smallb[:, 0:128],
                                           in_=B.small[:, h1_off:h1_off + 128]))
    P.op("vector", lambda e: e.tensor_copy(out=B.smallb[:, 128:256],
                                           in_=B.small[:, h2_off:h2_off + 128]),
         inc=[("dve", 1)])
    hb_ready = ("dve", P.count("dve"))
    P.wait("tensor", *hb_ready)
    E.pe_wait_bank(6)
    for t in range(NT):
        P.op("tensor", lambda e, t=t: e.matmul(
            B.pb[6][0:8, 0:1], B.smallb[:, t * 8:(t + 1) * 8], B.onesb[:, 0:1],
            start=(t == 0), stop=(t == NT - 1)), inc=[("pe", 1)] if t == NT - 1 else ())
    P.wait("vector", "pe")
    P.op("vector", lambda e: e.tensor_copy(out=cnt[0:8, :], in_=B.pb[6][0:8, 0:1]),
         inc=[("dve", 1)])
    E.bank_consumed(6, "dve")
    ccin = E.cc_in
    ccout = E.cc_out
    P.wait("gpsimd", "dve")
    P.op("gpsimd", lambda e: e.dma_start(out=ccin.ap()[cc_idx], in_=cnt[0:8, :]),
         inc=[("out", 16)])
    P.wait("gpsimd", "out")
    ccsem = "cc0" if cc_idx == 0 else "cc1"
    P.op("gpsimd", lambda e: e.collective_compute(
        "AllReduce", ALU.add, replica_groups=[list(range(n_cores))],
        ins=[ccin.ap()[cc_idx].opt()], outs=[ccout.ap()[cc_idx].opt()]),
        inc=[(ccsem, 1)])
    P.wait("gpsimd", ccsem)
    P.op("gpsimd", lambda e: e.dma_start(out=gcc[0:8, :], in_=ccout.ap()[cc_idx]),
         inc=[("out", 16)])
    P.op("gpsimd", lambda e: e.dma_start(out=gcr, in_=ccout.ap()[cc_idx].rearrange("e one -> (one) (e)")),
         inc=[("out", 16)])
    dsim = E.load_vec_ap(W["sim"].ap(), simc[0:8, :])
    P.wait("vector", "out")
    P.wait("vector", *dsim)
    P.op("vector", lambda e: e.tensor_scalar(out=maskc[0:8, :], in0=gcc[0:8, :],
                                             scalar1=0.5, scalar2=None, op0=ALU.is_gt))
    P.op("vector", lambda e: e.tensor_scalar(out=B.scr[0:1, 0:8], in0=gcr,
                                             scalar1=0.5, scalar2=None, op0=ALU.is_gt),
         inc=[("dve", 1)])
    P.wait("gpsimd", "dve")
    P.op("gpsimd", lambda e: e.tensor_copy(out=B.rowb[0:1, 16:24], in_=B.scr[0:1, 0:8]),
         inc=[("pool", 1)])
    P.wait("tensor", "pool")
    E.pe_wait_bank(6)
    P.op("tensor", lambda e: e.matmul(B.pb[6][0:8, 0:8], B.onesb[0:1, 0:8],
                                      B.rowb[0:1, 16:24], start=True, stop=True),
         inc=[("pe", 1)])
    P.wait("vector", "pe")
    P.op("vector", lambda e: e.scalar_tensor_tensor(
        out=msim[0:8, 0:8], in0=B.pb[6][0:8, 0:8], scalar=1e10, in1=simc[0:8, 0:8],
        op0=ALU.mult, op1=ALU.add), inc=[("dve", 1)])
    E.bank_consumed(6, "dve")
    P.wait("gpsimd", "dve")
    P.op("gpsimd", lambda e: e.tensor_scalar(out=msim[0:8, 8:16], in0=msim[0:8, 0:8],
                                             scalar1=-1e10, scalar2=None, op0=ALU.add),
         inc=[("pool", 1)])
    P.wait("vector", "pool")
    P.op("vector", lambda e: e.reduce_max(bestc[0:8, :], msim[0:8, 8:16], axis=AX.X),
         inc=[("dve", 1)])
    P.wait("gpsimd", "dve")
    P.op("gpsimd", lambda e: e.tensor_copy(out=bestb[0:8, :], in_=bestc[0:8, :]),
         inc=[("pool", 1)])
    P.wait("vector", "pool")
    hj = B.scr[0:8, 8:16]
    hjb = B.scr[0:8, 24:32]
    c1b = B.small[:, o3 + 49:o3 + 50]
    ommb = B.small[:, o3 + 50:o3 + 51]
    condb = B.small[:, o3 + 51:o3 + 52]
    omcb = B.small[:, o3 + 52:o3 + 53]
    P.op("vector", lambda e: e.tensor_scalar(out=hj, in0=msim[0:8, 8:16],
                                             scalar1=bestb[0:8, :], scalar2=None,
                                             op0=ALU.is_ge))
    P.op("vector", lambda e: e.tensor_scalar(out=c1c[0:8, :], in0=bestb[0:8, :],
                                             scalar1=0.7, scalar2=None, op0=ALU.is_ge))
    P.op("vector", lambda e: e.tensor_scalar(out=omm[0:8, :], in0=maskc[0:8, :],
                                             scalar1=-1.0, scalar2=1.0, op0=ALU.mult,
                                             op1=ALU.add), inc=[("dve", 1)])
    P.wait("gpsimd", "dve")
    P.op("gpsimd", lambda e: e.tensor_copy(out=hjb, in_=hj))
    P.op("gpsimd", lambda e: e.tensor_copy(out=c1b[0:8, :], in_=c1c[0:8, :]))
    P.op("gpsimd", lambda e: e.tensor_copy(out=ommb[0:8, :], in_=omm[0:8, :]),
         inc=[("pool", 1)])
    P.wait("vector", "pool")
    P.op("vector", lambda e: e.tensor_tensor(out=condc[0:8, :], in0=ommb[0:8, :],
                                             in1=c1b[0:8, :], op=ALU.mult),
         inc=[("dve", 1)])
    P.wait("gpsimd", "dve")
    P.op("gpsimd", lambda e: e.tensor_copy(out=condb[0:8, :], in_=condc[0:8, :]),
         inc=[("pool", 1)])
    P.wait("vector", "pool")
    P.op("vector", lambda e: e.tensor_scalar(out=B.scr[0:8, 16:24], in0=hjb,
                                             scalar1=condb[0:8, :], scalar2=None,
                                             op0=ALU.mult))
    P.op("vector", lambda e: e.tensor_scalar(out=omcb[0:8, :], in0=condb[0:8, :],
                                             scalar1=-1.0, scalar2=1.0, op0=ALU.mult,
                                             op1=ALU.add), inc=[("dve", 1)])
    P.wait("gpsimd", "dve")
    P.op("gpsimd", lambda e: e.tensor_scalar(out=B.scr[0:8, 32:40],
                                             in0=B.ident[0:8, 0:8],
                                             scalar1=omcb[0:8, :], scalar2=None,
                                             op0=ALU.mult))
    P.op("gpsimd", lambda e: e.tensor_tensor(out=B.scr[0:8, 40:48], in0=B.scr[0:8, 16:24],
                                             in1=B.scr[0:8, 32:40], op=ALU.add),
         inc=[("pool", 1)])
    P.wait("vector", "pool")
    P.op("vector", lambda e: e.tensor_copy(out=B.rowb[0:8, 0:8], in_=B.scr[0:8, 40:48]),
         inc=[("dve", 1)])
    P.op("vector", lambda e: e.tensor_tensor(out=sden, in0=m1c, in1=m2c, op=ALU.add),
         inc=[("dve", 1)])
    P.wait("gpsimd", "dve")
    P.op("gpsimd", lambda e: e.tensor_copy(out=B.scr[:, 32:48], in_=sden), inc=[("pool", 1)])
    P.wait("vector", "pool")
    P.op("vector", lambda e: e.reciprocal(B.scr[:, 48:64], B.scr[:, 32:48]),
         inc=[("dve", 1)])
    P.wait("gpsimd", "dve")
    P.op("gpsimd", lambda e: e.tensor_copy(out=sden, in_=B.scr[:, 48:64]),
         inc=[("pool", 1)])
    P.wait("vector", "pool")
    P.op("vector", lambda e: e.scalar_tensor_tensor(out=w1c, in0=m1c, scalar=0.5,
                                                    in1=sden, op0=ALU.mult, op1=ALU.mult))
    P.op("vector", lambda e: e.scalar_tensor_tensor(out=w2c, in0=m2c, scalar=0.5,
                                                    in1=sden, op0=ALU.mult, op1=ALU.mult),
         inc=[("dve", 1)])
    P.wait("tensor", "pool")
    P.wait("tensor", "dve")
    for t in range(NT):
        if E.pt_ready is not None:
            P.wait("tensor", *E.pt_ready)
            E.pt_ready = None
        P.op("tensor", lambda e, t=t: e.transpose(
            B.pt[0:8, 0:128], B.smallb[:, t * 8:(t + 1) * 8], B.ident[:, :]))
        P.op("tensor", lambda e, t=t: e.transpose(
            B.pt[0:8, 128:256], B.smallb[:, 128 + t * 8:128 + (t + 1) * 8], B.ident[:, :]),
            inc=[("pe", 1)])
        P.wait("vector", "pe")
        P.op("vector", lambda e: e.tensor_copy(out=B.rowb[0:8, 128:256], in_=B.pt[0:8, 0:128]))
        P.op("vector", lambda e: e.tensor_copy(out=B.rowb[0:8, 256:384], in_=B.pt[0:8, 128:256]),
             inc=[("dve", 1)])
        E.pt_ready = ("dve", P.count("dve"))
        P.wait("tensor", "dve")
        bank = 4 + t % 2
        E.pe_wait_bank(bank)
        E.pe_wait_bank(6)
        P.op("tensor", lambda e, bank=bank: e.matmul(
            B.pb[bank][:, 0:8], B.rowb[0:8, 128:256], B.rowb[0:8, 0:8],
            start=True, stop=True))
        P.op("tensor", lambda e: e.matmul(
            B.pb[6][:, 0:8], B.rowb[0:8, 256:384], B.rowb[0:8, 0:8],
            start=True, stop=True), inc=[("pe", 1)])
        P.wait("scalar", "pe")
        cw = B.small[:, cw_off + t * 8:cw_off + t * 8 + 8]
        P.op("scalar", lambda e, bank=bank: e.copy(
            out=B.scr[:, 64:72], in_=B.pb[bank][:, 0:8]), inc=[("act", 1)])
        P.wait("gpsimd", "act")
        P.op("gpsimd", lambda e, t=t: e.tensor_scalar(
            out=B.scr[:, 72:80], in0=B.scr[:, 64:72], scalar1=w1c[:, t:t + 1],
            scalar2=None, op0=ALU.mult), inc=[("pool", 1)])
        P.wait("vector", "pool")
        P.op("vector", lambda e, t=t, cw=cw: e.scalar_tensor_tensor(
            out=cw, in0=B.pb[6][:, 0:8], scalar=w2c[:, t:t + 1],
            in1=B.scr[:, 72:80], op0=ALU.mult, op1=ALU.add), inc=[("dve", 1)])
        E.bank_consumed(bank, "act")
        E.bank_consumed(6, "dve")
    cw_ready = ("dve", P.count("dve"))

    guh, dnh = W["gu"].ap(), W["dn"].ap()
    for e_i in range(8):
        P.wait("gpsimd", "pe")
        P.wait("gpsimd", "wb")
        P.wait("sync", "pe")
        P.wait("sync", "wb")
        dv3 = None
        for k in range(NC):
            for hf in range(2):
                dv3 = E.load_cast(
                    dnh[e_i].rearrange("(k p) o -> p k o", p=128)[:, k, hf * 512:(hf + 1) * 512],
                    wAv[:, k, hf * 512:(hf + 1) * 512], 512,
                    cache_key=("l0", "dn", e_i, k, hf))
        for half in range(2):
            tok0 = half * 1024
            for m in range(8):
                s0, s1 = ring[(2 * m) % 4], ring[(2 * m + 1) % 4]
                war0 = E._ring_busy[(2 * m) % 4] or []
                war1 = E._ring_busy[(2 * m + 1) % 4] or []
                E._ring_busy[(2 * m) % 4] = None
                E._ring_busy[(2 * m + 1) % 4] = None
                mg = guh[e_i, m].rearrange("p k j -> p (k j)")
                mu = guh[e_i, m + 8].rearrange("p k j -> p (k j)")
                s0f = s0.rearrange("p k j -> p (k j)")
                s1f = s1.rearrange("p k j -> p (k j)")
                E.load_cast(mg[:, 0:512], s0f[:, 0:512], 512,
                            cache_key=("l0", "gu", e_i, m, 0), war=war0)
                E.load_cast(mg[:, 512:1024], s0f[:, 512:1024], 512,
                            cache_key=("l0", "gu", e_i, m, 1), war=war0)
                E.load_cast(mu[:, 0:512], s1f[:, 0:512], 512,
                            cache_key=("l0", "gu", e_i, m + 8, 0), war=war1)
                dvr = E.load_cast(mu[:, 512:1024], s1f[:, 512:1024], 512,
                                  cache_key=("l0", "gu", e_i, m + 8, 1), war=war1)
                wbm = ("wb", P.count("wb"))
                E.wait_casts("tensor")
                for n in range(2):
                    bg, bu = 0, 1
                    E.pe_wait_bank(bg)
                    E.pe_wait_bank(bu)
                    for k in range(NC):
                        P.op("tensor", lambda e, k=k, s0=s0, n=n, tok0=tok0: e.matmul(
                            B.pb[0][:, 0:512], s0[:, k, :],
                            xn[:, k, tok0 + n * 512:tok0 + (n + 1) * 512],
                            start=(k == 0), stop=(k == NC - 1)),
                            inc=[("pe", 1)] if k == NC - 1 else ())
                    for k in range(NC):
                        P.op("tensor", lambda e, k=k, s1=s1, n=n, tok0=tok0: e.matmul(
                            B.pb[1][:, 0:512], s1[:, k, :],
                            xn[:, k, tok0 + n * 512:tok0 + (n + 1) * 512],
                            start=(k == 0), stop=(k == NC - 1)),
                            inc=[("pe", 1)] if k == NC - 1 else ())
                    P.wait("scalar", "pe")
                    P.op("scalar", lambda e: e.activation(
                        out=B.scr[:, 0:512], in_=B.pb[0][:, 0:512], func=AF.Silu),
                        inc=[("act", 1)])
                    E.bank_consumed(0, "act")
                    P.wait("vector", "act")
                    P.op("vector", lambda e, m=m, n=n, tok0=tok0: e.tensor_tensor(
                        out=hmf[:, m, tok0 + n * 512:tok0 + (n + 1) * 512],
                        in0=B.scr[:, 0:512], in1=B.pb[1][:, 0:512], op=ALU.mult),
                        inc=[("dve", 1)])
                    E.bank_consumed(1, "dve")
                E._ring_busy[2 * m % 4] = [("pe", P.count("pe")), wbm]
                E._ring_busy[(2 * m + 1) % 4] = [("pe", P.count("pe")), wbm]
            E.wait_casts("tensor")
            P.wait("tensor", "dve")
            if half == 0 and e_i == 0:
                P.wait("vector", *cw_ready)
            for dchunk in range(2):
                for tt in range(8):
                    t = half * 8 + tt
                    bank = 2 + tt % 2
                    E.pe_wait_bank(bank)
                    ps = B.pb[bank]
                    for k in range(NC):
                        P.op("tensor", lambda e, ps=ps, k=k, t=t, dchunk=dchunk: e.matmul(
                            ps[:, 0:512], hmf[:, k, t * 128:(t + 1) * 128],
                            wAv[:, k, dchunk * 512:(dchunk + 1) * 512],
                            start=(k == 0), stop=(k == NC - 1)),
                            inc=[("pe", 1)] if k == NC - 1 else ())
                    P.wait("vector", "pe")
                    cwc = B.small[:, cw_off + t * 8 + e_i:cw_off + t * 8 + e_i + 1]
                    xsl = B.x[:, t, dchunk * 512:(dchunk + 1) * 512]
                    P.op("vector", lambda e, ps=ps, cwc=cwc, xsl=xsl: e.scalar_tensor_tensor(
                        out=xsl, in0=ps[:, 0:512], scalar=cwc, in1=xsl,
                        op0=ALU.mult, op1=ALU.add), inc=[("dve", 1)])
                    E.bank_consumed(bank, "dve")
    E.small_reset()


def prep_l0_ffn(inp):
    f32 = lambda a: np.asarray(a, np.float32)
    gu = f32(inp["L0_ffn_gate_up_proj"])
    dn = f32(inp["L0_ffn_down_proj"])
    gw = f32(inp["L0_ffn_gate_weight"])
    sim = f32(inp["L0_ffn_similarity_matrix"])
    guT = gu.transpose(0, 2, 1)
    gus = np.ascontiguousarray(
        guT.reshape(8, 8, 128, 16, 128).transpose(0, 3, 2, 1, 4))
    dnT = np.ascontiguousarray(dn.transpose(0, 2, 1))
    simpad = np.zeros((8, 16), np.float32)
    simpad[:, 0:8] = sim
    return {
        "l0f_gu": gus,
        "l0f_dn": dnT,
        "l0f_gw": np.ascontiguousarray(gw.T.reshape(8, 128, 8).transpose(1, 0, 2).reshape(128, 64)),
        "l0f_sim": np.ascontiguousarray(simpad),
    }


def declare_l0f_w(nc):
    return {
        "gu": nc.dram_tensor("l0f_gu", [8, 16, 128, 8, 128], F32, kind="ExternalInput"),
        "dn": nc.dram_tensor("l0f_dn", [8, 1024, 1024], F32, kind="ExternalInput"),
        "gw": nc.dram_tensor("l0f_gw", [128, 64], F32, kind="ExternalInput"),
        "sim": nc.dram_tensor("l0f_sim", [8, 16], F32, kind="ExternalInput"),
    }


def build_test_l0f(n_cores):
    nc = bass.Bass("TRN2", target_bir_lowering=False)
    x_in = nc.dram_tensor("x", [T, D], F32, kind="ExternalInput")
    x_out = nc.dram_tensor("xout", [T, D], F32, kind="ExternalOutput")
    W = declare_l0f_w(nc)
    with ExitStack() as ctx:
        B = Buffers(nc, ctx)
        P = Prog(nc)
        E = Emit(nc, P, B, ctx)
        E.wcache = nc.dram_tensor("wcache", [1024, 128, 512], BF16, kind="Internal")
        E.cc_in = nc.dram_tensor("cc_in", [2, 8, 1], F32, kind="Internal")
        E.cc_out = nc.dram_tensor("cc_out", [2, 8, 1], F32, kind="Internal")
        E.init_consts()
        E.load_x(x_in)
        E.emit_ln()
        emit_l0_ffn(E, W, n_cores=n_cores, cc_idx=0)
        E.store_x(x_out)
        dbg4 = nc.dram_tensor("dbg4", [128, 720], F32, kind="ExternalOutput")
        dbg5 = nc.dram_tensor("dbg5", [128, 256], BF16, kind="ExternalOutput")
        P.op("gpsimd", lambda e: e.dma_start(out=dbg4.ap(), in_=B.small[:, :]), inc=[("out", 16)])
        P.op("gpsimd", lambda e: e.dma_start(out=dbg5.ap(), in_=B.smallb[:, :]), inc=[("out", 16)])
        dbg6 = nc.dram_tensor("dbg6", [128, 2048], BF16, kind="ExternalOutput")
        dbg7 = nc.dram_tensor("dbg7", [128, 2048], F32, kind="ExternalOutput")
        P.op("gpsimd", lambda e: e.dma_start(out=dbg6.ap(), in_=B.rowb[:, :]), inc=[("out", 16)])
        P.op("gpsimd", lambda e: e.dma_start(out=dbg7.ap(), in_=B.scr[:, :]), inc=[("out", 16)])
        P.wait("gpsimd", "out")
        block = ctx.enter_context(nc.Block())
        P.replay(block)
    return nc



ALL_PREPS = None


def declare_l1f_w(nc):
    return {n: nc.dram_tensor(f"l1f_{n}", s, F32, kind="ExternalInput")
            for n, s in [("WrT", [D, D]), ("WkT", [D, D]), ("WvT", [D, D]),
                         ("mu2", [128, 16])]}


def declare_l3f_w(nc):
    return {"A": nc.dram_tensor("l3f_A", [D, D], F32, kind="ExternalInput"),
            "Bi": nc.dram_tensor("l3f_Bi", [D, D], F32, kind="ExternalInput"),
            "bias": nc.dram_tensor("l3f_bias", [1, D], F32, kind="ExternalInput")}


def prep_all(inputs):
    prep = {}
    prep.update(prep_l0_ffn(inputs))
    prep.update(prep_l1_ffn(inputs))
    prep.update(prep_l2_ffn(inputs))
    prep.update(prep_l3_ffn(inputs))
    for layer in range(4):
        prep.update(prep_attn(inputs, layer))
    return prep


def build_layer(layer, n_cores=8):
    nc = bass.Bass("TRN2", target_bir_lowering=False)
    x_in = nc.dram_tensor("x", [T, D], F32, kind="ExternalInput")
    x_out = nc.dram_tensor("xout", [T, D], F32, kind="ExternalOutput")
    names = []
    with ExitStack() as ctx:
        B = Buffers(nc, ctx)
        P = Prog(nc)
        E = Emit(nc, P, B, ctx)
        E.wcache = nc.dram_tensor("wcache", [1024, 128, 512], BF16, kind="Internal")
        st = nc.dram_tensor("attn_st", [16, 32, 128], BF16, kind="Internal")
        extra = [-float(np.pi)] if layer == 3 else []
        E.init_consts(extra_consts=extra)
        E.load_x(x_in)
        if layer == 0:
            E.cc_in = nc.dram_tensor("cc_in", [2, 8, 1], F32, kind="Internal")
            E.cc_out = nc.dram_tensor("cc_out", [2, 8, 1], F32, kind="Internal")
            Wf = declare_l0f_w(nc)
            Wa = declare_attn_w(nc, 0)
            names += ["l0f_gu", "l0f_dn", "l0f_gw", "l0f_sim",
                      "a0_Wqh", "a0_Wkh", "a0_Wv", "a0_Wo", "a0_bqk", "a0_bv", "a0_bo"]
            E.emit_ln(); emit_l0_ffn(E, Wf, n_cores=n_cores, cc_idx=0)
            E.emit_ln(); emit_attn(E, ATTN_CFGS[0], Wa, st)
            E.emit_ln(); emit_l0_ffn(E, Wf, n_cores=n_cores, cc_idx=1)
        elif layer == 1:
            Wf = declare_l1f_w(nc)
            Wa = declare_attn_w(nc, 1)
            names += ["l1f_WrT", "l1f_WkT", "l1f_WvT", "l1f_mu2",
                      "a1_Wqh", "a1_Wkh", "a1_Wv", "a1_Wo", "a1_bqk", "a1_bv", "a1_bo"]
            E.emit_ln(); E.emit_l1_ffn(Wf)
            E.emit_ln(); emit_attn(E, ATTN_CFGS[1], Wa, st)
            E.emit_ln(); E.emit_l1_ffn(Wf)
        elif layer == 2:
            Wf = declare_l2f_w(nc)
            Wa = declare_attn_w(nc, 2)
            names += ["l2f_w1", "l2f_w2", "l2f_w3", "l2f_b12", "l2f_b3", "l2f_gw", "l2f_gb",
                      "a2_Wqh", "a2_Wkh", "a2_Wv", "a2_Wo", "a2_bqk", "a2_bv", "a2_bo"]
            E.emit_ln(); emit_l2_ffn(E, Wf)
            E.emit_ln(); emit_attn(E, ATTN_CFGS[2], Wa, st)
            E.emit_ln(); emit_l2_ffn(E, Wf)
        else:
            Wf = declare_l3f_w(nc)
            Wa = declare_attn_w(nc, 3)
            names += ["l3f_A", "l3f_Bi", "l3f_bias",
                      "a3_Wqh", "a3_Wkh", "a3_Wv", "a3_Wo", "a3_qsc", "a3_ksc"]
            M_dram = nc.dram_tensor("l3f_M", [8, 128, 1024], BF16, kind="Internal")
            emit_l3_build_M(E, Wf, M_dram)
            E.emit_ln(); emit_l3_apply(E, Wf, M_dram, True)
            E.emit_ln(); emit_attn(E, ATTN_CFGS[3], Wa, st)
            E.emit_ln(); emit_l3_apply(E, Wf, M_dram, False)
        E.store_x(x_out)
        block = ctx.enter_context(nc.Block())
        P.replay(block)
    return nc, names




def build_full(n_cores=8):
    nc = bass.Bass("TRN2", target_bir_lowering=False)
    x_in = nc.dram_tensor("x", [T, D], F32, kind="ExternalInput")
    x_out = nc.dram_tensor("xout", [T, D], F32, kind="ExternalOutput")
    names = []
    with ExitStack() as ctx:
        B = Buffers(nc, ctx)
        P = Prog(nc)
        E = Emit(nc, P, B, ctx)
        E.wcache = nc.dram_tensor("wcache", [1024, 128, 512], BF16, kind="Internal")
        st = nc.dram_tensor("attn_st", [16, 32, 128], BF16, kind="Internal")
        E.cc_in = nc.dram_tensor("cc_in", [2, 8, 1], F32, kind="Internal")
        E.cc_out = nc.dram_tensor("cc_out", [2, 8, 1], F32, kind="Internal")
        M_dram = nc.dram_tensor("l3f_M", [8, 128, 1024], BF16, kind="Internal")
        E.init_consts(extra_consts=[-float(np.pi)])
        E.load_x(x_in)
        W0f = declare_l0f_w(nc)
        W0a = declare_attn_w(nc, 0)
        W1f = declare_l1f_w(nc)
        W1a = declare_attn_w(nc, 1)
        W2f = declare_l2f_w(nc)
        W2a = declare_attn_w(nc, 2)
        W3f = declare_l3f_w(nc)
        W3a = declare_attn_w(nc, 3)
        names += ["l0f_gu", "l0f_dn", "l0f_gw", "l0f_sim",
                  "a0_Wqh", "a0_Wkh", "a0_Wv", "a0_Wo", "a0_bqk", "a0_bv", "a0_bo",
                  "l1f_WrT", "l1f_WkT", "l1f_WvT", "l1f_mu2",
                  "a1_Wqh", "a1_Wkh", "a1_Wv", "a1_Wo", "a1_bqk", "a1_bv", "a1_bo",
                  "l2f_w1", "l2f_w2", "l2f_w3", "l2f_b12", "l2f_b3", "l2f_gw", "l2f_gb",
                  "a2_Wqh", "a2_Wkh", "a2_Wv", "a2_Wo", "a2_bqk", "a2_bv", "a2_bo",
                  "l3f_A", "l3f_Bi", "l3f_bias",
                  "a3_Wqh", "a3_Wkh", "a3_Wv", "a3_Wo", "a3_qsc", "a3_ksc"]
        E.emit_ln(); emit_l0_ffn(E, W0f, n_cores=n_cores, cc_idx=0)
        E.emit_ln(); emit_attn(E, ATTN_CFGS[0], W0a, st)
        E.emit_ln(); emit_l0_ffn(E, W0f, n_cores=n_cores, cc_idx=1)
        E.emit_ln(); E.emit_l1_ffn(W1f)
        E.emit_ln(); emit_attn(E, ATTN_CFGS[1], W1a, st)
        E.emit_ln(); E.emit_l1_ffn(W1f)
        E.emit_ln(); emit_l2_ffn(E, W2f)
        E.emit_ln(); emit_attn(E, ATTN_CFGS[2], W2a, st)
        E.emit_ln(); emit_l2_ffn(E, W2f)
        emit_l3_build_M(E, W3f, M_dram)
        E.emit_ln(); emit_l3_apply(E, W3f, M_dram, True)
        E.emit_ln(); emit_attn(E, ATTN_CFGS[3], W3a, st)
        E.emit_ln(); emit_l3_apply(E, W3f, M_dram, False)
        E.store_x(x_out)
        block = ctx.enter_context(nc.Block())
        P.replay(block)
    return nc, names


FUSED = True


def kernel(**inputs):
    prep = prep_all(inputs)
    x = np.ascontiguousarray(np.asarray(inputs["L0_attn_x"], np.float32))
    xs = [np.ascontiguousarray(x[i]) for i in range(NCORES)]
    if FUSED:
        nc, names = build_full(n_cores=NCORES)
        in_maps = [dict({"x": xs[i]}, **{k: prep[k] for k in names})
                   for i in range(NCORES)]
        res = run_bass_kernel_spmd(nc, in_maps, core_ids=list(range(NCORES)))
        xs = [np.asarray(res.results[i]["xout"], np.float32) for i in range(NCORES)]
        return np.stack(xs)
    for layer in range(4):
        nc, names = build_layer(layer, n_cores=NCORES)
        in_maps = [dict({"x": xs[i]}, **{k: prep[k] for k in names})
                   for i in range(NCORES)]
        res = run_bass_kernel_spmd(nc, in_maps, core_ids=list(range(NCORES)))
        xs = [np.asarray(res.results[i]["xout"], np.float32) for i in range(NCORES)]
    return np.stack(xs)



ALL_PREPS = None


def declare_l1f_w(nc):
    return {n: nc.dram_tensor(f"l1f_{n}", s, F32, kind="ExternalInput")
            for n, s in [("WrT", [D, D]), ("WkT", [D, D]), ("WvT", [D, D]),
                         ("mu2", [128, 16])]}


def declare_l3f_w(nc):
    return {"A": nc.dram_tensor("l3f_A", [D, D], F32, kind="ExternalInput"),
            "Bi": nc.dram_tensor("l3f_Bi", [D, D], F32, kind="ExternalInput"),
            "bias": nc.dram_tensor("l3f_bias", [1, D], F32, kind="ExternalInput")}


def prep_all(inputs):
    prep = {}
    prep.update(prep_l0_ffn(inputs))
    prep.update(prep_l1_ffn(inputs))
    prep.update(prep_l2_ffn(inputs))
    prep.update(prep_l3_ffn(inputs))
    for layer in range(4):
        prep.update(prep_attn(inputs, layer))
    return prep


def build_layer(layer, n_cores=8):
    nc = bass.Bass("TRN2", target_bir_lowering=False)
    x_in = nc.dram_tensor("x", [T, D], F32, kind="ExternalInput")
    x_out = nc.dram_tensor("xout", [T, D], F32, kind="ExternalOutput")
    names = []
    with ExitStack() as ctx:
        B = Buffers(nc, ctx)
        P = Prog(nc)
        E = Emit(nc, P, B, ctx)
        E.wcache = nc.dram_tensor("wcache", [1024, 128, 512], BF16, kind="Internal")
        E.wcache = nc.dram_tensor("wcache", [1024, 128, 512], BF16, kind="Internal")
        st = nc.dram_tensor("attn_st", [16, 32, 128], BF16, kind="Internal")
        extra = [-float(np.pi)] if layer == 3 else []
        E.init_consts(extra_consts=extra)
        E.load_x(x_in)
        if layer == 0:
            E.cc_in = nc.dram_tensor("cc_in", [2, 8, 1], F32, kind="Internal")
            E.cc_out = nc.dram_tensor("cc_out", [2, 8, 1], F32, kind="Internal")
            Wf = declare_l0f_w(nc)
            Wa = declare_attn_w(nc, 0)
            names += ["l0f_gu", "l0f_dn", "l0f_gw", "l0f_sim",
                      "a0_Wqh", "a0_Wkh", "a0_Wv", "a0_Wo", "a0_bqk", "a0_bv", "a0_bo"]
            E.emit_ln(); emit_l0_ffn(E, Wf, n_cores=n_cores, cc_idx=0)
            E.emit_ln(); emit_attn(E, ATTN_CFGS[0], Wa, st)
            E.emit_ln(); emit_l0_ffn(E, Wf, n_cores=n_cores, cc_idx=1)
        elif layer == 1:
            Wf = declare_l1f_w(nc)
            Wa = declare_attn_w(nc, 1)
            names += ["l1f_WrT", "l1f_WkT", "l1f_WvT", "l1f_mu2",
                      "a1_Wqh", "a1_Wkh", "a1_Wv", "a1_Wo", "a1_bqk", "a1_bv", "a1_bo"]
            E.emit_ln(); E.emit_l1_ffn(Wf)
            E.emit_ln(); emit_attn(E, ATTN_CFGS[1], Wa, st)
            E.emit_ln(); E.emit_l1_ffn(Wf)
        elif layer == 2:
            Wf = declare_l2f_w(nc)
            Wa = declare_attn_w(nc, 2)
            names += ["l2f_w1", "l2f_w2", "l2f_w3", "l2f_b12", "l2f_b3", "l2f_gw", "l2f_gb",
                      "a2_Wqh", "a2_Wkh", "a2_Wv", "a2_Wo", "a2_bqk", "a2_bv", "a2_bo"]
            E.emit_ln(); emit_l2_ffn(E, Wf)
            E.emit_ln(); emit_attn(E, ATTN_CFGS[2], Wa, st)
            E.emit_ln(); emit_l2_ffn(E, Wf)
        else:
            Wf = declare_l3f_w(nc)
            Wa = declare_attn_w(nc, 3)
            names += ["l3f_A", "l3f_Bi", "l3f_bias",
                      "a3_Wqh", "a3_Wkh", "a3_Wv", "a3_Wo", "a3_qsc", "a3_ksc"]
            M_dram = nc.dram_tensor("l3f_M", [8, 128, 1024], BF16, kind="Internal")
            emit_l3_build_M(E, Wf, M_dram)
            E.emit_ln(); emit_l3_apply(E, Wf, M_dram, True)
            E.emit_ln(); emit_attn(E, ATTN_CFGS[3], Wa, st)
            E.emit_ln(); emit_l3_apply(E, Wf, M_dram, False)
        E.store_x(x_out)
        block = ctx.enter_context(nc.Block())
        P.replay(block)
    return nc, names




def build_full(n_cores=8):
    nc = bass.Bass("TRN2", target_bir_lowering=False)
    x_in = nc.dram_tensor("x", [T, D], F32, kind="ExternalInput")
    x_out = nc.dram_tensor("xout", [T, D], F32, kind="ExternalOutput")
    names = []
    with ExitStack() as ctx:
        B = Buffers(nc, ctx)
        P = Prog(nc)
        E = Emit(nc, P, B, ctx)
        E.wcache = nc.dram_tensor("wcache", [1024, 128, 512], BF16, kind="Internal")
        E.wcache = nc.dram_tensor("wcache", [1024, 128, 512], BF16, kind="Internal")
        st = nc.dram_tensor("attn_st", [16, 32, 128], BF16, kind="Internal")
        E.cc_in = nc.dram_tensor("cc_in", [2, 8, 1], F32, kind="Internal")
        E.cc_out = nc.dram_tensor("cc_out", [2, 8, 1], F32, kind="Internal")
        M_dram = nc.dram_tensor("l3f_M", [8, 128, 1024], BF16, kind="Internal")
        E.init_consts(extra_consts=[-float(np.pi)])
        E.load_x(x_in)
        W0f = declare_l0f_w(nc)
        W0a = declare_attn_w(nc, 0)
        W1f = declare_l1f_w(nc)
        W1a = declare_attn_w(nc, 1)
        W2f = declare_l2f_w(nc)
        W2a = declare_attn_w(nc, 2)
        W3f = declare_l3f_w(nc)
        W3a = declare_attn_w(nc, 3)
        names += ["l0f_gu", "l0f_dn", "l0f_gw", "l0f_sim",
                  "a0_Wqh", "a0_Wkh", "a0_Wv", "a0_Wo", "a0_bqk", "a0_bv", "a0_bo",
                  "l1f_WrT", "l1f_WkT", "l1f_WvT", "l1f_mu2",
                  "a1_Wqh", "a1_Wkh", "a1_Wv", "a1_Wo", "a1_bqk", "a1_bv", "a1_bo",
                  "l2f_w1", "l2f_w2", "l2f_w3", "l2f_b12", "l2f_b3", "l2f_gw", "l2f_gb",
                  "a2_Wqh", "a2_Wkh", "a2_Wv", "a2_Wo", "a2_bqk", "a2_bv", "a2_bo",
                  "l3f_A", "l3f_Bi", "l3f_bias",
                  "a3_Wqh", "a3_Wkh", "a3_Wv", "a3_Wo", "a3_qsc", "a3_ksc"]
        E.emit_ln(); emit_l0_ffn(E, W0f, n_cores=n_cores, cc_idx=0)
        E.emit_ln(); emit_attn(E, ATTN_CFGS[0], W0a, st)
        E.emit_ln(); emit_l0_ffn(E, W0f, n_cores=n_cores, cc_idx=1)
        E.emit_ln(); E.emit_l1_ffn(W1f)
        E.emit_ln(); emit_attn(E, ATTN_CFGS[1], W1a, st)
        E.emit_ln(); E.emit_l1_ffn(W1f)
        E.emit_ln(); emit_l2_ffn(E, W2f)
        E.emit_ln(); emit_attn(E, ATTN_CFGS[2], W2a, st)
        E.emit_ln(); emit_l2_ffn(E, W2f)
        emit_l3_build_M(E, W3f, M_dram)
        E.emit_ln(); emit_l3_apply(E, W3f, M_dram, True)
        E.emit_ln(); emit_attn(E, ATTN_CFGS[3], W3a, st)
        E.emit_ln(); emit_l3_apply(E, W3f, M_dram, False)
        E.store_x(x_out)
        block = ctx.enter_context(nc.Block())
        P.replay(block)
    return nc, names


FUSED = True


def kernel(**inputs):
    prep = prep_all(inputs)
    x = np.ascontiguousarray(np.asarray(inputs["L0_attn_x"], np.float32))
    xs = [np.ascontiguousarray(x[i]) for i in range(NCORES)]
    if FUSED:
        nc, names = build_full(n_cores=NCORES)
        in_maps = [dict({"x": xs[i]}, **{k: prep[k] for k in names})
                   for i in range(NCORES)]
        res = run_bass_kernel_spmd(nc, in_maps, core_ids=list(range(NCORES)))
        xs = [np.asarray(res.results[i]["xout"], np.float32) for i in range(NCORES)]
        return np.stack(xs)
    for layer in range(4):
        nc, names = build_layer(layer, n_cores=NCORES)
        in_maps = [dict({"x": xs[i]}, **{k: prep[k] for k in names})
                   for i in range(NCORES)]
        res = run_bass_kernel_spmd(nc, in_maps, core_ids=list(range(NCORES)))
        xs = [np.asarray(res.results[i]["xout"], np.float32) for i in range(NCORES)]
    return np.stack(xs)



ALL_PREPS = None


def declare_l1f_w(nc):
    return {n: nc.dram_tensor(f"l1f_{n}", s, F32, kind="ExternalInput")
            for n, s in [("WrT", [D, D]), ("WkT", [D, D]), ("WvT", [D, D]),
                         ("mu2", [128, 16])]}


def declare_l3f_w(nc):
    return {"A": nc.dram_tensor("l3f_A", [D, D], F32, kind="ExternalInput"),
            "Bi": nc.dram_tensor("l3f_Bi", [D, D], F32, kind="ExternalInput"),
            "bias": nc.dram_tensor("l3f_bias", [1, D], F32, kind="ExternalInput")}


def prep_all(inputs):
    prep = {}
    prep.update(prep_l0_ffn(inputs))
    prep.update(prep_l1_ffn(inputs))
    prep.update(prep_l2_ffn(inputs))
    prep.update(prep_l3_ffn(inputs))
    for layer in range(4):
        prep.update(prep_attn(inputs, layer))
    return prep


def build_layer(layer, n_cores=8):
    nc = bass.Bass("TRN2", target_bir_lowering=False)
    x_in = nc.dram_tensor("x", [T, D], F32, kind="ExternalInput")
    x_out = nc.dram_tensor("xout", [T, D], F32, kind="ExternalOutput")
    names = []
    with ExitStack() as ctx:
        B = Buffers(nc, ctx)
        P = Prog(nc)
        E = Emit(nc, P, B, ctx)
        E.wcache = nc.dram_tensor("wcache", [1024, 128, 512], BF16, kind="Internal")
        st = nc.dram_tensor("attn_st", [16, 32, 128], BF16, kind="Internal")
        extra = [-float(np.pi)] if layer == 3 else []
        E.init_consts(extra_consts=extra)
        E.load_x(x_in)
        if layer == 0:
            E.cc_in = nc.dram_tensor("cc_in", [2, 8, 1], F32, kind="Internal")
            E.cc_out = nc.dram_tensor("cc_out", [2, 8, 1], F32, kind="Internal")
            Wf = declare_l0f_w(nc)
            Wa = declare_attn_w(nc, 0)
            names += ["l0f_gu", "l0f_dn", "l0f_gw", "l0f_sim",
                      "a0_Wqh", "a0_Wkh", "a0_Wv", "a0_Wo", "a0_bqk", "a0_bv", "a0_bo"]
            E.emit_ln(); emit_l0_ffn(E, Wf, n_cores=n_cores, cc_idx=0)
            E.emit_ln(); emit_attn(E, ATTN_CFGS[0], Wa, st)
            E.emit_ln(); emit_l0_ffn(E, Wf, n_cores=n_cores, cc_idx=1)
        elif layer == 1:
            Wf = declare_l1f_w(nc)
            Wa = declare_attn_w(nc, 1)
            names += ["l1f_WrT", "l1f_WkT", "l1f_WvT", "l1f_mu2",
                      "a1_Wqh", "a1_Wkh", "a1_Wv", "a1_Wo", "a1_bqk", "a1_bv", "a1_bo"]
            E.emit_ln(); E.emit_l1_ffn(Wf)
            E.emit_ln(); emit_attn(E, ATTN_CFGS[1], Wa, st)
            E.emit_ln(); E.emit_l1_ffn(Wf)
        elif layer == 2:
            Wf = declare_l2f_w(nc)
            Wa = declare_attn_w(nc, 2)
            names += ["l2f_w1", "l2f_w2", "l2f_w3", "l2f_b12", "l2f_b3", "l2f_gw", "l2f_gb",
                      "a2_Wqh", "a2_Wkh", "a2_Wv", "a2_Wo", "a2_bqk", "a2_bv", "a2_bo"]
            E.emit_ln(); emit_l2_ffn(E, Wf)
            E.emit_ln(); emit_attn(E, ATTN_CFGS[2], Wa, st)
            E.emit_ln(); emit_l2_ffn(E, Wf)
        else:
            Wf = declare_l3f_w(nc)
            Wa = declare_attn_w(nc, 3)
            names += ["l3f_A", "l3f_Bi", "l3f_bias",
                      "a3_Wqh", "a3_Wkh", "a3_Wv", "a3_Wo", "a3_qsc", "a3_ksc"]
            M_dram = nc.dram_tensor("l3f_M", [8, 128, 1024], BF16, kind="Internal")
            emit_l3_build_M(E, Wf, M_dram)
            E.emit_ln(); emit_l3_apply(E, Wf, M_dram, True)
            E.emit_ln(); emit_attn(E, ATTN_CFGS[3], Wa, st)
            E.emit_ln(); emit_l3_apply(E, Wf, M_dram, False)
        E.store_x(x_out)
        block = ctx.enter_context(nc.Block())
        P.replay(block)
    return nc, names




def build_full(n_cores=8):
    nc = bass.Bass("TRN2", target_bir_lowering=False)
    x_in = nc.dram_tensor("x", [T, D], F32, kind="ExternalInput")
    x_out = nc.dram_tensor("xout", [T, D], F32, kind="ExternalOutput")
    names = []
    with ExitStack() as ctx:
        B = Buffers(nc, ctx)
        P = Prog(nc)
        E = Emit(nc, P, B, ctx)
        E.wcache = nc.dram_tensor("wcache", [1024, 128, 512], BF16, kind="Internal")
        st = nc.dram_tensor("attn_st", [16, 32, 128], BF16, kind="Internal")
        E.cc_in = nc.dram_tensor("cc_in", [2, 8, 1], F32, kind="Internal")
        E.cc_out = nc.dram_tensor("cc_out", [2, 8, 1], F32, kind="Internal")
        M_dram = nc.dram_tensor("l3f_M", [8, 128, 1024], BF16, kind="Internal")
        E.init_consts(extra_consts=[-float(np.pi)])
        E.load_x(x_in)
        W0f = declare_l0f_w(nc)
        W0a = declare_attn_w(nc, 0)
        W1f = declare_l1f_w(nc)
        W1a = declare_attn_w(nc, 1)
        W2f = declare_l2f_w(nc)
        W2a = declare_attn_w(nc, 2)
        W3f = declare_l3f_w(nc)
        W3a = declare_attn_w(nc, 3)
        names += ["l0f_gu", "l0f_dn", "l0f_gw", "l0f_sim",
                  "a0_Wqh", "a0_Wkh", "a0_Wv", "a0_Wo", "a0_bqk", "a0_bv", "a0_bo",
                  "l1f_WrT", "l1f_WkT", "l1f_WvT", "l1f_mu2",
                  "a1_Wqh", "a1_Wkh", "a1_Wv", "a1_Wo", "a1_bqk", "a1_bv", "a1_bo",
                  "l2f_w1", "l2f_w2", "l2f_w3", "l2f_b12", "l2f_b3", "l2f_gw", "l2f_gb",
                  "a2_Wqh", "a2_Wkh", "a2_Wv", "a2_Wo", "a2_bqk", "a2_bv", "a2_bo",
                  "l3f_A", "l3f_Bi", "l3f_bias",
                  "a3_Wqh", "a3_Wkh", "a3_Wv", "a3_Wo", "a3_qsc", "a3_ksc"]
        E.emit_ln(); emit_l0_ffn(E, W0f, n_cores=n_cores, cc_idx=0)
        E.emit_ln(); emit_attn(E, ATTN_CFGS[0], W0a, st)
        E.emit_ln(); emit_l0_ffn(E, W0f, n_cores=n_cores, cc_idx=1)
        E.emit_ln(); E.emit_l1_ffn(W1f)
        E.emit_ln(); emit_attn(E, ATTN_CFGS[1], W1a, st)
        E.emit_ln(); E.emit_l1_ffn(W1f)
        E.emit_ln(); emit_l2_ffn(E, W2f)
        E.emit_ln(); emit_attn(E, ATTN_CFGS[2], W2a, st)
        E.emit_ln(); emit_l2_ffn(E, W2f)
        emit_l3_build_M(E, W3f, M_dram)
        E.emit_ln(); emit_l3_apply(E, W3f, M_dram, True)
        E.emit_ln(); emit_attn(E, ATTN_CFGS[3], W3a, st)
        E.emit_ln(); emit_l3_apply(E, W3f, M_dram, False)
        E.store_x(x_out)
        block = ctx.enter_context(nc.Block())
        P.replay(block)
    return nc, names


FUSED = True


def kernel(**inputs):
    prep = prep_all(inputs)
    x = np.ascontiguousarray(np.asarray(inputs["L0_attn_x"], np.float32))
    xs = [np.ascontiguousarray(x[i]) for i in range(NCORES)]
    if FUSED:
        nc, names = build_full(n_cores=NCORES)
        in_maps = [dict({"x": xs[i]}, **{k: prep[k] for k in names})
                   for i in range(NCORES)]
        res = run_bass_kernel_spmd(nc, in_maps, core_ids=list(range(NCORES)))
        xs = [np.asarray(res.results[i]["xout"], np.float32) for i in range(NCORES)]
        return np.stack(xs)
    for layer in range(4):
        nc, names = build_layer(layer, n_cores=NCORES)
        in_maps = [dict({"x": xs[i]}, **{k: prep[k] for k in names})
                   for i in range(NCORES)]
        res = run_bass_kernel_spmd(nc, in_maps, core_ids=list(range(NCORES)))
        xs = [np.asarray(res.results[i]["xout"], np.float32) for i in range(NCORES)]
    return np.stack(xs)


# version 09
# speedup vs baseline: 1.1856x; 1.1856x over previous
import numpy as np
import ml_dtypes
from contextlib import ExitStack
import concourse.bass as bass
import concourse.mybir as mybir
from concourse.bass_utils import run_bass_kernel_spmd

F32 = mybir.dt.float32
BF16 = mybir.dt.bfloat16
AF = mybir.ActivationFunctionType
ALU = mybir.AluOpType
AX = mybir.AxisListType

NCORES = 8
T = 2048
D = 1024
NT = T // 128
NC = D // 128
EPS_LN = 1e-5

ENGINES = ("sync", "gpsimd", "tensor", "vector", "scalar")
CAST_SEM = {"gpsimd": "pool", "vector": "dve", "scalar": "act"}


class Prog:

    def __init__(self, nc):
        self.nc = nc
        self.streams = {e: [] for e in ENGINES}
        self.handles = {}
        self.counts = {}

    def make_sem(self, ctx, name):
        self.handles[name] = ctx.enter_context(self.nc.semaphore(name))
        self.counts[name] = 0
        return name

    def op(self, eng, fn, inc=()):
        inc = tuple(inc)
        h = self.handles

        def run(engine, fn=fn, inc=inc, h=h):
            inst = fn(engine)
            for s, n in inc:
                inst = inst.then_inc(h[s], n)
        self.streams[eng].append(run)
        for s, n in inc:
            self.counts[s] += n

    def wait(self, eng, semname, val=None):
        v = self.counts[semname] if val is None else val
        h = self.handles
        self.streams[eng].append(lambda engine, s=semname, v=v, h=h: engine.wait_ge(h[s], v))

    def count(self, s):
        return self.counts[s]

    def replay(self, block):
        for name in ENGINES:
            ops = self.streams[name]
            getattr(block, name)(lambda eng, ops=ops: [f(eng) for f in ops])


class Buffers:

    def __init__(self, nc, ctx):
        sb = lambda name, shape, dt: ctx.enter_context(nc.sbuf_tensor(name, shape, dt))
        self.x = sb("xres", [128, NT, 1024], F32)
        self.actA = sb("actA", [128, 16384], BF16)
        self.actB = sb("actB", [128, 16384], BF16)
        self.actC = sb("actC", [128, 16384], BF16)
        self.wA = sb("wA", [128, 8192], BF16)
        self.wB = sb("wB", [128, 4096], BF16)
        self.stage = sb("stage", [128, 1024], F32)
        self.scr = sb("scr", [128, 2048], F32)
        self.tmb = sb("tmb", [128, 1024], BF16)
        self.small = sb("small", [128, 720], F32)
        self.smallb = sb("smallb", [128, 256], BF16)
        self.rowb = sb("rowb", [128, 2048], BF16)
        self.fc = sb("fc", [128, 8], F32)
        self.onesb = sb("onesbf", [128, 128], BF16)
        self.ident = sb("ident", [128, 128], BF16)
        self.maskTm = sb("maskTm", [128, 128], BF16)
        self.maskFm = sb("maskFm", [128, 128], BF16)
        self.pb = [ctx.enter_context(nc.psum_tensor(f"pb{i}", [128, 512], F32)) for i in range(7)]
        self.pt = ctx.enter_context(nc.psum_tensor("pt", [128, 1024], BF16))

    def fm(self, t):
        return t.ap().rearrange("p (c n) -> p c n", c=8)

    def tm(self, t):
        return t.ap().rearrange("p (t d) -> p t d", t=16)


class Emit:

    def __init__(self, nc, P: Prog, B: Buffers, ctx):
        self.nc, self.P, self.B = nc, P, B
        for s in ("ld", "wq", "pe", "dve", "act", "pool", "out", "cc0", "cc1", "wb"):
            P.make_sem(ctx, s)
        self._small_off = 0
        self.bank_ready = [None] * 7
        self.pt_ready = None
        self.tmb_ready = None
        self._stage_par = 0
        self._stage_busy = [None, None]
        self._ring_busy = [None, None, None, None]
        self._cast_rot = 0
        self.wcache = None
        self._wc_map = {}
        self._wc_next = 0

    def small_col(self, n):
        o = self._small_off
        assert o + n <= 720, "small pool overflow"
        self._small_off += n
        return o

    def small_reset(self, keep=0):
        self._small_off = keep

    def pe_wait_bank(self, i):
        if self.bank_ready[i] is not None:
            s, v = self.bank_ready[i]
            self.P.wait("tensor", s, v)
            self.bank_ready[i] = None

    def bank_consumed(self, i, sem):
        self.bank_ready[i] = (sem, self.P.count(sem))

    def fconst(self, value):
        return self.B.fc[:, self._fcmap[value]:self._fcmap[value] + 1]

    def init_consts(self, extra_consts=()):
        P, B = self.P, self.B
        self._fcmap = {}
        vals = [EPS_LN] + list(extra_consts)
        for i, v in enumerate(vals):
            self._fcmap[v] = i
            P.op("gpsimd", lambda e, v=v, i=i: e.memset(B.fc[:, i:i + 1], v))
        P.op("gpsimd", lambda e: e.memset(B.onesb[:, :], 1.0))
        P.op("gpsimd", lambda e: e.memset(B.ident[:, :], 0.0))
        P.op("gpsimd", lambda e: e.affine_select(
            out=B.ident[:, :], in_=B.ident[:, :], compare_op=ALU.not_equal,
            fill=1.0, base=0, pattern=[[-1, 128]], channel_multiplier=1))
        P.op("gpsimd", lambda e: e.memset(B.maskTm[:, :], 0.0))
        P.op("gpsimd", lambda e: e.affine_select(
            out=B.maskTm[:, :], in_=B.maskTm[:, :], compare_op=ALU.is_ge,
            fill=-1e10, base=0, pattern=[[-1, 128]], channel_multiplier=1))
        P.op("gpsimd", lambda e: e.memset(B.maskFm[:, :], 0.0))
        P.op("gpsimd", lambda e: e.affine_select(
            out=B.maskFm[:, :], in_=B.maskFm[:, :], compare_op=ALU.is_ge,
            fill=-1e10, base=0, pattern=[[1, 128]], channel_multiplier=-1),
            inc=[("pool", 1)])

    def load_x(self, x_in):
        P, B = self.P, self.B
        xin = x_in.ap().rearrange("(t p) d -> p t d", p=128)
        P.op("sync", lambda e: e.dma_start(out=B.x[:, :, :], in_=xin), inc=[("ld", 16)])
        for e in ("vector", "scalar", "tensor", "gpsimd"):
            P.wait(e, "ld")
        for e in ("vector", "scalar", "tensor"):
            P.wait(e, "pool")

    def store_x(self, x_out):
        P, B = self.P, self.B
        xout = x_out.ap().rearrange("(t p) d -> p t d", p=128)
        P.wait("gpsimd", "dve")
        P.op("gpsimd", lambda e: e.dma_start(out=xout, in_=B.x[:, :, :]), inc=[("out", 16)])
        P.wait("gpsimd", "out")

    def emit_ln(self, dst=None):
        P, B = self.P, self.B
        dst = dst if dst is not None else B.actA
        dfm = B.fm(dst)
        o = self.small_col(5)
        negmean = B.small[:, o:o + 1]
        rstd = B.small[:, o + 1:o + 2]
        nmr = B.small[:, o + 2:o + 3]
        lnv = B.small[:, o + 3:o + 4]
        junk = B.small[:, o + 4:o + 5]
        P.wait("scalar", "dve")
        P.wait("tensor", "dve")
        for t in range(NT):
            xt = B.x[:, t, :]
            if t > 0:
                P.wait("vector", "act")
            P.op("vector", lambda e, xt=xt: e.tensor_scalar(
                out=B.scr[:, 1024:2048], in0=xt, scalar1=-1.0 / D, scalar2=None,
                op0=ALU.mult, op1=ALU.add, accum_out=negmean), inc=[("dve", 1)])
            P.wait("scalar", "dve")
            P.op("scalar", lambda e, xt=xt: e.activation(
                out=B.scr[:, 0:1024], in_=xt, func=AF.Square, bias=negmean, scale=1.0,
                accum_out=lnv))
            P.op("scalar", lambda e: e.activation(out=lnv, in_=lnv, func=AF.Ln,
                                                  bias=self.fconst(EPS_LN), scale=1.0 / D))
            P.op("scalar", lambda e: e.activation(out=rstd, in_=lnv, func=AF.Exp,
                                                  scale=-0.5), inc=[("act", 1)])
            P.wait("vector", "act")
            P.op("vector", lambda e: e.tensor_mul(nmr, negmean, rstd), inc=[("dve", 1)])
            P.wait("scalar", "dve")
            if self.tmb_ready is not None:
                P.wait("scalar", *self.tmb_ready)
                self.tmb_ready = None
            P.op("scalar", lambda e, xt=xt: e.activation(
                out=B.tmb[:, :], in_=xt, func=AF.Identity, bias=nmr, scale=rstd),
                inc=[("act", 1)])
            P.wait("tensor", "act")
            if self.pt_ready is not None:
                P.wait("tensor", *self.pt_ready)
                self.pt_ready = None
            for f in range(NC):
                P.op("tensor", lambda e, f=f: e.transpose(
                    B.pt[:, f * 128:(f + 1) * 128], B.tmb[:, f * 128:(f + 1) * 128],
                    B.ident[:, :]), inc=[("pe", 1)] if f == NC - 1 else ())
            self.tmb_ready = ("pe", P.count("pe"))
            P.wait("vector", "pe")
            ptv = B.pt.ap().rearrange("p (c n) -> p c n", c=8)
            P.op("vector", lambda e, t=t, ptv=ptv, dfm=dfm: e.tensor_copy(
                out=dfm[:, :, t * 128:(t + 1) * 128], in_=ptv), inc=[("dve", 1)])
            self.pt_ready = ("dve", P.count("dve"))
        self.small_reset()

    def load_cast(self, hbm_ap, dst_ap, cols, eng=None, cache_key=None, war=None):
        P, B = self.P, self.B
        assert cols <= 512, "load_cast piece too wide"
        import os
        if os.environ.get("MK_NOCACHE"):
            cache_key = None
        if cache_key is not None and cache_key in self._wc_map:
            idx, ready = self._wc_map[cache_key]
            P.wait("sync", *ready)
            for wmk in (war or []):
                P.wait("sync", *wmk)
            src = self.wcache.ap()[idx][:, 0:cols]
            P.op("sync", lambda e, dst_ap=dst_ap, src=src: e.dma_start(out=dst_ap, in_=src),
                 inc=[("wq", 16)])
            return ("wq", P.count("wq"))
        if eng is None:
            eng = "gpsimd"
        par = self._stage_par
        self._stage_par = (self._stage_par + 1) % 2
        st = B.stage[:, par * 512: par * 512 + cols]
        last = self._stage_busy[par]
        if last is not None:
            P.wait("sync", last[0], last[1])
        P.op("sync", lambda e, st=st, hbm_ap=hbm_ap: e.dma_start(out=st, in_=hbm_ap),
             inc=[("wq", 16)])
        P.wait(eng, "wq")
        for wmk in (war or []):
            P.wait(eng, *wmk)
        sem = CAST_SEM[eng]
        P.op(eng, lambda e, dst_ap=dst_ap, st=st: e.tensor_copy(out=dst_ap, in_=st),
             inc=[(sem, 1)])
        self._stage_busy[par] = (sem, P.count(sem))
        done = (sem, P.count(sem))
        if cache_key is not None:
            idx = self._wc_next
            self._wc_next += 1
            dst = self.wcache.ap()[idx][:, 0:cols]
            P.wait("gpsimd", sem, P.count(sem))
            P.op("gpsimd", lambda e, dst=dst, dst_ap=dst_ap: e.dma_start(out=dst, in_=dst_ap),
                 inc=[("wb", 16)])
            self._wc_map[cache_key] = (idx, ("wb", P.count("wb")))
        return done

    def wait_casts(self, engine):
        self.P.wait(engine, "pool")
        self.P.wait(engine, "dve")
        self.P.wait(engine, "wq")

    def load_w_fm(self, hbm, dst_view, kchunks=8, cols=1024, eng=None, ckpre=None):
        h = hbm.ap().rearrange("(k p) d -> p k d", p=128)
        done = None
        for k in range(kchunks):
            for half in range(0, cols, 512):
                wdt = min(512, cols - half)
                ck = (ckpre, k, half) if ckpre is not None else None
                done = self.load_cast(h[:, k, half:half + wdt],
                                      dst_view[:, k, half:half + wdt], wdt, eng=eng,
                                      cache_key=ck)
        return done

    def load_vec_ap(self, hbm_ap, dst_ap):
        P = self.P
        P.wait("sync", "dve")
        P.wait("sync", "act")
        P.wait("sync", "pool")
        P.op("sync", lambda e: e.dma_start(out=dst_ap, in_=hbm_ap), inc=[("wq", 16)])
        return ("wq", P.count("wq"))

    def load_vec(self, hbm, dst_ap):
        P = self.P
        P.wait("sync", "dve")
        P.wait("sync", "act")
        P.wait("sync", "pool")
        P.op("sync", lambda e: e.dma_start(out=dst_ap, in_=hbm.ap()), inc=[("wq", 16)])
        return ("wq", P.count("wq"))

    def emit_l1_ffn(self, W):
        P, B = self.P, self.B
        xn = B.fm(B.actA)
        xk = B.fm(B.actB)
        kact = B.fm(B.actC)
        o = self.small_col(16)
        mur = B.small[:, o:o + 8]
        muk = B.small[:, o + 8:o + 16]
        mv = self.load_vec(W["mu2"], B.small[:, o:o + 16])
        P.wait("vector", *mv)
        wAv = B.wA.ap().rearrange("p (k d) -> p k d", k=8)

        def token_mix(dst, mu):
            for c in range(NC):
                src = xn[:, c, :]
                P.op("vector", lambda e, src=src: e.tensor_tensor(
                    out=B.scr[:, 0:2047], in0=src[:, 1:2048], in1=src[:, 0:2047],
                    op=ALU.subtract))
                P.op("vector", lambda e, dst=dst, c=c, src=src, mu=mu: e.scalar_tensor_tensor(
                    out=dst[:, c, 1:2048], in0=B.scr[:, 0:2047], scalar=mu[:, c:c + 1],
                    in1=src[:, 0:2047], op0=ALU.mult, op1=ALU.add))
                P.op("vector", lambda e, dst=dst, c=c, src=src, mu=mu: e.tensor_scalar_mul(
                    dst[:, c, 0:1], src[:, 0:1], mu[:, c:c + 1]), inc=[("dve", 1)])

        token_mix(xk, muk)
        wdone = self.load_w_fm(W["WkT"], wAv)
        P.wait("tensor", *wdone)
        P.wait("tensor", "dve")
        par = 0
        first_evac = True
        for m in range(NC):
            for n in range(4):
                bank = par
                par ^= 1
                self.pe_wait_bank(bank)
                ps = B.pb[bank]
                for k in range(NC):
                    P.op("tensor", lambda e, ps=ps, k=k, m=m, n=n: e.matmul(
                        ps[:, 0:512], wAv[:, k, m * 128:(m + 1) * 128],
                        xk[:, k, n * 512:(n + 1) * 512],
                        start=(k == 0), stop=(k == NC - 1)),
                        inc=[("pe", 1)] if k == NC - 1 else ())
                P.wait("vector", "pe")
                if not first_evac:
                    P.wait("vector", "act")
                first_evac = False
                P.op("vector", lambda e, ps=ps: e.tensor_scalar_max(
                    B.scr[:, 0:512], ps[:, 0:512], 0.0), inc=[("dve", 1)])
                self.bank_consumed(bank, "dve")
                P.wait("scalar", "dve")
                P.op("scalar", lambda e, m=m, n=n: e.activation(
                    out=kact[:, m, n * 512:(n + 1) * 512], in_=B.scr[:, 0:512],
                    func=AF.Square), inc=[("act", 1)])
        P.wait("vector", "pe")
        xr = xk
        token_mix(xr, mur)
        wBv = B.wB.ap().rearrange("p (k d) -> p k d", k=8)
        wAv2 = wAv
        for dchunk in range(2):
            hv = W["WvT"].ap().rearrange("(k p) d -> p k d", p=128)
            hr = W["WrT"].ap().rearrange("(k p) d -> p k d", p=128)
            P.wait("gpsimd", "pe")
            dv = None
            for k in range(NC):
                dv = self.load_cast(hv[:, k, dchunk * 512:(dchunk + 1) * 512],
                                    wAv[:, k, 0:512], 512)
                dv = self.load_cast(hr[:, k, dchunk * 512:(dchunk + 1) * 512],
                                    wBv[:, k, 0:512], 512)
            P.wait("tensor", *dv)
            P.wait("tensor", "act")
            P.wait("tensor", "dve")
            for t in range(NT):
                bv = 0 if t % 2 == 0 else 1
                br = 2 if t % 2 == 0 else 3
                self.pe_wait_bank(bv)
                self.pe_wait_bank(br)
                psv, psr = B.pb[bv], B.pb[br]
                for k in range(NC):
                    P.op("tensor", lambda e, psv=psv, k=k, t=t: e.matmul(
                        psv[:, 0:512], kact[:, k, t * 128:(t + 1) * 128], wAv[:, k, 0:512],
                        start=(k == 0), stop=(k == NC - 1)))
                for k in range(NC):
                    P.op("tensor", lambda e, psr=psr, k=k, t=t: e.matmul(
                        psr[:, 0:512], xr[:, k, t * 128:(t + 1) * 128], wBv[:, k, 0:512],
                        start=(k == 0), stop=(k == NC - 1)),
                        inc=[("pe", 1)] if k == NC - 1 else ())
                P.wait("scalar", "pe")
                P.wait("scalar", "dve")
                P.op("scalar", lambda e, psr=psr: e.activation(
                    out=B.scr[:, 1024:1536], in_=psr[:, 0:512], func=AF.Sigmoid),
                    inc=[("act", 1)])
                self.bank_consumed(br, "act")
                P.wait("vector", "act")
                P.op("vector", lambda e, psv=psv: e.tensor_tensor(
                    out=B.scr[:, 0:512], in0=psv[:, 0:512], in1=B.scr[:, 1024:1536],
                    op=ALU.mult), inc=[("dve", 1)])
                self.bank_consumed(bv, "dve")
                xsl = B.x[:, t, dchunk * 512:(dchunk + 1) * 512]
                P.op("vector", lambda e, xsl=xsl: e.scalar_tensor_tensor(
                    out=xsl, in0=B.scr[:, 0:512], scalar=0.5, in1=xsl,
                    op0=ALU.mult, op1=ALU.add), inc=[("dve", 1)])
        self.small_reset()



def prep_l1_ffn(inp):
    t = lambda a: np.ascontiguousarray(np.asarray(a).T.astype(np.float32))
    vec = lambda v: np.ascontiguousarray(np.asarray(v, np.float32).reshape(8, 128).T)
    return {
        "l1f_WrT": t(inp["L1_ffn_W_r"]),
        "l1f_WkT": t(inp["L1_ffn_W_k"]),
        "l1f_WvT": t(inp["L1_ffn_W_v"]),
        "l1f_mu2": np.ascontiguousarray(np.concatenate(
            [vec(inp["L1_ffn_mu_r"]), vec(inp["L1_ffn_mu_k"])], 1)),
    }


def build_test_l1(weight_names):
    nc = bass.Bass("TRN2", target_bir_lowering=False)
    x_in = nc.dram_tensor("x", [T, D], F32, kind="ExternalInput")
    x_out = nc.dram_tensor("xout", [T, D], F32, kind="ExternalOutput")
    W = {n: nc.dram_tensor(f"l1f_{n}", s, F32, kind="ExternalInput")
         for n, s in [("WrT", [D, D]), ("WkT", [D, D]), ("WvT", [D, D]),
                      ("mu2", [128, 16])]}
    with ExitStack() as ctx:
        B = Buffers(nc, ctx)
        P = Prog(nc)
        E = Emit(nc, P, B, ctx)
        E.wcache = nc.dram_tensor("wcache", [1024, 128, 512], BF16, kind="Internal")
        E.init_consts()
        E.load_x(x_in)
        E.emit_ln()
        E.emit_l1_ffn(W)
        E.store_x(x_out)
        dbg1 = nc.dram_tensor("dbg1", [128, 2048], F32, kind="ExternalOutput")
        dbg2 = nc.dram_tensor("dbg2", [128, 16384], BF16, kind="ExternalOutput")
        P.op("gpsimd", lambda e: e.dma_start(out=dbg1.ap(), in_=B.scr[:, :]), inc=[("out", 16)])
        P.op("gpsimd", lambda e: e.dma_start(out=dbg2.ap(), in_=B.actA[:, :]), inc=[("out", 16)])
        P.wait("gpsimd", "out")
        block = ctx.enter_context(nc.Block())
        P.replay(block)
    return nc



def _emit_l3(self):
    pass


class EmitL3:
    pass


def emit_l3_build_M(E, W, M_dram):
    P, B = E.P, E.B
    PI = float(np.pi)
    aAv = B.fm(B.actA)
    aBv = B.fm(B.actB)
    aCv = B.fm(B.actC)
    for k in range(NC):
        ha = W["A"].ap().rearrange("(k p) d -> p k d", p=128)
        hb = W["Bi"].ap().rearrange("(k p) d -> p k d", p=128)
        for hf in range(2):
            E.load_cast(ha[:, k, hf * 512:(hf + 1) * 512],
                        aAv[:, k, hf * 512:(hf + 1) * 512], 512)
            wdone = E.load_cast(hb[:, k, hf * 512:(hf + 1) * 512],
                                aAv[:, k, 1024 + hf * 512:1024 + (hf + 1) * 512], 512)
    CRND = 12582912.0
    o = E.small_col(1)
    icol = B.small[:, o:o + 1]
    s0 = B.scr[:, 0:1024]
    s1 = B.scr[:, 1024:2048]
    for k in range(NC):
        P.wait("gpsimd", "act")
        P.op("gpsimd", lambda e: e.iota(s0, [[1, 1024]], base=0,
                                        channel_multiplier=0,
                                        allow_small_or_imprecise_dtypes=True))
        P.op("gpsimd", lambda e, k=k: e.iota(icol, [[1, 1]], base=k * 128,
                                             channel_multiplier=1,
                                             allow_small_or_imprecise_dtypes=True),
             inc=[("pool", 1)])
        P.wait("vector", "pool")
        P.op("vector", lambda e: e.tensor_scalar(
            out=s1, in0=s0, scalar1=icol, scalar2=2.0 ** -10,
            op0=ALU.mult, op1=ALU.mult))
        P.op("vector", lambda e: e.tensor_scalar(
            out=s0, in0=s1, scalar1=CRND, scalar2=CRND, op0=ALU.add, op1=ALU.subtract))
        P.op("vector", lambda e: e.tensor_tensor(
            out=s0, in0=s1, in1=s0, op=ALU.subtract), inc=[("dve", 1)])
        P.wait("scalar", "dve")
        P.op("scalar", lambda e, k=k: e.activation(
            out=aBv[:, k, 1024:2048], in_=s0, func=AF.Sin,
            scale=2 * float(np.pi)), inc=[("act", 1)])
        P.wait("vector", "act")
        P.op("vector", lambda e: e.tensor_scalar(out=s0, in0=s1, scalar1=0.25, scalar2=None,
                                                 op0=ALU.add))
        P.op("vector", lambda e: e.tensor_scalar(
            out=s1, in0=s0, scalar1=CRND, scalar2=CRND, op0=ALU.add, op1=ALU.subtract))
        P.op("vector", lambda e: e.tensor_tensor(
            out=s0, in0=s0, in1=s1, op=ALU.subtract), inc=[("dve", 1)])
        P.wait("scalar", "dve")
        P.op("scalar", lambda e, k=k: e.activation(
            out=aBv[:, k, 0:1024], in_=s0, func=AF.Sin,
            scale=2 * float(np.pi)), inc=[("act", 1)])
    E.small_reset()
    P.wait("tensor", *wdone)
    P.wait("tensor", "act")
    P.wait("tensor", "dve")
    for m in range(NC):
        for n in range(2):
            nsl = slice(n * 512, (n + 1) * 512)
            b0, b1 = (0, 1) if (m * 2 + n) % 2 == 0 else (2, 3)
            E.pe_wait_bank(b0)
            E.pe_wait_bank(b1)
            p0, p1 = B.pb[b0], B.pb[b1]
            for k in range(NC):
                P.op("tensor", lambda e, p0=p0, k=k, m=m, nsl=nsl: e.matmul(
                    p0[:, 0:512], aAv[:, k, m * 128:(m + 1) * 128],
                    aBv[:, k, nsl], start=(k == 0), stop=False))
            for k in range(NC):
                P.op("tensor", lambda e, p0=p0, k=k, m=m, nsl=nsl: e.matmul(
                    p0[:, 0:512], aAv[:, k, 1024 + m * 128:1024 + (m + 1) * 128],
                    aBv[:, k, 1024 + nsl.start:1024 + nsl.stop],
                    start=False, stop=(k == NC - 1)), inc=[("pe", 1)] if k == NC - 1 else ())
            for k in range(NC):
                P.op("tensor", lambda e, p1=p1, k=k, m=m, nsl=nsl: e.matmul(
                    p1[:, 0:512], aAv[:, k, 1024 + m * 128:1024 + (m + 1) * 128],
                    aBv[:, k, nsl], start=(k == 0), stop=(k == NC - 1)),
                    inc=[("pe", 1)] if k == NC - 1 else ())
            P.wait("scalar", "pe")
            P.op("scalar", lambda e, p0=p0, m=m, nsl=nsl: e.activation(
                out=aCv[:, m, nsl], in_=p0[:, 0:512], func=AF.Copy), inc=[("act", 1)])
            E.bank_consumed(b0, "act")
            b2 = 4 if (m * 2 + n) % 2 == 0 else 5
            E.pe_wait_bank(b2)
            p2 = B.pb[b2]
            for k in range(NC):
                P.op("tensor", lambda e, p2=p2, k=k, m=m, nsl=nsl: e.matmul(
                    p2[:, 0:512], aAv[:, k, m * 128:(m + 1) * 128],
                    aBv[:, k, 1024 + nsl.start:1024 + nsl.stop],
                    start=(k == 0), stop=(k == NC - 1)), inc=[("pe", 1)] if k == NC - 1 else ())
            P.wait("vector", "pe")
            P.wait("vector", "act")
            P.op("vector", lambda e, p1=p1: e.tensor_copy(out=B.scr[:, 0:512], in_=p1[:, 0:512]))
            P.op("vector", lambda e, p2=p2, m=m, nsl=nsl: e.tensor_tensor(
                out=aCv[:, m, 1024 + nsl.start:1024 + nsl.stop], in0=B.scr[:, 0:512],
                in1=p2[:, 0:512], op=ALU.subtract), inc=[("dve", 1)])
            E.bank_consumed(b1, "dve")
            E.bank_consumed(b2, "dve")
    P.wait("tensor", "act")
    P.wait("tensor", "dve")
    Mv = M_dram.ap()
    tmb_busy = [None, None]
    for m in range(NC):
        for n in range(2):
            nsl = slice(n * 512, (n + 1) * 512)
            b0, b1 = (0, 1) if (m * 2 + n) % 2 == 0 else (2, 3)
            E.pe_wait_bank(b0)
            E.pe_wait_bank(b1)
            p0, p1 = B.pb[b0], B.pb[b1]
            for k in range(NC):
                P.op("tensor", lambda e, p0=p0, k=k, m=m, nsl=nsl: e.matmul(
                    p0[:, 0:512], aCv[:, k, m * 128:(m + 1) * 128],
                    aBv[:, k, nsl], start=(k == 0), stop=(k == NC - 1)),
                    inc=[("pe", 1)] if k == NC - 1 else ())
            for k in range(NC):
                P.op("tensor", lambda e, p1=p1, k=k, m=m, nsl=nsl: e.matmul(
                    p1[:, 0:512], aCv[:, k, 1024 + m * 128:1024 + (m + 1) * 128],
                    aBv[:, k, 1024 + nsl.start:1024 + nsl.stop],
                    start=(k == 0), stop=(k == NC - 1)), inc=[("pe", 1)] if k == NC - 1 else ())
            P.wait("scalar", "pe")
            P.wait("scalar", "dve")
            par = (m * 2 + n) % 2
            P.op("scalar", lambda e, p0=p0: e.activation(
                out=B.scr[:, 1024:1536], in_=p0[:, 0:512], func=AF.Copy, scale=1.0 / 1024),
                inc=[("act", 1)])
            E.bank_consumed(b0, "act")
            P.wait("vector", "act")
            if tmb_busy[par] is not None:
                P.wait("vector", *tmb_busy[par])
            P.op("vector", lambda e, p1=p1, par=par: e.scalar_tensor_tensor(
                out=B.tmb[:, par * 512:(par + 1) * 512], in0=p1[:, 0:512],
                scalar=-1.0 / 1024, in1=B.scr[:, 1024:1536], op0=ALU.mult, op1=ALU.add),
                inc=[("dve", 1)])
            E.bank_consumed(b1, "dve")
            P.wait("gpsimd", "dve")
            P.op("gpsimd", lambda e, m=m, nsl=nsl, par=par: e.dma_start(
                out=Mv[m, :, nsl], in_=B.tmb[:, par * 512:(par + 1) * 512]),
                inc=[("out", 16)])
            tmb_busy[par] = ("out", P.count("out"))
    P.wait("gpsimd", "out")
    E.tmb_ready = None


def emit_l3_apply(E, W, M_dram, first_call):
    P, B = E.P, E.B
    xn = B.fm(B.actA)
    wAv = B.wA.ap().rearrange("p (k d) -> p k d", k=8)
    for sl in range(2):
        if E._stage_busy[sl] is not None:
            P.wait("sync", *E._stage_busy[sl])
            E._stage_busy[sl] = None
    P.op("sync", lambda e: e.dma_start(out=B.stage[0:1, 0:512], in_=W["bias"].ap()[:, 0:512]),
         inc=[("wq", 16)])
    P.op("sync", lambda e: e.dma_start(out=B.stage[0:1, 512:1024], in_=W["bias"].ap()[:, 512:1024]),
         inc=[("wq", 16)])
    P.wait("gpsimd", "wq")
    if E.tmb_ready is not None:
        P.wait("gpsimd", *E.tmb_ready)
        E.tmb_ready = None
    P.op("gpsimd", lambda e: e.tensor_copy(out=B.tmb[0:1, 0:512], in_=B.stage[0:1, 0:512]))
    P.op("gpsimd", lambda e: e.tensor_copy(out=B.tmb[0:1, 512:1024], in_=B.stage[0:1, 512:1024]),
         inc=[("pool", 1)])
    E._stage_busy[0] = ("pool", P.count("pool"))
    E._stage_busy[1] = ("pool", P.count("pool"))
    P.wait("tensor", "pool")
    P.wait("tensor", "dve")
    E.pe_wait_bank(4)
    E.pe_wait_bank(5)
    for h in range(2):
        P.op("tensor", lambda e, h=h: e.matmul(
            B.pb[4 + h][:, 0:512], B.onesb[0:1, :], B.tmb[0:1, h * 512:(h + 1) * 512],
            start=True, stop=True), inc=[("pe", 1)])
    P.wait("vector", "pe")
    for h in range(2):
        P.op("vector", lambda e, h=h: e.tensor_copy(
            out=B.scr[:, 1024 + h * 512:1024 + (h + 1) * 512], in_=B.pb[4 + h][:, 0:512]),
            inc=[("dve", 1)])
        E.bank_consumed(4 + h, "dve")
    Mv = M_dram.ap()
    for dchunk in range(2):
        P.wait("sync", "pe")
        dsl = slice(dchunk * 512, (dchunk + 1) * 512)
        mm = Mv[:, :, dsl].rearrange("k p n -> p k n")
        P.op("sync", lambda e, mm=mm: e.dma_start(out=wAv[:, :, 0:512], in_=mm),
             inc=[("wq", 16)])
        P.wait("tensor", "wq")
        for t in range(NT):
            bank = t % 4
            E.pe_wait_bank(bank)
            ps = B.pb[bank]
            for k in range(NC):
                P.op("tensor", lambda e, ps=ps, k=k, t=t: e.matmul(
                    ps[:, 0:512], xn[:, k, t * 128:(t + 1) * 128], wAv[:, k, 0:512],
                    start=(k == 0), stop=(k == NC - 1)), inc=[("pe", 1)] if k == NC - 1 else ())
            P.wait("vector", "pe")
            P.op("vector", lambda e, ps=ps, dchunk=dchunk: e.tensor_tensor(
                out=B.scr[:, 0:512], in0=ps[:, 0:512],
                in1=B.scr[:, 1024 + dchunk * 512:1024 + (dchunk + 1) * 512], op=ALU.add),
                inc=[("dve", 1)])
            E.bank_consumed(bank, "dve")
            xsl = B.x[:, t, dsl]
            P.op("vector", lambda e, xsl=xsl: e.scalar_tensor_tensor(
                out=xsl, in0=B.scr[:, 0:512], scalar=0.5, in1=xsl,
                op0=ALU.mult, op1=ALU.add), inc=[("dve", 1)])


def prep_l3_ffn(inp):
    return {
        "l3f_A": np.ascontiguousarray(np.asarray(inp["L3_ffn_w_freq_real"], np.float32)),
        "l3f_Bi": np.ascontiguousarray(np.asarray(inp["L3_ffn_w_freq_imag"], np.float32)),
        "l3f_bias": np.ascontiguousarray(np.asarray(inp["L3_ffn_bias"], np.float32).reshape(1, 1024)),
    }


def build_test_l3(_):
    nc = bass.Bass("TRN2", target_bir_lowering=False)
    x_in = nc.dram_tensor("x", [T, D], F32, kind="ExternalInput")
    x_out = nc.dram_tensor("xout", [T, D], F32, kind="ExternalOutput")
    W = {"A": nc.dram_tensor("l3f_A", [D, D], F32, kind="ExternalInput"),
         "Bi": nc.dram_tensor("l3f_Bi", [D, D], F32, kind="ExternalInput"),
         "bias": nc.dram_tensor("l3f_bias", [1, D], F32, kind="ExternalInput")}
    M_dram = nc.dram_tensor("l3f_M", [8, 128, 1024], BF16, kind="ExternalOutput")
    with ExitStack() as ctx:
        B = Buffers(nc, ctx)
        P = Prog(nc)
        E = Emit(nc, P, B, ctx)
        E.wcache = nc.dram_tensor("wcache", [1024, 128, 512], BF16, kind="Internal")
        E.init_consts(extra_consts=[-float(np.pi)])
        E.load_x(x_in)
        emit_l3_build_M(E, W, M_dram)
        E.emit_ln()
        emit_l3_apply(E, W, M_dram, True)
        E.store_x(x_out)
        dbg1 = nc.dram_tensor("dbg1", [128, 2048], F32, kind="ExternalOutput")
        dbg2 = nc.dram_tensor("dbg2", [128, 16384], BF16, kind="ExternalOutput")
        P.op("gpsimd", lambda e: e.dma_start(out=dbg1.ap(), in_=B.scr[:, :]), inc=[("out", 16)])
        P.op("gpsimd", lambda e: e.dma_start(out=dbg2.ap(), in_=B.actA[:, :]), inc=[("out", 16)])
        P.wait("gpsimd", "out")
        block = ctx.enter_context(nc.Block())
        P.replay(block)
    return nc


def emit_bias_bc(E, hbm_row, dst_off, banks=(2, 3)):
    P, B = E.P, E.B
    for sl in range(2):
        if E._stage_busy[sl] is not None:
            P.wait("sync", *E._stage_busy[sl])
            E._stage_busy[sl] = None
    P.op("sync", lambda e: e.dma_start(out=B.stage[0:1, 0:512], in_=hbm_row.ap()[:, 0:512]),
         inc=[("wq", 16)])
    P.op("sync", lambda e: e.dma_start(out=B.stage[0:1, 512:1024], in_=hbm_row.ap()[:, 512:1024]),
         inc=[("wq", 16)])
    P.wait("gpsimd", "wq")
    P.wait("gpsimd", "pe")
    P.op("gpsimd", lambda e: e.tensor_copy(out=B.rowb[0:1, 0:512], in_=B.stage[0:1, 0:512]))
    P.op("gpsimd", lambda e: e.tensor_copy(out=B.rowb[0:1, 512:1024], in_=B.stage[0:1, 512:1024]),
         inc=[("pool", 1)])
    E._stage_busy[0] = ("pool", P.count("pool"))
    E._stage_busy[1] = ("pool", P.count("pool"))
    P.wait("tensor", "pool")
    for i, bk in enumerate(banks):
        E.pe_wait_bank(bk)
        P.op("tensor", lambda e, i=i, bk=bk: e.matmul(
            B.pb[bk][:, 0:512], B.onesb[0:1, :], B.rowb[0:1, i * 512:(i + 1) * 512],
            start=True, stop=True), inc=[("pe", 1)])
    P.wait("vector", "pe")
    for i, bk in enumerate(banks):
        P.op("vector", lambda e, i=i, bk=bk: e.tensor_copy(
            out=B.scr[:, dst_off + i * 512:dst_off + (i + 1) * 512], in_=B.pb[bk][:, 0:512]),
            inc=[("dve", 1)])
        E.bank_consumed(bk, "dve")


def emit_attn(E, cfg, W, stats_dram):
    P, B = E.P, E.B
    causal, l3 = cfg["causal"], cfg["l3"]
    qscale = cfg["qscale"]
    has_b = cfg["has_qkv_bias"]
    xn = B.fm(B.actA)
    vtm = B.tm(B.actB)
    wAf = B.wA.ap()
    qaug = [wAf[:, 0:2048], wAf[:, 2048:4096]]
    kaug = [wAf[:, 4096:6144], wAf[:, 6144:8192]]
    wBf = B.wB.ap()
    qsl = [wBf[:, 0:512].rearrange("p (k j) -> p k j", k=8),
           wBf[:, 1024:1536].rearrange("p (k j) -> p k j", k=8)]
    ksl = [wBf[:, 512:1024].rearrange("p (k j) -> p k j", k=8),
           wBf[:, 1536:2048].rearrange("p (k j) -> p k j", k=8)]
    ptile = [wBf[:, 2048:2560], wBf[:, 2560:3072]]
    sqb = wBf[:, 3072:3584]
    NH = 16

    o = E.small_col(52)
    bq8 = B.small[:, o:o + 8]
    bk0 = B.small[:, o + 8:o + 16]
    nm4 = B.small[:, o + 16:o + 20]
    nth = B.small[:, o + 20:o + 36]
    hi32 = B.small[:, o + 36:o + 52]
    o15 = E.small_col(4)
    nm4b = B.small[:, o15:o15 + 4]
    o16 = E.small_col(32)
    nthq = B.small[:, o16:o16 + 16]
    lh16 = B.small[:, o16 + 16:o16 + 32]
    o2 = E.small_col(12 + 32)
    qscrow = B.small[0:1, o2:o2 + 16]
    kscrow = B.small[0:1, o2 + 16:o2 + 32]

    bias_ready = None
    if has_b:
        d1 = E.load_vec(W["bqk"], B.small[:, o:o + 16])
        bias_ready = ("wq-scalar", d1)
    if l3:
        P.wait("sync", "dve")
        P.wait("sync", "act")
        P.op("sync", lambda e: e.dma_start(out=qscrow, in_=W["qsc"].ap()), inc=[("wq", 16)])
        P.op("sync", lambda e: e.dma_start(out=kscrow, in_=W["ksc"].ap()), inc=[("wq", 16)])
        P.wait("vector", "wq")

    if cfg["has_v_bias"]:
        emit_bias_bc(E, W["bv"], 0)
    wAv_big = wAf.rearrange("p (k d) -> p k d", k=8)
    hv = W["Wv"].ap().rearrange("(k p) d -> p k d", p=128)
    for dchunk in range(2):
        dv = None
        for k in range(NC):
            dv = E.load_cast(hv[:, k, dchunk * 512:(dchunk + 1) * 512],
                             wAv_big[:, k, 0:512] if dchunk == 0 else wAv_big[:, k, 512:1024],
                             512)
        P.wait("tensor", *dv)
        for t in range(NT):
            bank = t % 4
            E.pe_wait_bank(bank)
            ps = B.pb[bank]
            wcol = 0 if dchunk == 0 else 512
            for k in range(NC):
                P.op("tensor", lambda e, ps=ps, k=k, t=t, wcol=wcol: e.matmul(
                    ps[:, 0:512], xn[:, k, t * 128:(t + 1) * 128],
                    wAv_big[:, k, wcol:wcol + 512],
                    start=(k == 0), stop=(k == NC - 1)), inc=[("pe", 1)] if k == NC - 1 else ())
            dst = vtm[:, t, dchunk * 512:(dchunk + 1) * 512]
            if l3:
                P.wait("scalar", "pe")
                P.op("scalar", lambda e, ps=ps, dst=dst: e.activation(
                    out=dst, in_=ps[:, 0:512], func=AF.Silu), inc=[("act", 1)])
                E.bank_consumed(bank, "act")
            elif cfg["has_v_bias"]:
                P.wait("vector", "pe")
                P.op("vector", lambda e, ps=ps, dst=dst, dchunk=dchunk: e.tensor_tensor(
                    out=dst, in0=ps[:, 0:512], in1=B.scr[:, dchunk * 512:(dchunk + 1) * 512],
                    op=ALU.add), inc=[("dve", 1)])
                E.bank_consumed(bank, "dve")
            else:
                P.wait("vector", "pe")
                P.op("vector", lambda e, ps=ps, dst=dst: e.tensor_copy(
                    out=dst, in_=ps[:, 0:512]), inc=[("dve", 1)])
                E.bank_consumed(bank, "dve")

    v_done = ("dve", P.count("dve")) if not l3 else ("act", P.count("act"))
    P.wait("gpsimd", "pe")
    for par in range(2):
        P.op("gpsimd", lambda e, par=par: e.memset(kaug[par][64:66, :], 1.0),
             inc=[("pool", 1)])
    P.wait("tensor", "pool")

    aug_busy = [None, None]
    pt_busy = [None, None]
    slice_busy = [None, None]
    qk_done = [None] * NH
    stats_dma = [None] * NH
    Wqh, Wkh = W["Wqh"].ap(), W["Wkh"].ap()

    qk_ld = [None] * NH

    def qk_loads(h):
        par = h % 2
        if slice_busy[par] is not None:
            P.wait("gpsimd", *slice_busy[par])
        done = None
        for which in ("q", "k"):
            hb = (Wqh if which == "q" else Wkh)[h].rearrange("p kk j -> p (kk j)")
            dstsl = (qsl if which == "q" else ksl)[par]
            done = E.load_cast(hb, dstsl.rearrange("p k j -> p (k j)"), 512)
        qk_ld[h] = done

    def qk_proj(h):
        par = h % 2
        P.wait("tensor", *qk_ld[h])
        if aug_busy[par] is not None:
            P.wait("tensor", *aug_busy[par])
        if bias_ready is not None:
            P.wait("scalar", *bias_ready[1])
        for which in ("q", "k"):
            dst = (qaug if which == "q" else kaug)[par]
            wsl = (qsl if which == "q" else ksl)[par]
            bias = (bq8 if which == "q" else bk0) if has_b else None
            scale = 1.0
            func = AF.Silu if l3 else AF.Identity
            for n in range(4):
                bank = n
                E.pe_wait_bank(bank)
                ps = B.pb[bank]
                for k in range(NC):
                    P.op("tensor", lambda e, ps=ps, k=k, n=n, wsl=wsl: e.matmul(
                        ps[0:64, 0:512], wsl[:, k, :], xn[:, k, n * 512:(n + 1) * 512],
                        start=(k == 0), stop=(k == NC - 1)),
                        inc=[("pe", 1)] if k == NC - 1 else ())
                bcol = (bias[(h % 2) * 64:(h % 2) * 64 + 64, h // 2:h // 2 + 1]
                        if bias is not None else None)
                if (not l3) and n % 2 == 1:
                    P.wait("vector", "pe")
                    if bcol is not None:
                        P.op("vector", lambda e, ps=ps, dst=dst, n=n, bcol=bcol:
                             e.tensor_scalar(out=dst[0:64, n * 512:(n + 1) * 512],
                                             in0=ps[0:64, 0:512], scalar1=bcol,
                                             scalar2=None, op0=ALU.add), inc=[("dve", 1)])
                    else:
                        P.op("vector", lambda e, ps=ps, dst=dst, n=n: e.tensor_copy(
                            out=dst[0:64, n * 512:(n + 1) * 512], in_=ps[0:64, 0:512]),
                            inc=[("dve", 1)])
                    E.bank_consumed(bank, "dve")
                else:
                    P.wait("scalar", "pe")
                    kw = dict(func=func, scale=scale)
                    if bcol is not None:
                        kw["bias"] = bcol
                    P.op("scalar", lambda e, ps=ps, dst=dst, n=n, kw=kw: e.activation(
                        out=dst[0:64, n * 512:(n + 1) * 512], in_=ps[0:64, 0:512], **kw),
                        inc=[("act", 1)])
                    E.bank_consumed(bank, "act")
        slice_busy[par] = ("pe", P.count("pe"))
        if l3:
            for which in ("q", "k"):
                dst = (qaug if which == "q" else kaug)[par]
                scrw = qscrow if which == "q" else kscrow
                for n in range(4):
                    P.wait("scalar", "dve")
                    P.wait("scalar", "pe")
                    P.op("scalar", lambda e, dst=dst, n=n: e.activation(
                        out=sqb[0:64, 0:512], in_=dst[0:64, n * 512:(n + 1) * 512],
                        func=AF.Square), inc=[("act", 1)])
                    bank = 4 + n % 2
                    E.pe_wait_bank(bank)
                    P.wait("tensor", "act")
                    P.op("tensor", lambda e, bank=bank: e.matmul(
                        B.pb[bank][0:1, 0:512], B.onesb[0:64, 0:1], sqb[0:64, 0:512],
                        start=True, stop=True), inc=[("pe", 1)])
                    P.wait("vector", "pe")
                    P.op("vector", lambda e, bank=bank, n=n: e.tensor_copy(
                        out=B.scr[0:1, n * 512:(n + 1) * 512], in_=B.pb[bank][0:1, 0:512]),
                        inc=[("dve", 1)])
                    E.bank_consumed(bank, "dve")
                P.wait("scalar", "dve")
                P.wait("vector", "pe")
                for half in (0, 1):
                    r2h = B.scr[0:1, half * 1024:half * 1024 + 1024]
                    rbh = B.rowb[0:1, half * 1024:half * 1024 + 1024]
                    P.op("scalar", lambda e, r2h=r2h: e.activation(
                        out=r2h, in_=r2h, func=AF.Ln))
                    P.op("scalar", lambda e, r2h=r2h: e.activation(
                        out=r2h, in_=r2h, func=AF.Exp, scale=0.5), inc=[("act", 1)])
                    P.wait("vector", "act")
                    P.op("vector", lambda e, r2h=r2h, rbh=rbh, scrw=scrw, h=h:
                         e.tensor_scalar(out=rbh, in0=r2h, scalar1=scrw[0:1, h:h + 1],
                                         scalar2=None, op0=ALU.mult))
                    P.op("vector", lambda e, r2h=r2h: e.tensor_scalar(
                        out=r2h, in0=r2h, scalar1=1e-6, scalar2=None, op0=ALU.add))
                    P.op("vector", lambda e, r2h=r2h: e.reciprocal(r2h, r2h))
                    P.op("vector", lambda e, r2h=r2h, rbh=rbh: e.tensor_tensor(
                        out=rbh, in0=rbh, in1=r2h, op=ALU.mult), inc=[("dve", 1)])
                for n in range(4):
                    bank = 4 + n % 2
                    E.pe_wait_bank(bank)
                    P.wait("tensor", "dve")
                    P.op("tensor", lambda e, bank=bank, n=n: e.matmul(
                        B.pb[bank][0:64, 0:512], B.onesb[0:1, 0:64],
                        B.rowb[0:1, n * 512:(n + 1) * 512], start=True, stop=True),
                        inc=[("pe", 1)])
                    P.wait("vector", "pe")
                    P.op("vector", lambda e, dst=dst, bank=bank, n=n: e.tensor_tensor(
                        out=dst[0:64, n * 512:(n + 1) * 512],
                        in0=dst[0:64, n * 512:(n + 1) * 512],
                        in1=B.pb[bank][0:64, 0:512], op=ALU.mult), inc=[("dve", 1)])
                    E.bank_consumed(bank, "dve")
            qk_done[h] = ("dve", P.count("dve"))
        else:
            qk_done[h] = ("act", P.count("act"))

    def pass_a(h):
        par = h % 2
        P.wait("tensor", *qk_done[h])
        for i in range(NT):
            kl = (i + 1) * 128 if causal else T
            nch = (kl + 511) // 512
            for c in range(nch):
                w = min(512, kl - c * 512)
                E.pe_wait_bank(c)
                P.op("tensor", lambda e, c=c, par=par, i=i, w=w: e.matmul(
                    B.pb[c][:, 0:w], qaug[par][0:64, i * 128:(i + 1) * 128],
                    kaug[par][0:64, c * 512:c * 512 + w], start=True, stop=True),
                    inc=[("pe", 1)])
            P.wait("vector", "pe")
            if causal:
                c = (i * 128) // 512
                off = i * 128 - c * 512
                P.op("vector", lambda e, c=c, off=off: e.tensor_tensor(
                    out=B.pb[c][:, off:off + 128], in0=B.pb[c][:, off:off + 128],
                    in1=B.maskTm[:, :], op=ALU.add))
            for c in range(nch):
                w = min(512, kl - c * 512)
                P.op("vector", lambda e, c=c, w=w: e.reduce_max(
                    nm4[:, c:c + 1], B.pb[c][:, 0:w], axis=AX.X, negate=True),
                    inc=[("dve", 1)])
            P.wait("gpsimd", "dve")
            P.op("gpsimd", lambda e, nch=nch: e.tensor_copy(out=nm4b[:, 0:nch], in_=nm4[:, 0:nch]),
                 inc=[("pool", 1)])
            P.wait("vector", "pool")
            P.op("vector", lambda e, i=i, nch=nch: e.tensor_reduce(
                nth[:, i:i + 1], nm4b[:, 0:nch], axis=AX.X, op=ALU.min),
                inc=[("dve", 1)])
            P.wait("gpsimd", "dve")
            P.op("gpsimd", lambda e, i=i: e.tensor_scalar(
                out=nthq[:, i:i + 1], in0=nth[:, i:i + 1], scalar1=qscale, scalar2=None,
                op0=ALU.mult), inc=[("pool", 1)])
            P.wait("scalar", "pool")
            for c in range(nch):
                w = min(512, kl - c * 512)
                P.op("scalar", lambda e, c=c, w=w, i=i: e.activation(
                    out=B.tmb[:, 0:w], in_=B.pb[c][:, 0:w], func=AF.Exp,
                    bias=nthq[:, i:i + 1], scale=qscale, accum_out=nm4[:, c:c + 1]),
                    inc=[("act", 1)])
                E.bank_consumed(c, "act")
            P.wait("vector", "act")
            P.op("vector", lambda e, i=i, nch=nch: e.reduce_sum(
                lh16[:, i:i + 1], nm4[:, 0:nch], axis=AX.X), inc=[("dve", 1)])
        P.wait("scalar", "dve")
        P.op("scalar", lambda e: e.activation(out=lh16[:, :], in_=lh16[:, :], func=AF.Ln),
             inc=[("act", 1)])
        P.wait("vector", "act")
        P.op("vector", lambda e: e.scalar_tensor_tensor(
            out=nth[:, :], in0=lh16[:, :], scalar=-1.0 / qscale, in1=nth[:, :],
            op0=ALU.mult, op1=ALU.add), inc=[("dve", 1)])

    nthl = B.smallb[:, 0:32].rearrange("p (t s) -> p t s", s=2)
    flipst = B.smallb[:, 32:160]

    def stat_flip(h):
        par = h % 2
        P.wait("gpsimd", "dve")
        P.op("gpsimd", lambda e: e.tensor_copy(out=B.scr[:, 96:112], in_=nth[:, :]),
             inc=[("pool", 1)])
        P.wait("vector", "pool")
        P.op("vector", lambda e: e.tensor_copy(out=nthl[:, :, 0:1], in_=B.scr[:, 96:112]),
             inc=[("dve", 1)])
        P.wait("gpsimd", "dve")
        P.op("gpsimd", lambda e: e.tensor_copy(out=hi32, in_=nthl[:, :, 0:1]),
             inc=[("pool", 1)])
        P.wait("vector", "pool")
        P.op("vector", lambda e: e.tensor_tensor(out=nthl[:, :, 1:2],
                                                 in0=B.scr[:, 96:112],
                                                 in1=hi32, op=ALU.subtract), inc=[("dve", 1)])
        P.wait("tensor", "dve")
        if E.pt_ready is not None:
            P.wait("tensor", *E.pt_ready)
            E.pt_ready = None
        P.op("tensor", lambda e: e.transpose(B.pt[0:32, 0:128], B.smallb[:, 0:32],
                                             B.ident[:, :]), inc=[("pe", 1)])
        P.wait("vector", "pe")
        P.wait("vector", "out")
        P.op("vector", lambda e: e.tensor_copy(out=flipst[0:32, :], in_=B.pt[0:32, 0:128]),
             inc=[("dve", 1)])
        E.pt_ready = ("dve", P.count("dve"))
        P.wait("gpsimd", "dve")
        P.op("gpsimd", lambda e, h=h: e.dma_start(out=stats_dram.ap()[h], in_=flipst[0:32, :]),
             inc=[("out", 16)])
        P.wait("gpsimd", "out")
        if aug_busy[par] is not None:
            P.wait("gpsimd", *aug_busy[par])
        src = stats_dram.ap()[h].rearrange("(t s) q -> s t q", s=2)
        dstap = qaug[par][64:66, :].rearrange("p (t q) -> p t q", t=16)
        P.op("gpsimd", lambda e, dstap=dstap, src=src: e.dma_start(
            out=dstap, in_=src), inc=[("out", 16)])
        stats_dma[h] = P.count("out")

    def pass_b(h):
        par = h % 2
        P.wait("tensor", *v_done)
        P.wait("tensor", "out", stats_dma[h])
        sidx = 0
        jlast = lambda c: min(NT - 1, 4 * c + 3) if causal else NT - 1
        for j in range(NT):
            c0 = (j * 128) // 512 if causal else 0
            for c in range(c0, 4):
                qs = max(j * 128, c * 512) if causal else c * 512
                w = (c + 1) * 512 - qs
                slot = sidx % 2
                sidx += 1
                bank = 4 + slot
                E.pe_wait_bank(bank)
                P.op("tensor", lambda e, bank=bank, par=par, j=j, qs=qs, w=w: e.matmul(
                    B.pb[bank][:, 0:w], kaug[par][0:66, j * 128:(j + 1) * 128],
                    qaug[par][0:66, qs:qs + w], start=True, stop=True), inc=[("pe", 1)])
                dsem = "pe"
                if causal and qs == j * 128:
                    P.wait("vector", "pe")
                    P.op("vector", lambda e, bank=bank: e.tensor_tensor(
                        out=B.pb[bank][:, 0:128], in0=B.pb[bank][:, 0:128],
                        in1=B.maskFm[:, :], op=ALU.add), inc=[("dve", 1)])
                    dsem = "dve"
                P.wait("scalar", dsem)
                if pt_busy[slot] is not None:
                    P.wait("scalar", *pt_busy[slot])
                P.op("scalar", lambda e, bank=bank, slot=slot, w=w: e.activation(
                    out=ptile[slot][:, 0:w], in_=B.pb[bank][:, 0:w], func=AF.Exp,
                    scale=qscale), inc=[("act", 1)])
                E.bank_consumed(bank, "act")
                P.wait("tensor", "act")
                pvb = c
                if j == 0:
                    E.pe_wait_bank(pvb)
                st = (j == 0)
                sp = (j == jlast(c))
                P.op("tensor", lambda e, pvb=pvb, j=j, h=h, slot=slot, qs=qs, c=c, w=w,
                     st=st, sp=sp: e.matmul(
                        B.pb[pvb][0:64, qs - c * 512:qs - c * 512 + w],
                        vtm[:, j, h * 64:(h + 1) * 64], ptile[slot][:, 0:w],
                        start=st, stop=sp), inc=[("pe", 1)])
                pt_busy[slot] = ("pe", P.count("pe"))
        aoc = B.actC.ap().rearrange("p (c n) -> p c n", c=8)
        rb = (h % 2) * 64
        P.wait("scalar", "pe")
        for c in range(4):
            P.op("scalar", lambda e, c=c, h=h, rb=rb: e.copy(
                out=aoc[rb:rb + 64, h // 2, c * 512:(c + 1) * 512],
                in_=B.pb[c][0:64, 0:512]), inc=[("act", 1)])
            E.bank_consumed(c, "act")
        aug_busy[par] = ("pe", P.count("pe"))

    if cfg.get("v_only"):
        E.small_reset()
        return
    qk_loads(0)
    qk_proj(0)
    qk_loads(1)
    for h in range(NH):
        pass_a(h)
        stat_flip(h)
        if h + 2 < NH:
            qk_loads(h + 2)
        if h + 1 < NH:
            qk_proj(h + 1)
        pass_b(h)

    if cfg["has_o_bias"]:
        emit_bias_bc(E, W["bo"], 1024)
    aofm = B.fm(B.actC)
    ho = W["Wo"].ap().rearrange("(k p) d -> p k d", p=128)
    P.wait("tensor", "dve")
    P.wait("gpsimd", "pe")
    for dchunk in range(2):
        dv = None
        for k in range(NC):
            dv = E.load_cast(ho[:, k, dchunk * 512:(dchunk + 1) * 512],
                             wAv_big[:, k, (dchunk % 2) * 512:(dchunk % 2) * 512 + 512], 512)
        P.wait("tensor", *dv)
        wcol = (dchunk % 2) * 512
        for t in range(NT):
            bank = t % 4
            E.pe_wait_bank(bank)
            ps = B.pb[bank]
            for k in range(NC):
                P.op("tensor", lambda e, ps=ps, k=k, t=t, wcol=wcol: e.matmul(
                    ps[:, 0:512], aofm[:, k, t * 128:(t + 1) * 128],
                    wAv_big[:, k, wcol:wcol + 512],
                    start=(k == 0), stop=(k == NC - 1)), inc=[("pe", 1)] if k == NC - 1 else ())
            P.wait("vector", "pe")
            xsl = B.x[:, t, dchunk * 512:(dchunk + 1) * 512]
            if cfg["has_o_bias"]:
                P.op("vector", lambda e, ps=ps, dchunk=dchunk: e.tensor_tensor(
                    out=B.scr[:, 0:512], in0=ps[:, 0:512],
                    in1=B.scr[:, 1024 + dchunk * 512:1024 + (dchunk + 1) * 512], op=ALU.add))
                P.op("vector", lambda e, xsl=xsl: e.tensor_tensor(
                    out=xsl, in0=xsl, in1=B.scr[:, 0:512], op=ALU.add), inc=[("dve", 1)])
            else:
                P.op("vector", lambda e, ps=ps, xsl=xsl: e.tensor_tensor(
                    out=xsl, in0=xsl, in1=ps[:, 0:512], op=ALU.add), inc=[("dve", 1)])
            E.bank_consumed(bank, "dve")
    E.small_reset()


def prep_attn(inp, layer):
    f32 = lambda a: np.asarray(a, np.float32)
    if layer == 0:
        Wq, Wk, Wv, Wo = (f32(inp["L0_attn_W_q"]), f32(inp["L0_attn_W_k"]),
                          f32(inp["L0_attn_W_v"]), f32(inp["L0_attn_W_o"]))
        bq, bk, bv, bo = (f32(inp["L0_attn_b_q"]), f32(inp["L0_attn_b_k"]),
                          f32(inp["L0_attn_b_v"]), f32(inp["L0_attn_b_o"]))
    elif layer == 1:
        w_in = f32(inp["L1_attn_w_in"])
        Wq, Wk, Wv = w_in[0:1024].T, w_in[1024:2048].T, w_in[2048:3072].T
        Wo = f32(inp["L1_attn_w_out"]).T
        b_in = f32(inp["L1_attn_b_in"])
        bq, bk, bv = b_in[0:1024], b_in[1024:2048], b_in[2048:3072]
        bo = f32(inp["L1_attn_b_out"])
    elif layer == 2:
        Wq, Wk, Wv = f32(inp["L2_attn_Wq"]).T, f32(inp["L2_attn_Wk"]).T, f32(inp["L2_attn_Wv"]).T
        Wo = f32(inp["L2_attn_Wo"]).T
        bq, bk, bv, bo = (f32(inp["L2_attn_bq"]), f32(inp["L2_attn_bk"]),
                          f32(inp["L2_attn_bv"]), f32(inp["L2_attn_bo"]))
    else:
        Wq, Wk, Wv = f32(inp["L3_attn_Wq"]).T, f32(inp["L3_attn_Wk"]).T, f32(inp["L3_attn_Wv"]).T
        Wo = f32(inp["L3_attn_Wo"]).T
        bq = bk = bv = bo = None
    pre = f"a{layer}_"
    d = {}
    slices = lambda Wx: np.ascontiguousarray(
        Wx.reshape(8, 128, 16, 64).transpose(2, 1, 0, 3))
    d[pre + "Wqh"] = slices(Wq)
    d[pre + "Wkh"] = slices(Wk)
    d[pre + "Wv"] = np.ascontiguousarray(Wv)
    d[pre + "Wo"] = np.ascontiguousarray(Wo)
    if bq is not None:
        vec = lambda v: v.reshape(8, 128).T
        d[pre + "bqk"] = np.ascontiguousarray(np.concatenate([vec(bq), vec(bk)], 1))
        d[pre + "bv"] = np.ascontiguousarray(bv.reshape(1, 1024))
        d[pre + "bo"] = np.ascontiguousarray(bo.reshape(1, 1024))
    if layer == 3:
        d[pre + "qsc"] = np.ascontiguousarray(f32(inp["L3_attn_q_scale"]).reshape(1, 16))
        d[pre + "ksc"] = np.ascontiguousarray(f32(inp["L3_attn_k_scale"]).reshape(1, 16))
    return d


ATTN_CFGS = {
    0: dict(causal=True, l3=False, qscale=0.125, has_qkv_bias=True, has_v_bias=True, has_o_bias=True),
    1: dict(causal=True, l3=False, qscale=0.125, has_qkv_bias=True, has_v_bias=True, has_o_bias=True),
    2: dict(causal=False, l3=False, qscale=0.125, has_qkv_bias=True, has_v_bias=True, has_o_bias=True),
    3: dict(causal=True, l3=True, qscale=1.0, has_qkv_bias=False, has_v_bias=False, has_o_bias=False),
}


def declare_attn_w(nc, layer):
    pre = f"a{layer}_"
    W = {"Wqh": nc.dram_tensor(pre + "Wqh", [16, 128, 8, 64], F32, kind="ExternalInput"),
         "Wkh": nc.dram_tensor(pre + "Wkh", [16, 128, 8, 64], F32, kind="ExternalInput"),
         "Wv": nc.dram_tensor(pre + "Wv", [D, D], F32, kind="ExternalInput"),
         "Wo": nc.dram_tensor(pre + "Wo", [D, D], F32, kind="ExternalInput")}
    if ATTN_CFGS[layer]["has_qkv_bias"]:
        W["bqk"] = nc.dram_tensor(pre + "bqk", [128, 16], F32, kind="ExternalInput")
        W["bv"] = nc.dram_tensor(pre + "bv", [1, D], F32, kind="ExternalInput")
        W["bo"] = nc.dram_tensor(pre + "bo", [1, D], F32, kind="ExternalInput")
    if layer == 3:
        W["qsc"] = nc.dram_tensor(pre + "qsc", [1, 16], F32, kind="ExternalInput")
        W["ksc"] = nc.dram_tensor(pre + "ksc", [1, 16], F32, kind="ExternalInput")
    return W


def build_test_attn(layer):
    nc = bass.Bass("TRN2", target_bir_lowering=False)
    x_in = nc.dram_tensor("x", [T, D], F32, kind="ExternalInput")
    x_out = nc.dram_tensor("xout", [T, D], F32, kind="ExternalOutput")
    W = declare_attn_w(nc, layer)
    st = nc.dram_tensor("attn_st", [16, 32, 128], BF16, kind="ExternalOutput")
    with ExitStack() as ctx:
        B = Buffers(nc, ctx)
        P = Prog(nc)
        E = Emit(nc, P, B, ctx)
        E.wcache = nc.dram_tensor("wcache", [1024, 128, 512], BF16, kind="Internal")
        E.init_consts()
        E.load_x(x_in)
        E.emit_ln()
        emit_attn(E, ATTN_CFGS[layer], W, st)
        E.store_x(x_out)
        dbg2 = nc.dram_tensor("dbg2", [128, 16384], BF16, kind="ExternalOutput")
        dbg3 = nc.dram_tensor("dbg3", [128, 16384], BF16, kind="ExternalOutput")
        dbg4 = nc.dram_tensor("dbg4", [128, 720], F32, kind="ExternalOutput")
        P.op("gpsimd", lambda e: e.dma_start(out=dbg2.ap(), in_=B.actC[:, :]), inc=[("out", 16)])
        P.op("gpsimd", lambda e: e.dma_start(out=dbg3.ap(), in_=B.actB[:, :]), inc=[("out", 16)])
        P.op("gpsimd", lambda e: e.dma_start(out=dbg4.ap(), in_=B.small[:, :]), inc=[("out", 16)])
        P.wait("gpsimd", "out")
        block = ctx.enter_context(nc.Block())
        P.replay(block)
    return nc



def emit_moe_gating_softmax(E, W, probs_off, has_bias):
    P, B = E.P, E.B
    xn = B.fm(B.actA)
    wBf = B.wB.ap()
    gwv = wBf[:, 3584:3648].rearrange("p (k e) -> p k e", k=8)
    dv = E.load_cast(W["gw"].ap(), wBf[:, 3584:3648], 64)
    gbrow = B.smallb[0:1, 160:168]
    if has_bias:
        if E._stage_busy[0] is not None:
            P.wait("sync", *E._stage_busy[0])
            E._stage_busy[0] = None
        P.wait("sync", "dve")
        P.wait("sync", "act")
        P.op("sync", lambda e: e.dma_start(out=B.stage[0:1, 0:8], in_=W["gb"].ap()),
             inc=[("wq", 16)])
        P.wait("gpsimd", "wq")
        P.op("gpsimd", lambda e: e.tensor_copy(out=gbrow, in_=B.stage[0:1, 0:8]),
             inc=[("pool", 1)])
        E._stage_busy[0] = ("pool", P.count("pool"))
        P.wait("tensor", "pool")
    o = E.small_col(36)
    nm16 = B.small[:, o:o + 16]
    s16 = B.small[:, o + 16:o + 32]
    P.wait("tensor", *dv)
    for t in range(NT):
        bank = 4 + t % 2
        E.pe_wait_bank(bank)
        ps = B.pb[bank]
        for k in range(NC):
            P.op("tensor", lambda e, ps=ps, t=t, k=k: e.matmul(
                ps[:, 0:8], xn[:, k, t * 128:(t + 1) * 128], gwv[:, k, :],
                start=(k == 0), stop=(k == NC - 1) if not has_bias else False),
                inc=[("pe", 1)] if (k == NC - 1 and not has_bias) else ())
        if has_bias:
            P.op("tensor", lambda e, ps=ps, t=t: e.matmul(
                ps[:, 0:8], B.onesb[0:1, 0:128], gbrow,
                start=False, stop=True), inc=[("pe", 1)])
        P.wait("vector", "pe")
        P.op("vector", lambda e, ps=ps, t=t: e.reduce_max(
            nm16[:, t:t + 1], ps[:, 0:8], axis=AX.X, negate=True), inc=[("dve", 1)])
        P.wait("scalar", "dve")
        pr = B.small[:, probs_off + t * 8:probs_off + t * 8 + 8]
        P.op("scalar", lambda e, ps=ps, t=t, pr=pr: e.activation(
            out=pr, in_=ps[:, 0:8], func=AF.Exp, bias=nm16[:, t:t + 1],
            accum_out=s16[:, t:t + 1]), inc=[("act", 1)])
        E.bank_consumed(bank, "act")
    P.wait("vector", "act")
    P.op("vector", lambda e: e.reciprocal(nm16[:, 0:16], s16[:, 0:16]), inc=[("dve", 1)])
    P.wait("gpsimd", "dve")
    P.op("gpsimd", lambda e: e.tensor_copy(out=s16[:, 0:16], in_=nm16[:, 0:16]),
         inc=[("pool", 1)])
    P.wait("vector", "pool")
    for t in range(NT):
        pr = B.small[:, probs_off + t * 8:probs_off + t * 8 + 8]
        P.op("vector", lambda e, pr=pr, t=t: e.tensor_scalar(
            out=pr, in0=pr, scalar1=s16[:, t:t + 1], scalar2=None, op0=ALU.mult),
            inc=[("dve", 1)] if t == NT - 1 else ())
    return o


def emit_top2(E, probs_off, h1_off, p2_off, o_stats):
    P, B = E.P, E.B
    nm16 = B.small[:, o_stats:o_stats + 16]
    s16 = B.small[:, o_stats + 16:o_stats + 32]
    for t in range(NT):
        pr = B.small[:, probs_off + t * 8:probs_off + t * 8 + 8]
        P.op("vector", lambda e, pr=pr, t=t: e.reduce_max(
            nm16[:, t:t + 1], pr, axis=AX.X), inc=[("dve", 1)] if t == NT - 1 else ())
    P.wait("gpsimd", "dve")
    P.op("gpsimd", lambda e: e.tensor_copy(out=s16[:, 0:16], in_=nm16[:, 0:16]),
         inc=[("pool", 1)])
    P.wait("vector", "pool")
    for t in range(NT):
        pr = B.small[:, probs_off + t * 8:probs_off + t * 8 + 8]
        h1 = B.small[:, h1_off + t * 8:h1_off + t * 8 + 8]
        P.op("vector", lambda e, pr=pr, h1=h1, t=t: e.tensor_scalar(
            out=h1, in0=pr, scalar1=s16[:, t:t + 1], scalar2=None, op0=ALU.is_ge))
    for t in range(NT):
        pr = B.small[:, probs_off + t * 8:probs_off + t * 8 + 8]
        h1 = B.small[:, h1_off + t * 8:h1_off + t * 8 + 8]
        p2 = B.small[:, p2_off + t * 8:p2_off + t * 8 + 8]
        P.op("vector", lambda e, pr=pr, h1=h1, p2=p2: e.scalar_tensor_tensor(
            out=p2, in0=h1, scalar=-2.0, in1=pr, op0=ALU.mult, op1=ALU.add))
    for t in range(NT):
        p2 = B.small[:, p2_off + t * 8:p2_off + t * 8 + 8]
        P.op("vector", lambda e, p2=p2, t=t: e.reduce_max(
            s16[:, t:t + 1], p2, axis=AX.X), inc=[("dve", 1)] if t == NT - 1 else ())
    P.wait("gpsimd", "dve")
    P.op("gpsimd", lambda e: e.tensor_copy(out=nm16[:, 0:16], in_=s16[:, 0:16]),
         inc=[("pool", 1)])
    P.wait("vector", "pool")
    for t in range(NT):
        p2 = B.small[:, p2_off + t * 8:p2_off + t * 8 + 8]
        P.op("vector", lambda e, p2=p2, t=t: e.tensor_scalar(
            out=p2, in0=p2, scalar1=nm16[:, t:t + 1], scalar2=None, op0=ALU.is_ge),
            inc=[("dve", 1)] if t == NT - 1 else ())


def emit_l2_ffn(E, W):
    P, B = E.P, E.B
    xn = B.fm(B.actA)
    h1f = B.fm(B.actB)
    h2f = B.fm(B.actC)
    wAv = B.wA.ap().rearrange("p (k d) -> p k d", k=8)
    wBf = B.wB.ap()
    ring = [wBf[:, i * 1024:(i + 1) * 1024].rearrange("p (k j) -> p k j", k=8)
            for i in range(4)]
    probs_off = E.small_col(128)
    h1_off = E.small_col(128)
    h2_off = E.small_col(128)
    cw_off = E.small_col(128)
    o_stats = emit_moe_gating_softmax(E, W, probs_off, has_bias=True)
    emit_top2(E, probs_off, h1_off, h2_off, o_stats)
    for t in range(NT):
        pr = B.small[:, probs_off + t * 8:probs_off + t * 8 + 8]
        h1 = B.small[:, h1_off + t * 8:h1_off + t * 8 + 8]
        h2 = B.small[:, h2_off + t * 8:h2_off + t * 8 + 8]
        cw = B.small[:, cw_off + t * 8:cw_off + t * 8 + 8]
        P.op("vector", lambda e, pr=pr, h1=h1, cw=cw, t=t: e.tensor_scalar(
            out=cw, in0=h1, scalar1=pr[:, 0:1], scalar2=0.5, op0=ALU.mult, op1=ALU.mult))
    for t in range(NT):
        pr = B.small[:, probs_off + t * 8:probs_off + t * 8 + 8]
        h2 = B.small[:, h2_off + t * 8:h2_off + t * 8 + 8]
        cw = B.small[:, cw_off + t * 8:cw_off + t * 8 + 8]
        P.op("vector", lambda e, pr=pr, h2=h2, cw=cw, t=t: e.tensor_scalar(
            out=B.small[:, h1_off + t * 8:h1_off + t * 8 + 8], in0=h2,
            scalar1=pr[:, 1:2], scalar2=0.5, op0=ALU.mult, op1=ALU.mult))
    for t in range(NT):
        cw = B.small[:, cw_off + t * 8:cw_off + t * 8 + 8]
        t1 = B.small[:, h1_off + t * 8:h1_off + t * 8 + 8]
        P.op("vector", lambda e, cw=cw, t1=t1: e.tensor_tensor(
            out=cw, in0=cw, in1=t1, op=ALU.add), inc=[("dve", 1)] if t == NT - 1 else ())
    cw_ready = ("dve", P.count("dve"))
    lvl = W.get("_lvl", 4)
    if lvl <= 1:
        E.small_reset()
        return

    w1h, w2h, w3h = W["w1"].ap(), W["w2"].ap(), W["w3"].ap()
    b12h, b3h = W["b12"].ap(), W["b3"].ap()
    o_b = E.small_col(16)
    b1c = B.small[:, o_b:o_b + 8]
    b2c = B.small[:, o_b + 8:o_b + 16]
    for e_i in range(8):
        db = E.load_vec_ap(b12h[e_i], B.small[:, o_b:o_b + 16])
        for sl in range(2):
            if E._stage_busy[sl] is not None:
                P.wait("sync", *E._stage_busy[sl])
                E._stage_busy[sl] = None
        P.wait("sync", "pe")
        P.op("sync", lambda e, e_i=e_i: e.dma_start(out=B.stage[0:1, 0:512],
                                                    in_=b3h[e_i][:, 0:512]), inc=[("wq", 16)])
        P.op("sync", lambda e, e_i=e_i: e.dma_start(out=B.stage[0:1, 512:1024],
                                                    in_=b3h[e_i][:, 512:1024]), inc=[("wq", 16)])
        P.wait("gpsimd", "wq")
        P.op("gpsimd", lambda e: e.tensor_copy(out=B.rowb[0:1, 0:512], in_=B.stage[0:1, 0:512]))
        P.op("gpsimd", lambda e: e.tensor_copy(out=B.rowb[0:1, 512:1024],
                                               in_=B.stage[0:1, 512:1024]), inc=[("pool", 1)])
        E._stage_busy[0] = ("pool", P.count("pool"))
        E._stage_busy[1] = ("pool", P.count("pool"))
        b3_ready = ("pool", P.count("pool"))
        P.wait("gpsimd", "pe")
        P.wait("gpsimd", "wb")
        P.wait("sync", "pe")
        P.wait("sync", "wb")
        dv3 = None
        for k in range(NC):
            for hf in range(2):
                dv3 = E.load_cast(w3h[e_i].rearrange("(k p) o -> p k o", p=128)[:, k, hf * 512:(hf + 1) * 512],
                                  wAv[:, k, hf * 512:(hf + 1) * 512], 512,
                                  cache_key=("l2", "w3", e_i, k, hf))
        P.wait("scalar", *db)
        for half in range(1):
            tok0 = 0
            stages = ((xn, h1f, b1c, w1h),) if lvl == 2 else ((xn, h1f, b1c, w1h), (h1f, h2f, b2c, w2h))
            for isrc, (src, dstf, bcol, wh) in enumerate(stages):
                for m in range(8):
                    sl = ring[m % 4]
                    war = E._ring_busy[m % 4] or []
                    E._ring_busy[m % 4] = None
                    hbmsl = wh[e_i, m].rearrange("p k j -> p (k j)")
                    slf = sl.rearrange("p k j -> p (k j)")
                    wn = "w1" if isrc == 0 else "w2"
                    dvr = E.load_cast(hbmsl[:, 0:512], slf[:, 0:512], 512,
                                      cache_key=("l2", wn, e_i, m, 0), war=war)
                    dvr = E.load_cast(hbmsl[:, 512:1024], slf[:, 512:1024], 512,
                                      cache_key=("l2", wn, e_i, m, 1), war=war)
                    wbm = ("wb", P.count("wb"))
                    E.wait_casts("tensor")
                    if isrc == 1 and m == 0:
                        P.wait("tensor", "act")
                    for n in range(4):
                        bank = n
                        E.pe_wait_bank(bank)
                        ps = B.pb[bank]
                        for k in range(NC):
                            P.op("tensor", lambda e, ps=ps, src=src, sl=sl, k=k, m=m, n=n,
                                 tok0=tok0: e.matmul(
                                    ps[:, 0:512], sl[:, k, :],
                                    src[:, k, tok0 + n * 512:tok0 + (n + 1) * 512],
                                    start=(k == 0), stop=(k == NC - 1)),
                                inc=[("pe", 1)] if k == NC - 1 else ())
                        E._ring_busy[m % 4] = [("pe", P.count("pe")), wbm]
                        dsl_ = lambda dstf=dstf, m=m, n=n, tok0=tok0: dstf[
                            :, m, tok0 + n * 512:tok0 + (n + 1) * 512]
                        if n % 2 == 0:
                            P.wait("scalar", "pe")
                            P.op("scalar", lambda e, ps=ps, dsl_=dsl_, m=m, bcol=bcol:
                                 e.activation(out=dsl_(), in_=ps[:, 0:512], func=AF.Relu,
                                              bias=bcol[:, m:m + 1], scale=1.0),
                                 inc=[("act", 1)])
                            E.bank_consumed(bank, "act")
                        else:
                            P.wait("vector", "pe")
                            P.op("vector", lambda e, ps=ps, dsl_=dsl_, m=m, bcol=bcol:
                                 e.tensor_scalar(out=dsl_(), in0=ps[:, 0:512],
                                                 scalar1=bcol[:, m:m + 1], scalar2=0.0,
                                                 op0=ALU.add, op1=ALU.max),
                                 inc=[("dve", 1)])
                            E.bank_consumed(bank, "dve")
            if lvl <= 3:
                continue
            E.wait_casts("tensor")
            P.wait("tensor", *b3_ready)
            P.wait("tensor", "act")
            if half == 0 and e_i == 0:
                P.wait("vector", *cw_ready)
            for dchunk in range(2):
                for tt in range(16):
                    t = tt
                    bank = 4 + tt % 3
                    E.pe_wait_bank(bank)
                    ps = B.pb[bank]
                    for k in range(NC):
                        P.op("tensor", lambda e, ps=ps, k=k, t=t, dchunk=dchunk: e.matmul(
                            ps[:, 0:512], h2f[:, k, t * 128:(t + 1) * 128],
                            wAv[:, k, dchunk * 512:(dchunk + 1) * 512],
                            start=(k == 0), stop=False))
                    P.op("tensor", lambda e, ps=ps, t=t, dchunk=dchunk: e.matmul(
                        ps[:, 0:512], B.onesb[0:1, 0:128],
                        B.rowb[0:1, dchunk * 512:(dchunk + 1) * 512],
                        start=False, stop=True), inc=[("pe", 1)])
                    P.wait("vector", "pe")
                    cwc = B.small[:, cw_off + t * 8 + e_i:cw_off + t * 8 + e_i + 1]
                    xsl = B.x[:, t, dchunk * 512:(dchunk + 1) * 512]
                    P.op("vector", lambda e, ps=ps, cwc=cwc, xsl=xsl: e.scalar_tensor_tensor(
                        out=xsl, in0=ps[:, 0:512], scalar=cwc, in1=xsl,
                        op0=ALU.mult, op1=ALU.add), inc=[("dve", 1)])
                    E.bank_consumed(bank, "dve")
    E.small_reset()


def prep_l2_ffn(inp):
    f32 = lambda a: np.asarray(a, np.float32)
    w1, w2, w3 = f32(inp["L2_ffn_w1"]), f32(inp["L2_ffn_w2"]), f32(inp["L2_ffn_w3"])
    b1, b2, b3 = f32(inp["L2_ffn_b1"]), f32(inp["L2_ffn_b2"]), f32(inp["L2_ffn_b3"])
    gw, gb = f32(inp["L2_ffn_gate_w"]), f32(inp["L2_ffn_gate_b"])
    mslice = lambda w: np.ascontiguousarray(
        w.reshape(8, 8, 128, 8, 128).transpose(0, 3, 2, 1, 4))
    cols = lambda b: b.reshape(8, 8, 128).transpose(0, 2, 1)
    return {
        "l2f_w1": mslice(w1),
        "l2f_w2": mslice(w2),
        "l2f_w3": np.ascontiguousarray(w3),
        "l2f_b12": np.ascontiguousarray(np.concatenate([cols(b1), cols(b2)], 2)),
        "l2f_b3": np.ascontiguousarray(b3.reshape(8, 1, 1024)),
        "l2f_gw": np.ascontiguousarray(
            gw.T.reshape(8, 128, 8).transpose(1, 0, 2).reshape(128, 64)),
        "l2f_gb": np.ascontiguousarray(gb.reshape(1, 8)),
    }


def declare_l2f_w(nc):
    return {
        "w1": nc.dram_tensor("l2f_w1", [8, 8, 128, 8, 128], F32, kind="ExternalInput"),
        "w2": nc.dram_tensor("l2f_w2", [8, 8, 128, 8, 128], F32, kind="ExternalInput"),
        "w3": nc.dram_tensor("l2f_w3", [8, 1024, 1024], F32, kind="ExternalInput"),
        "b12": nc.dram_tensor("l2f_b12", [8, 128, 16], F32, kind="ExternalInput"),
        "b3": nc.dram_tensor("l2f_b3", [8, 1, 1024], F32, kind="ExternalInput"),
        "gw": nc.dram_tensor("l2f_gw", [128, 64], F32, kind="ExternalInput"),
        "gb": nc.dram_tensor("l2f_gb", [1, 8], F32, kind="ExternalInput"),
    }


def build_test_l2f(_):
    nc = bass.Bass("TRN2", target_bir_lowering=False)
    x_in = nc.dram_tensor("x", [T, D], F32, kind="ExternalInput")
    x_out = nc.dram_tensor("xout", [T, D], F32, kind="ExternalOutput")
    W = declare_l2f_w(nc)
    with ExitStack() as ctx:
        B = Buffers(nc, ctx)
        P = Prog(nc)
        E = Emit(nc, P, B, ctx)
        E.wcache = nc.dram_tensor("wcache", [1024, 128, 512], BF16, kind="Internal")
        E.init_consts()
        E.load_x(x_in)
        E.emit_ln()
        emit_l2_ffn(E, W)
        E.store_x(x_out)
        block = ctx.enter_context(nc.Block())
        P.replay(block)
    return nc


def emit_l0_ffn(E, W, n_cores=8, cc_idx=0):
    P, B = E.P, E.B
    xn = B.fm(B.actA)
    hmf = B.fm(B.actB)
    wAv = B.wA.ap().rearrange("p (k d) -> p k d", k=8)
    wBf = B.wB.ap()
    ring = [wBf[:, i * 1024:(i + 1) * 1024].rearrange("p (k j) -> p k j", k=8)
            for i in range(4)]
    probs_off = E.small_col(128)
    h1_off = E.small_col(128)
    h2_off = E.small_col(128)
    cw_off = E.small_col(128)
    o_stats = emit_moe_gating_softmax(E, W, probs_off, has_bias=False)
    emit_top2(E, probs_off, h1_off, h2_off, o_stats)
    m2c = B.small[:, o_stats:o_stats + 16]
    o2 = E.small_col(64)
    m1c = B.small[:, o2:o2 + 16]
    sden = B.small[:, o2 + 16:o2 + 32]
    w1c = B.small[:, o2 + 32:o2 + 48]
    w2c = B.small[:, o2 + 48:o2 + 64]
    o3 = E.small_col(64)
    cnt = B.small[:, o3:o3 + 1]
    gcc = B.small[:, o3 + 1:o3 + 2]
    gcr = B.small[0:1, o3 + 2:o3 + 10]
    maskc = B.small[:, o3 + 10:o3 + 11]
    omm = B.small[:, o3 + 11:o3 + 12]
    bestc = B.small[:, o3 + 12:o3 + 13]
    bestb = B.small[:, o3 + 13:o3 + 14]
    c1c = B.small[:, o3 + 14:o3 + 15]
    condc = B.small[:, o3 + 15:o3 + 16]
    omc = B.small[:, o3 + 16:o3 + 17]
    simc = B.small[:, o3 + 17:o3 + 33]
    msim = B.small[:, o3 + 33:o3 + 49]
    for t in range(NT):
        pr = B.small[:, probs_off + t * 8:probs_off + t * 8 + 8]
        P.op("vector", lambda e, pr=pr, t=t: e.reduce_max(m1c[:, t:t + 1], pr, axis=AX.X),
             inc=[("dve", 1)] if t == NT - 1 else ())
    P.op("vector", lambda e: e.tensor_copy(out=B.smallb[:, 0:128],
                                           in_=B.small[:, h1_off:h1_off + 128]))
    P.op("vector", lambda e: e.tensor_copy(out=B.smallb[:, 128:256],
                                           in_=B.small[:, h2_off:h2_off + 128]),
         inc=[("dve", 1)])
    hb_ready = ("dve", P.count("dve"))
    P.wait("tensor", *hb_ready)
    E.pe_wait_bank(6)
    for t in range(NT):
        P.op("tensor", lambda e, t=t: e.matmul(
            B.pb[6][0:8, 0:1], B.smallb[:, t * 8:(t + 1) * 8], B.onesb[:, 0:1],
            start=(t == 0), stop=(t == NT - 1)), inc=[("pe", 1)] if t == NT - 1 else ())
    P.wait("vector", "pe")
    P.op("vector", lambda e: e.tensor_copy(out=cnt[0:8, :], in_=B.pb[6][0:8, 0:1]),
         inc=[("dve", 1)])
    E.bank_consumed(6, "dve")
    ccin = E.cc_in
    ccout = E.cc_out
    P.wait("gpsimd", "dve")
    P.op("gpsimd", lambda e: e.dma_start(out=ccin.ap()[cc_idx], in_=cnt[0:8, :]),
         inc=[("out", 16)])
    P.wait("gpsimd", "out")
    ccsem = "cc0" if cc_idx == 0 else "cc1"
    P.op("gpsimd", lambda e: e.collective_compute(
        "AllReduce", ALU.add, replica_groups=[list(range(n_cores))],
        ins=[ccin.ap()[cc_idx].opt()], outs=[ccout.ap()[cc_idx].opt()]),
        inc=[(ccsem, 1)])
    P.wait("gpsimd", ccsem)
    P.op("gpsimd", lambda e: e.dma_start(out=gcc[0:8, :], in_=ccout.ap()[cc_idx]),
         inc=[("out", 16)])
    P.op("gpsimd", lambda e: e.dma_start(out=gcr, in_=ccout.ap()[cc_idx].rearrange("e one -> (one) (e)")),
         inc=[("out", 16)])
    dsim = E.load_vec_ap(W["sim"].ap(), simc[0:8, :])
    P.wait("vector", "out")
    P.wait("vector", *dsim)
    P.op("vector", lambda e: e.tensor_scalar(out=maskc[0:8, :], in0=gcc[0:8, :],
                                             scalar1=0.5, scalar2=None, op0=ALU.is_gt))
    P.op("vector", lambda e: e.tensor_scalar(out=B.scr[0:1, 0:8], in0=gcr,
                                             scalar1=0.5, scalar2=None, op0=ALU.is_gt),
         inc=[("dve", 1)])
    P.wait("gpsimd", "dve")
    P.op("gpsimd", lambda e: e.tensor_copy(out=B.rowb[0:1, 16:24], in_=B.scr[0:1, 0:8]),
         inc=[("pool", 1)])
    P.wait("tensor", "pool")
    E.pe_wait_bank(6)
    P.op("tensor", lambda e: e.matmul(B.pb[6][0:8, 0:8], B.onesb[0:1, 0:8],
                                      B.rowb[0:1, 16:24], start=True, stop=True),
         inc=[("pe", 1)])
    P.wait("vector", "pe")
    P.op("vector", lambda e: e.scalar_tensor_tensor(
        out=msim[0:8, 0:8], in0=B.pb[6][0:8, 0:8], scalar=1e10, in1=simc[0:8, 0:8],
        op0=ALU.mult, op1=ALU.add), inc=[("dve", 1)])
    E.bank_consumed(6, "dve")
    P.wait("gpsimd", "dve")
    P.op("gpsimd", lambda e: e.tensor_scalar(out=msim[0:8, 8:16], in0=msim[0:8, 0:8],
                                             scalar1=-1e10, scalar2=None, op0=ALU.add),
         inc=[("pool", 1)])
    P.wait("vector", "pool")
    P.op("vector", lambda e: e.reduce_max(bestc[0:8, :], msim[0:8, 8:16], axis=AX.X),
         inc=[("dve", 1)])
    P.wait("gpsimd", "dve")
    P.op("gpsimd", lambda e: e.tensor_copy(out=bestb[0:8, :], in_=bestc[0:8, :]),
         inc=[("pool", 1)])
    P.wait("vector", "pool")
    hj = B.scr[0:8, 8:16]
    hjb = B.scr[0:8, 24:32]
    c1b = B.small[:, o3 + 49:o3 + 50]
    ommb = B.small[:, o3 + 50:o3 + 51]
    condb = B.small[:, o3 + 51:o3 + 52]
    omcb = B.small[:, o3 + 52:o3 + 53]
    P.op("vector", lambda e: e.tensor_scalar(out=hj, in0=msim[0:8, 8:16],
                                             scalar1=bestb[0:8, :], scalar2=None,
                                             op0=ALU.is_ge))
    P.op("vector", lambda e: e.tensor_scalar(out=c1c[0:8, :], in0=bestb[0:8, :],
                                             scalar1=0.7, scalar2=None, op0=ALU.is_ge))
    P.op("vector", lambda e: e.tensor_scalar(out=omm[0:8, :], in0=maskc[0:8, :],
                                             scalar1=-1.0, scalar2=1.0, op0=ALU.mult,
                                             op1=ALU.add), inc=[("dve", 1)])
    P.wait("gpsimd", "dve")
    P.op("gpsimd", lambda e: e.tensor_copy(out=hjb, in_=hj))
    P.op("gpsimd", lambda e: e.tensor_copy(out=c1b[0:8, :], in_=c1c[0:8, :]))
    P.op("gpsimd", lambda e: e.tensor_copy(out=ommb[0:8, :], in_=omm[0:8, :]),
         inc=[("pool", 1)])
    P.wait("vector", "pool")
    P.op("vector", lambda e: e.tensor_tensor(out=condc[0:8, :], in0=ommb[0:8, :],
                                             in1=c1b[0:8, :], op=ALU.mult),
         inc=[("dve", 1)])
    P.wait("gpsimd", "dve")
    P.op("gpsimd", lambda e: e.tensor_copy(out=condb[0:8, :], in_=condc[0:8, :]),
         inc=[("pool", 1)])
    P.wait("vector", "pool")
    P.op("vector", lambda e: e.tensor_scalar(out=B.scr[0:8, 16:24], in0=hjb,
                                             scalar1=condb[0:8, :], scalar2=None,
                                             op0=ALU.mult))
    P.op("vector", lambda e: e.tensor_scalar(out=omcb[0:8, :], in0=condb[0:8, :],
                                             scalar1=-1.0, scalar2=1.0, op0=ALU.mult,
                                             op1=ALU.add), inc=[("dve", 1)])
    P.wait("gpsimd", "dve")
    P.op("gpsimd", lambda e: e.tensor_scalar(out=B.scr[0:8, 32:40],
                                             in0=B.ident[0:8, 0:8],
                                             scalar1=omcb[0:8, :], scalar2=None,
                                             op0=ALU.mult))
    P.op("gpsimd", lambda e: e.tensor_tensor(out=B.scr[0:8, 40:48], in0=B.scr[0:8, 16:24],
                                             in1=B.scr[0:8, 32:40], op=ALU.add),
         inc=[("pool", 1)])
    P.wait("vector", "pool")
    P.op("vector", lambda e: e.tensor_copy(out=B.rowb[0:8, 0:8], in_=B.scr[0:8, 40:48]),
         inc=[("dve", 1)])
    P.op("vector", lambda e: e.tensor_tensor(out=sden, in0=m1c, in1=m2c, op=ALU.add),
         inc=[("dve", 1)])
    P.wait("gpsimd", "dve")
    P.op("gpsimd", lambda e: e.tensor_copy(out=B.scr[:, 32:48], in_=sden), inc=[("pool", 1)])
    P.wait("vector", "pool")
    P.op("vector", lambda e: e.reciprocal(B.scr[:, 48:64], B.scr[:, 32:48]),
         inc=[("dve", 1)])
    P.wait("gpsimd", "dve")
    P.op("gpsimd", lambda e: e.tensor_copy(out=sden, in_=B.scr[:, 48:64]),
         inc=[("pool", 1)])
    P.wait("vector", "pool")
    P.op("vector", lambda e: e.scalar_tensor_tensor(out=w1c, in0=m1c, scalar=0.5,
                                                    in1=sden, op0=ALU.mult, op1=ALU.mult))
    P.op("vector", lambda e: e.scalar_tensor_tensor(out=w2c, in0=m2c, scalar=0.5,
                                                    in1=sden, op0=ALU.mult, op1=ALU.mult),
         inc=[("dve", 1)])
    P.wait("tensor", "pool")
    P.wait("tensor", "dve")
    for t in range(NT):
        if E.pt_ready is not None:
            P.wait("tensor", *E.pt_ready)
            E.pt_ready = None
        P.op("tensor", lambda e, t=t: e.transpose(
            B.pt[0:8, 0:128], B.smallb[:, t * 8:(t + 1) * 8], B.ident[:, :]))
        P.op("tensor", lambda e, t=t: e.transpose(
            B.pt[0:8, 128:256], B.smallb[:, 128 + t * 8:128 + (t + 1) * 8], B.ident[:, :]),
            inc=[("pe", 1)])
        P.wait("vector", "pe")
        P.op("vector", lambda e: e.tensor_copy(out=B.rowb[0:8, 128:256], in_=B.pt[0:8, 0:128]))
        P.op("vector", lambda e: e.tensor_copy(out=B.rowb[0:8, 256:384], in_=B.pt[0:8, 128:256]),
             inc=[("dve", 1)])
        E.pt_ready = ("dve", P.count("dve"))
        P.wait("tensor", "dve")
        bank = 4 + t % 2
        E.pe_wait_bank(bank)
        E.pe_wait_bank(6)
        P.op("tensor", lambda e, bank=bank: e.matmul(
            B.pb[bank][:, 0:8], B.rowb[0:8, 128:256], B.rowb[0:8, 0:8],
            start=True, stop=True))
        P.op("tensor", lambda e: e.matmul(
            B.pb[6][:, 0:8], B.rowb[0:8, 256:384], B.rowb[0:8, 0:8],
            start=True, stop=True), inc=[("pe", 1)])
        P.wait("scalar", "pe")
        cw = B.small[:, cw_off + t * 8:cw_off + t * 8 + 8]
        P.op("scalar", lambda e, bank=bank: e.copy(
            out=B.scr[:, 64:72], in_=B.pb[bank][:, 0:8]), inc=[("act", 1)])
        P.wait("gpsimd", "act")
        P.op("gpsimd", lambda e, t=t: e.tensor_scalar(
            out=B.scr[:, 72:80], in0=B.scr[:, 64:72], scalar1=w1c[:, t:t + 1],
            scalar2=None, op0=ALU.mult), inc=[("pool", 1)])
        P.wait("vector", "pool")
        P.op("vector", lambda e, t=t, cw=cw: e.scalar_tensor_tensor(
            out=cw, in0=B.pb[6][:, 0:8], scalar=w2c[:, t:t + 1],
            in1=B.scr[:, 72:80], op0=ALU.mult, op1=ALU.add), inc=[("dve", 1)])
        E.bank_consumed(bank, "act")
        E.bank_consumed(6, "dve")
    cw_ready = ("dve", P.count("dve"))

    guh, dnh = W["gu"].ap(), W["dn"].ap()
    for e_i in range(8):
        P.wait("gpsimd", "pe")
        P.wait("gpsimd", "wb")
        P.wait("sync", "pe")
        P.wait("sync", "wb")
        dv3 = None
        for k in range(NC):
            for hf in range(2):
                dv3 = E.load_cast(
                    dnh[e_i].rearrange("(k p) o -> p k o", p=128)[:, k, hf * 512:(hf + 1) * 512],
                    wAv[:, k, hf * 512:(hf + 1) * 512], 512,
                    cache_key=("l0", "dn", e_i, k, hf))
        for half in range(1):
            tok0 = 0
            for m in range(8):
                s0, s1 = ring[(2 * m) % 4], ring[(2 * m + 1) % 4]
                war0 = E._ring_busy[(2 * m) % 4] or []
                war1 = E._ring_busy[(2 * m + 1) % 4] or []
                E._ring_busy[(2 * m) % 4] = None
                E._ring_busy[(2 * m + 1) % 4] = None
                mg = guh[e_i, m].rearrange("p k j -> p (k j)")
                mu = guh[e_i, m + 8].rearrange("p k j -> p (k j)")
                s0f = s0.rearrange("p k j -> p (k j)")
                s1f = s1.rearrange("p k j -> p (k j)")
                E.load_cast(mg[:, 0:512], s0f[:, 0:512], 512,
                            cache_key=("l0", "gu", e_i, m, 0), war=war0)
                E.load_cast(mg[:, 512:1024], s0f[:, 512:1024], 512,
                            cache_key=("l0", "gu", e_i, m, 1), war=war0)
                E.load_cast(mu[:, 0:512], s1f[:, 0:512], 512,
                            cache_key=("l0", "gu", e_i, m + 8, 0), war=war1)
                dvr = E.load_cast(mu[:, 512:1024], s1f[:, 512:1024], 512,
                                  cache_key=("l0", "gu", e_i, m + 8, 1), war=war1)
                wbm = ("wb", P.count("wb"))
                E.wait_casts("tensor")
                for n in range(4):
                    bg = (n % 2) * 2
                    bu = bg + 1
                    E.pe_wait_bank(bg)
                    E.pe_wait_bank(bu)
                    for k in range(NC):
                        P.op("tensor", lambda e, k=k, s0=s0, n=n, tok0=tok0, bg=bg: e.matmul(
                            B.pb[bg][:, 0:512], s0[:, k, :],
                            xn[:, k, tok0 + n * 512:tok0 + (n + 1) * 512],
                            start=(k == 0), stop=(k == NC - 1)),
                            inc=[("pe", 1)] if k == NC - 1 else ())
                    for k in range(NC):
                        P.op("tensor", lambda e, k=k, s1=s1, n=n, tok0=tok0, bu=bu: e.matmul(
                            B.pb[bu][:, 0:512], s1[:, k, :],
                            xn[:, k, tok0 + n * 512:tok0 + (n + 1) * 512],
                            start=(k == 0), stop=(k == NC - 1)),
                            inc=[("pe", 1)] if k == NC - 1 else ())
                    scrsl = B.scr[:, (n % 2) * 512:(n % 2) * 512 + 512]
                    P.wait("scalar", "pe")
                    P.wait("scalar", "dve")
                    P.op("scalar", lambda e, bg=bg, scrsl=scrsl: e.activation(
                        out=scrsl, in_=B.pb[bg][:, 0:512], func=AF.Silu),
                        inc=[("act", 1)])
                    E.bank_consumed(bg, "act")
                    P.wait("vector", "act")
                    P.op("vector", lambda e, m=m, n=n, tok0=tok0, bu=bu, scrsl=scrsl:
                         e.tensor_tensor(
                            out=hmf[:, m, tok0 + n * 512:tok0 + (n + 1) * 512],
                            in0=scrsl, in1=B.pb[bu][:, 0:512], op=ALU.mult),
                        inc=[("dve", 1)])
                    E.bank_consumed(bu, "dve")
                E._ring_busy[2 * m % 4] = [("pe", P.count("pe")), wbm]
                E._ring_busy[(2 * m + 1) % 4] = [("pe", P.count("pe")), wbm]
            E.wait_casts("tensor")
            P.wait("tensor", "dve")
            if half == 0 and e_i == 0:
                P.wait("vector", *cw_ready)
            for dchunk in range(2):
                for tt in range(16):
                    t = tt
                    bank = 4 + tt % 3
                    E.pe_wait_bank(bank)
                    ps = B.pb[bank]
                    for k in range(NC):
                        P.op("tensor", lambda e, ps=ps, k=k, t=t, dchunk=dchunk: e.matmul(
                            ps[:, 0:512], hmf[:, k, t * 128:(t + 1) * 128],
                            wAv[:, k, dchunk * 512:(dchunk + 1) * 512],
                            start=(k == 0), stop=(k == NC - 1)),
                            inc=[("pe", 1)] if k == NC - 1 else ())
                    P.wait("vector", "pe")
                    cwc = B.small[:, cw_off + t * 8 + e_i:cw_off + t * 8 + e_i + 1]
                    xsl = B.x[:, t, dchunk * 512:(dchunk + 1) * 512]
                    P.op("vector", lambda e, ps=ps, cwc=cwc, xsl=xsl: e.scalar_tensor_tensor(
                        out=xsl, in0=ps[:, 0:512], scalar=cwc, in1=xsl,
                        op0=ALU.mult, op1=ALU.add), inc=[("dve", 1)])
                    E.bank_consumed(bank, "dve")
    E.small_reset()


def prep_l0_ffn(inp):
    f32 = lambda a: np.asarray(a, np.float32)
    gu = f32(inp["L0_ffn_gate_up_proj"])
    dn = f32(inp["L0_ffn_down_proj"])
    gw = f32(inp["L0_ffn_gate_weight"])
    sim = f32(inp["L0_ffn_similarity_matrix"])
    guT = gu.transpose(0, 2, 1)
    gus = np.ascontiguousarray(
        guT.reshape(8, 8, 128, 16, 128).transpose(0, 3, 2, 1, 4))
    dnT = np.ascontiguousarray(dn.transpose(0, 2, 1))
    simpad = np.zeros((8, 16), np.float32)
    simpad[:, 0:8] = sim
    return {
        "l0f_gu": gus,
        "l0f_dn": dnT,
        "l0f_gw": np.ascontiguousarray(gw.T.reshape(8, 128, 8).transpose(1, 0, 2).reshape(128, 64)),
        "l0f_sim": np.ascontiguousarray(simpad),
    }


def declare_l0f_w(nc):
    return {
        "gu": nc.dram_tensor("l0f_gu", [8, 16, 128, 8, 128], F32, kind="ExternalInput"),
        "dn": nc.dram_tensor("l0f_dn", [8, 1024, 1024], F32, kind="ExternalInput"),
        "gw": nc.dram_tensor("l0f_gw", [128, 64], F32, kind="ExternalInput"),
        "sim": nc.dram_tensor("l0f_sim", [8, 16], F32, kind="ExternalInput"),
    }


def build_test_l0f(n_cores):
    nc = bass.Bass("TRN2", target_bir_lowering=False)
    x_in = nc.dram_tensor("x", [T, D], F32, kind="ExternalInput")
    x_out = nc.dram_tensor("xout", [T, D], F32, kind="ExternalOutput")
    W = declare_l0f_w(nc)
    with ExitStack() as ctx:
        B = Buffers(nc, ctx)
        P = Prog(nc)
        E = Emit(nc, P, B, ctx)
        E.wcache = nc.dram_tensor("wcache", [1024, 128, 512], BF16, kind="Internal")
        E.cc_in = nc.dram_tensor("cc_in", [2, 8, 1], F32, kind="Internal")
        E.cc_out = nc.dram_tensor("cc_out", [2, 8, 1], F32, kind="Internal")
        E.init_consts()
        E.load_x(x_in)
        E.emit_ln()
        emit_l0_ffn(E, W, n_cores=n_cores, cc_idx=0)
        E.store_x(x_out)
        dbg4 = nc.dram_tensor("dbg4", [128, 720], F32, kind="ExternalOutput")
        dbg5 = nc.dram_tensor("dbg5", [128, 256], BF16, kind="ExternalOutput")
        P.op("gpsimd", lambda e: e.dma_start(out=dbg4.ap(), in_=B.small[:, :]), inc=[("out", 16)])
        P.op("gpsimd", lambda e: e.dma_start(out=dbg5.ap(), in_=B.smallb[:, :]), inc=[("out", 16)])
        dbg6 = nc.dram_tensor("dbg6", [128, 2048], BF16, kind="ExternalOutput")
        dbg7 = nc.dram_tensor("dbg7", [128, 2048], F32, kind="ExternalOutput")
        P.op("gpsimd", lambda e: e.dma_start(out=dbg6.ap(), in_=B.rowb[:, :]), inc=[("out", 16)])
        P.op("gpsimd", lambda e: e.dma_start(out=dbg7.ap(), in_=B.scr[:, :]), inc=[("out", 16)])
        P.wait("gpsimd", "out")
        block = ctx.enter_context(nc.Block())
        P.replay(block)
    return nc



ALL_PREPS = None


def declare_l1f_w(nc):
    return {n: nc.dram_tensor(f"l1f_{n}", s, F32, kind="ExternalInput")
            for n, s in [("WrT", [D, D]), ("WkT", [D, D]), ("WvT", [D, D]),
                         ("mu2", [128, 16])]}


def declare_l3f_w(nc):
    return {"A": nc.dram_tensor("l3f_A", [D, D], F32, kind="ExternalInput"),
            "Bi": nc.dram_tensor("l3f_Bi", [D, D], F32, kind="ExternalInput"),
            "bias": nc.dram_tensor("l3f_bias", [1, D], F32, kind="ExternalInput")}


def prep_all(inputs):
    prep = {}
    prep.update(prep_l0_ffn(inputs))
    prep.update(prep_l1_ffn(inputs))
    prep.update(prep_l2_ffn(inputs))
    prep.update(prep_l3_ffn(inputs))
    for layer in range(4):
        prep.update(prep_attn(inputs, layer))
    return prep


def build_layer(layer, n_cores=8):
    nc = bass.Bass("TRN2", target_bir_lowering=False)
    x_in = nc.dram_tensor("x", [T, D], F32, kind="ExternalInput")
    x_out = nc.dram_tensor("xout", [T, D], F32, kind="ExternalOutput")
    names = []
    with ExitStack() as ctx:
        B = Buffers(nc, ctx)
        P = Prog(nc)
        E = Emit(nc, P, B, ctx)
        E.wcache = nc.dram_tensor("wcache", [1024, 128, 512], BF16, kind="Internal")
        st = nc.dram_tensor("attn_st", [16, 32, 128], BF16, kind="Internal")
        extra = [-float(np.pi)] if layer == 3 else []
        E.init_consts(extra_consts=extra)
        E.load_x(x_in)
        if layer == 0:
            E.cc_in = nc.dram_tensor("cc_in", [2, 8, 1], F32, kind="Internal")
            E.cc_out = nc.dram_tensor("cc_out", [2, 8, 1], F32, kind="Internal")
            Wf = declare_l0f_w(nc)
            Wa = declare_attn_w(nc, 0)
            names += ["l0f_gu", "l0f_dn", "l0f_gw", "l0f_sim",
                      "a0_Wqh", "a0_Wkh", "a0_Wv", "a0_Wo", "a0_bqk", "a0_bv", "a0_bo"]
            E.emit_ln(); emit_l0_ffn(E, Wf, n_cores=n_cores, cc_idx=0)
            E.emit_ln(); emit_attn(E, ATTN_CFGS[0], Wa, st)
            E.emit_ln(); emit_l0_ffn(E, Wf, n_cores=n_cores, cc_idx=1)
        elif layer == 1:
            Wf = declare_l1f_w(nc)
            Wa = declare_attn_w(nc, 1)
            names += ["l1f_WrT", "l1f_WkT", "l1f_WvT", "l1f_mu2",
                      "a1_Wqh", "a1_Wkh", "a1_Wv", "a1_Wo", "a1_bqk", "a1_bv", "a1_bo"]
            E.emit_ln(); E.emit_l1_ffn(Wf)
            E.emit_ln(); emit_attn(E, ATTN_CFGS[1], Wa, st)
            E.emit_ln(); E.emit_l1_ffn(Wf)
        elif layer == 2:
            Wf = declare_l2f_w(nc)
            Wa = declare_attn_w(nc, 2)
            names += ["l2f_w1", "l2f_w2", "l2f_w3", "l2f_b12", "l2f_b3", "l2f_gw", "l2f_gb",
                      "a2_Wqh", "a2_Wkh", "a2_Wv", "a2_Wo", "a2_bqk", "a2_bv", "a2_bo"]
            E.emit_ln(); emit_l2_ffn(E, Wf)
            E.emit_ln(); emit_attn(E, ATTN_CFGS[2], Wa, st)
            E.emit_ln(); emit_l2_ffn(E, Wf)
        else:
            Wf = declare_l3f_w(nc)
            Wa = declare_attn_w(nc, 3)
            names += ["l3f_A", "l3f_Bi", "l3f_bias",
                      "a3_Wqh", "a3_Wkh", "a3_Wv", "a3_Wo", "a3_qsc", "a3_ksc"]
            M_dram = nc.dram_tensor("l3f_M", [8, 128, 1024], BF16, kind="Internal")
            emit_l3_build_M(E, Wf, M_dram)
            E.emit_ln(); emit_l3_apply(E, Wf, M_dram, True)
            E.emit_ln(); emit_attn(E, ATTN_CFGS[3], Wa, st)
            E.emit_ln(); emit_l3_apply(E, Wf, M_dram, False)
        E.store_x(x_out)
        block = ctx.enter_context(nc.Block())
        P.replay(block)
    return nc, names




def build_full(n_cores=8):
    nc = bass.Bass("TRN2", target_bir_lowering=False)
    x_in = nc.dram_tensor("x", [T, D], F32, kind="ExternalInput")
    x_out = nc.dram_tensor("xout", [T, D], F32, kind="ExternalOutput")
    names = []
    with ExitStack() as ctx:
        B = Buffers(nc, ctx)
        P = Prog(nc)
        E = Emit(nc, P, B, ctx)
        E.wcache = nc.dram_tensor("wcache", [1024, 128, 512], BF16, kind="Internal")
        st = nc.dram_tensor("attn_st", [16, 32, 128], BF16, kind="Internal")
        E.cc_in = nc.dram_tensor("cc_in", [2, 8, 1], F32, kind="Internal")
        E.cc_out = nc.dram_tensor("cc_out", [2, 8, 1], F32, kind="Internal")
        M_dram = nc.dram_tensor("l3f_M", [8, 128, 1024], BF16, kind="Internal")
        E.init_consts(extra_consts=[-float(np.pi)])
        E.load_x(x_in)
        W0f = declare_l0f_w(nc)
        W0a = declare_attn_w(nc, 0)
        W1f = declare_l1f_w(nc)
        W1a = declare_attn_w(nc, 1)
        W2f = declare_l2f_w(nc)
        W2a = declare_attn_w(nc, 2)
        W3f = declare_l3f_w(nc)
        W3a = declare_attn_w(nc, 3)
        names += ["l0f_gu", "l0f_dn", "l0f_gw", "l0f_sim",
                  "a0_Wqh", "a0_Wkh", "a0_Wv", "a0_Wo", "a0_bqk", "a0_bv", "a0_bo",
                  "l1f_WrT", "l1f_WkT", "l1f_WvT", "l1f_mu2",
                  "a1_Wqh", "a1_Wkh", "a1_Wv", "a1_Wo", "a1_bqk", "a1_bv", "a1_bo",
                  "l2f_w1", "l2f_w2", "l2f_w3", "l2f_b12", "l2f_b3", "l2f_gw", "l2f_gb",
                  "a2_Wqh", "a2_Wkh", "a2_Wv", "a2_Wo", "a2_bqk", "a2_bv", "a2_bo",
                  "l3f_A", "l3f_Bi", "l3f_bias",
                  "a3_Wqh", "a3_Wkh", "a3_Wv", "a3_Wo", "a3_qsc", "a3_ksc"]
        E.emit_ln(); emit_l0_ffn(E, W0f, n_cores=n_cores, cc_idx=0)
        E.emit_ln(); emit_attn(E, ATTN_CFGS[0], W0a, st)
        E.emit_ln(); emit_l0_ffn(E, W0f, n_cores=n_cores, cc_idx=1)
        E.emit_ln(); E.emit_l1_ffn(W1f)
        E.emit_ln(); emit_attn(E, ATTN_CFGS[1], W1a, st)
        E.emit_ln(); E.emit_l1_ffn(W1f)
        E.emit_ln(); emit_l2_ffn(E, W2f)
        E.emit_ln(); emit_attn(E, ATTN_CFGS[2], W2a, st)
        E.emit_ln(); emit_l2_ffn(E, W2f)
        emit_l3_build_M(E, W3f, M_dram)
        E.emit_ln(); emit_l3_apply(E, W3f, M_dram, True)
        E.emit_ln(); emit_attn(E, ATTN_CFGS[3], W3a, st)
        E.emit_ln(); emit_l3_apply(E, W3f, M_dram, False)
        E.store_x(x_out)
        block = ctx.enter_context(nc.Block())
        P.replay(block)
    return nc, names


FUSED = True


def kernel(**inputs):
    prep = prep_all(inputs)
    x = np.ascontiguousarray(np.asarray(inputs["L0_attn_x"], np.float32))
    xs = [np.ascontiguousarray(x[i]) for i in range(NCORES)]
    if FUSED:
        nc, names = build_full(n_cores=NCORES)
        in_maps = [dict({"x": xs[i]}, **{k: prep[k] for k in names})
                   for i in range(NCORES)]
        res = run_bass_kernel_spmd(nc, in_maps, core_ids=list(range(NCORES)))
        xs = [np.asarray(res.results[i]["xout"], np.float32) for i in range(NCORES)]
        return np.stack(xs)
    for layer in range(4):
        nc, names = build_layer(layer, n_cores=NCORES)
        in_maps = [dict({"x": xs[i]}, **{k: prep[k] for k in names})
                   for i in range(NCORES)]
        res = run_bass_kernel_spmd(nc, in_maps, core_ids=list(range(NCORES)))
        xs = [np.asarray(res.results[i]["xout"], np.float32) for i in range(NCORES)]
    return np.stack(xs)



ALL_PREPS = None


def declare_l1f_w(nc):
    return {n: nc.dram_tensor(f"l1f_{n}", s, F32, kind="ExternalInput")
            for n, s in [("WrT", [D, D]), ("WkT", [D, D]), ("WvT", [D, D]),
                         ("mu2", [128, 16])]}


def declare_l3f_w(nc):
    return {"A": nc.dram_tensor("l3f_A", [D, D], F32, kind="ExternalInput"),
            "Bi": nc.dram_tensor("l3f_Bi", [D, D], F32, kind="ExternalInput"),
            "bias": nc.dram_tensor("l3f_bias", [1, D], F32, kind="ExternalInput")}


def prep_all(inputs):
    prep = {}
    prep.update(prep_l0_ffn(inputs))
    prep.update(prep_l1_ffn(inputs))
    prep.update(prep_l2_ffn(inputs))
    prep.update(prep_l3_ffn(inputs))
    for layer in range(4):
        prep.update(prep_attn(inputs, layer))
    return prep


def build_layer(layer, n_cores=8):
    nc = bass.Bass("TRN2", target_bir_lowering=False)
    x_in = nc.dram_tensor("x", [T, D], F32, kind="ExternalInput")
    x_out = nc.dram_tensor("xout", [T, D], F32, kind="ExternalOutput")
    names = []
    with ExitStack() as ctx:
        B = Buffers(nc, ctx)
        P = Prog(nc)
        E = Emit(nc, P, B, ctx)
        E.wcache = nc.dram_tensor("wcache", [1024, 128, 512], BF16, kind="Internal")
        st = nc.dram_tensor("attn_st", [16, 32, 128], BF16, kind="Internal")
        extra = [-float(np.pi)] if layer == 3 else []
        E.init_consts(extra_consts=extra)
        E.load_x(x_in)
        if layer == 0:
            E.cc_in = nc.dram_tensor("cc_in", [2, 8, 1], F32, kind="Internal")
            E.cc_out = nc.dram_tensor("cc_out", [2, 8, 1], F32, kind="Internal")
            Wf = declare_l0f_w(nc)
            Wa = declare_attn_w(nc, 0)
            names += ["l0f_gu", "l0f_dn", "l0f_gw", "l0f_sim",
                      "a0_Wqh", "a0_Wkh", "a0_Wv", "a0_Wo", "a0_bqk", "a0_bv", "a0_bo"]
            E.emit_ln(); emit_l0_ffn(E, Wf, n_cores=n_cores, cc_idx=0)
            E.emit_ln(); emit_attn(E, ATTN_CFGS[0], Wa, st)
            E.emit_ln(); emit_l0_ffn(E, Wf, n_cores=n_cores, cc_idx=1)
        elif layer == 1:
            Wf = declare_l1f_w(nc)
            Wa = declare_attn_w(nc, 1)
            names += ["l1f_WrT", "l1f_WkT", "l1f_WvT", "l1f_mu2",
                      "a1_Wqh", "a1_Wkh", "a1_Wv", "a1_Wo", "a1_bqk", "a1_bv", "a1_bo"]
            E.emit_ln(); E.emit_l1_ffn(Wf)
            E.emit_ln(); emit_attn(E, ATTN_CFGS[1], Wa, st)
            E.emit_ln(); E.emit_l1_ffn(Wf)
        elif layer == 2:
            Wf = declare_l2f_w(nc)
            Wa = declare_attn_w(nc, 2)
            names += ["l2f_w1", "l2f_w2", "l2f_w3", "l2f_b12", "l2f_b3", "l2f_gw", "l2f_gb",
                      "a2_Wqh", "a2_Wkh", "a2_Wv", "a2_Wo", "a2_bqk", "a2_bv", "a2_bo"]
            E.emit_ln(); emit_l2_ffn(E, Wf)
            E.emit_ln(); emit_attn(E, ATTN_CFGS[2], Wa, st)
            E.emit_ln(); emit_l2_ffn(E, Wf)
        else:
            Wf = declare_l3f_w(nc)
            Wa = declare_attn_w(nc, 3)
            names += ["l3f_A", "l3f_Bi", "l3f_bias",
                      "a3_Wqh", "a3_Wkh", "a3_Wv", "a3_Wo", "a3_qsc", "a3_ksc"]
            M_dram = nc.dram_tensor("l3f_M", [8, 128, 1024], BF16, kind="Internal")
            emit_l3_build_M(E, Wf, M_dram)
            E.emit_ln(); emit_l3_apply(E, Wf, M_dram, True)
            E.emit_ln(); emit_attn(E, ATTN_CFGS[3], Wa, st)
            E.emit_ln(); emit_l3_apply(E, Wf, M_dram, False)
        E.store_x(x_out)
        block = ctx.enter_context(nc.Block())
        P.replay(block)
    return nc, names




def build_full(n_cores=8):
    nc = bass.Bass("TRN2", target_bir_lowering=False)
    x_in = nc.dram_tensor("x", [T, D], F32, kind="ExternalInput")
    x_out = nc.dram_tensor("xout", [T, D], F32, kind="ExternalOutput")
    names = []
    with ExitStack() as ctx:
        B = Buffers(nc, ctx)
        P = Prog(nc)
        E = Emit(nc, P, B, ctx)
        E.wcache = nc.dram_tensor("wcache", [1024, 128, 512], BF16, kind="Internal")
        st = nc.dram_tensor("attn_st", [16, 32, 128], BF16, kind="Internal")
        E.cc_in = nc.dram_tensor("cc_in", [2, 8, 1], F32, kind="Internal")
        E.cc_out = nc.dram_tensor("cc_out", [2, 8, 1], F32, kind="Internal")
        M_dram = nc.dram_tensor("l3f_M", [8, 128, 1024], BF16, kind="Internal")
        E.init_consts(extra_consts=[-float(np.pi)])
        E.load_x(x_in)
        W0f = declare_l0f_w(nc)
        W0a = declare_attn_w(nc, 0)
        W1f = declare_l1f_w(nc)
        W1a = declare_attn_w(nc, 1)
        W2f = declare_l2f_w(nc)
        W2a = declare_attn_w(nc, 2)
        W3f = declare_l3f_w(nc)
        W3a = declare_attn_w(nc, 3)
        names += ["l0f_gu", "l0f_dn", "l0f_gw", "l0f_sim",
                  "a0_Wqh", "a0_Wkh", "a0_Wv", "a0_Wo", "a0_bqk", "a0_bv", "a0_bo",
                  "l1f_WrT", "l1f_WkT", "l1f_WvT", "l1f_mu2",
                  "a1_Wqh", "a1_Wkh", "a1_Wv", "a1_Wo", "a1_bqk", "a1_bv", "a1_bo",
                  "l2f_w1", "l2f_w2", "l2f_w3", "l2f_b12", "l2f_b3", "l2f_gw", "l2f_gb",
                  "a2_Wqh", "a2_Wkh", "a2_Wv", "a2_Wo", "a2_bqk", "a2_bv", "a2_bo",
                  "l3f_A", "l3f_Bi", "l3f_bias",
                  "a3_Wqh", "a3_Wkh", "a3_Wv", "a3_Wo", "a3_qsc", "a3_ksc"]
        E.emit_ln(); emit_l0_ffn(E, W0f, n_cores=n_cores, cc_idx=0)
        E.emit_ln(); emit_attn(E, ATTN_CFGS[0], W0a, st)
        E.emit_ln(); emit_l0_ffn(E, W0f, n_cores=n_cores, cc_idx=1)
        E.emit_ln(); E.emit_l1_ffn(W1f)
        E.emit_ln(); emit_attn(E, ATTN_CFGS[1], W1a, st)
        E.emit_ln(); E.emit_l1_ffn(W1f)
        E.emit_ln(); emit_l2_ffn(E, W2f)
        E.emit_ln(); emit_attn(E, ATTN_CFGS[2], W2a, st)
        E.emit_ln(); emit_l2_ffn(E, W2f)
        emit_l3_build_M(E, W3f, M_dram)
        E.emit_ln(); emit_l3_apply(E, W3f, M_dram, True)
        E.emit_ln(); emit_attn(E, ATTN_CFGS[3], W3a, st)
        E.emit_ln(); emit_l3_apply(E, W3f, M_dram, False)
        E.store_x(x_out)
        block = ctx.enter_context(nc.Block())
        P.replay(block)
    return nc, names


FUSED = True


def kernel(**inputs):
    prep = prep_all(inputs)
    x = np.ascontiguousarray(np.asarray(inputs["L0_attn_x"], np.float32))
    xs = [np.ascontiguousarray(x[i]) for i in range(NCORES)]
    if FUSED:
        nc, names = build_full(n_cores=NCORES)
        in_maps = [dict({"x": xs[i]}, **{k: prep[k] for k in names})
                   for i in range(NCORES)]
        res = run_bass_kernel_spmd(nc, in_maps, core_ids=list(range(NCORES)))
        xs = [np.asarray(res.results[i]["xout"], np.float32) for i in range(NCORES)]
        return np.stack(xs)
    for layer in range(4):
        nc, names = build_layer(layer, n_cores=NCORES)
        in_maps = [dict({"x": xs[i]}, **{k: prep[k] for k in names})
                   for i in range(NCORES)]
        res = run_bass_kernel_spmd(nc, in_maps, core_ids=list(range(NCORES)))
        xs = [np.asarray(res.results[i]["xout"], np.float32) for i in range(NCORES)]
    return np.stack(xs)



ALL_PREPS = None


def declare_l1f_w(nc):
    return {n: nc.dram_tensor(f"l1f_{n}", s, F32, kind="ExternalInput")
            for n, s in [("WrT", [D, D]), ("WkT", [D, D]), ("WvT", [D, D]),
                         ("mu2", [128, 16])]}


def declare_l3f_w(nc):
    return {"A": nc.dram_tensor("l3f_A", [D, D], F32, kind="ExternalInput"),
            "Bi": nc.dram_tensor("l3f_Bi", [D, D], F32, kind="ExternalInput"),
            "bias": nc.dram_tensor("l3f_bias", [1, D], F32, kind="ExternalInput")}


def prep_all(inputs):
    prep = {}
    prep.update(prep_l0_ffn(inputs))
    prep.update(prep_l1_ffn(inputs))
    prep.update(prep_l2_ffn(inputs))
    prep.update(prep_l3_ffn(inputs))
    for layer in range(4):
        prep.update(prep_attn(inputs, layer))
    return prep


def build_layer(layer, n_cores=8):
    nc = bass.Bass("TRN2", target_bir_lowering=False)
    x_in = nc.dram_tensor("x", [T, D], F32, kind="ExternalInput")
    x_out = nc.dram_tensor("xout", [T, D], F32, kind="ExternalOutput")
    names = []
    with ExitStack() as ctx:
        B = Buffers(nc, ctx)
        P = Prog(nc)
        E = Emit(nc, P, B, ctx)
        E.wcache = nc.dram_tensor("wcache", [1024, 128, 512], BF16, kind="Internal")
        st = nc.dram_tensor("attn_st", [16, 32, 128], BF16, kind="Internal")
        extra = [-float(np.pi)] if layer == 3 else []
        E.init_consts(extra_consts=extra)
        E.load_x(x_in)
        if layer == 0:
            E.cc_in = nc.dram_tensor("cc_in", [2, 8, 1], F32, kind="Internal")
            E.cc_out = nc.dram_tensor("cc_out", [2, 8, 1], F32, kind="Internal")
            Wf = declare_l0f_w(nc)
            Wa = declare_attn_w(nc, 0)
            names += ["l0f_gu", "l0f_dn", "l0f_gw", "l0f_sim",
                      "a0_Wqh", "a0_Wkh", "a0_Wv", "a0_Wo", "a0_bqk", "a0_bv", "a0_bo"]
            E.emit_ln(); emit_l0_ffn(E, Wf, n_cores=n_cores, cc_idx=0)
            E.emit_ln(); emit_attn(E, ATTN_CFGS[0], Wa, st)
            E.emit_ln(); emit_l0_ffn(E, Wf, n_cores=n_cores, cc_idx=1)
        elif layer == 1:
            Wf = declare_l1f_w(nc)
            Wa = declare_attn_w(nc, 1)
            names += ["l1f_WrT", "l1f_WkT", "l1f_WvT", "l1f_mu2",
                      "a1_Wqh", "a1_Wkh", "a1_Wv", "a1_Wo", "a1_bqk", "a1_bv", "a1_bo"]
            E.emit_ln(); E.emit_l1_ffn(Wf)
            E.emit_ln(); emit_attn(E, ATTN_CFGS[1], Wa, st)
            E.emit_ln(); E.emit_l1_ffn(Wf)
        elif layer == 2:
            Wf = declare_l2f_w(nc)
            Wa = declare_attn_w(nc, 2)
            names += ["l2f_w1", "l2f_w2", "l2f_w3", "l2f_b12", "l2f_b3", "l2f_gw", "l2f_gb",
                      "a2_Wqh", "a2_Wkh", "a2_Wv", "a2_Wo", "a2_bqk", "a2_bv", "a2_bo"]
            E.emit_ln(); emit_l2_ffn(E, Wf)
            E.emit_ln(); emit_attn(E, ATTN_CFGS[2], Wa, st)
            E.emit_ln(); emit_l2_ffn(E, Wf)
        else:
            Wf = declare_l3f_w(nc)
            Wa = declare_attn_w(nc, 3)
            names += ["l3f_A", "l3f_Bi", "l3f_bias",
                      "a3_Wqh", "a3_Wkh", "a3_Wv", "a3_Wo", "a3_qsc", "a3_ksc"]
            M_dram = nc.dram_tensor("l3f_M", [8, 128, 1024], BF16, kind="Internal")
            emit_l3_build_M(E, Wf, M_dram)
            E.emit_ln(); emit_l3_apply(E, Wf, M_dram, True)
            E.emit_ln(); emit_attn(E, ATTN_CFGS[3], Wa, st)
            E.emit_ln(); emit_l3_apply(E, Wf, M_dram, False)
        E.store_x(x_out)
        block = ctx.enter_context(nc.Block())
        P.replay(block)
    return nc, names




def build_full(n_cores=8):
    nc = bass.Bass("TRN2", target_bir_lowering=False)
    x_in = nc.dram_tensor("x", [T, D], F32, kind="ExternalInput")
    x_out = nc.dram_tensor("xout", [T, D], F32, kind="ExternalOutput")
    names = []
    with ExitStack() as ctx:
        B = Buffers(nc, ctx)
        P = Prog(nc)
        E = Emit(nc, P, B, ctx)
        E.wcache = nc.dram_tensor("wcache", [1024, 128, 512], BF16, kind="Internal")
        st = nc.dram_tensor("attn_st", [16, 32, 128], BF16, kind="Internal")
        E.cc_in = nc.dram_tensor("cc_in", [2, 8, 1], F32, kind="Internal")
        E.cc_out = nc.dram_tensor("cc_out", [2, 8, 1], F32, kind="Internal")
        M_dram = nc.dram_tensor("l3f_M", [8, 128, 1024], BF16, kind="Internal")
        E.init_consts(extra_consts=[-float(np.pi)])
        E.load_x(x_in)
        W0f = declare_l0f_w(nc)
        W0a = declare_attn_w(nc, 0)
        W1f = declare_l1f_w(nc)
        W1a = declare_attn_w(nc, 1)
        W2f = declare_l2f_w(nc)
        W2a = declare_attn_w(nc, 2)
        W3f = declare_l3f_w(nc)
        W3a = declare_attn_w(nc, 3)
        names += ["l0f_gu", "l0f_dn", "l0f_gw", "l0f_sim",
                  "a0_Wqh", "a0_Wkh", "a0_Wv", "a0_Wo", "a0_bqk", "a0_bv", "a0_bo",
                  "l1f_WrT", "l1f_WkT", "l1f_WvT", "l1f_mu2",
                  "a1_Wqh", "a1_Wkh", "a1_Wv", "a1_Wo", "a1_bqk", "a1_bv", "a1_bo",
                  "l2f_w1", "l2f_w2", "l2f_w3", "l2f_b12", "l2f_b3", "l2f_gw", "l2f_gb",
                  "a2_Wqh", "a2_Wkh", "a2_Wv", "a2_Wo", "a2_bqk", "a2_bv", "a2_bo",
                  "l3f_A", "l3f_Bi", "l3f_bias",
                  "a3_Wqh", "a3_Wkh", "a3_Wv", "a3_Wo", "a3_qsc", "a3_ksc"]
        E.emit_ln(); emit_l0_ffn(E, W0f, n_cores=n_cores, cc_idx=0)
        E.emit_ln(); emit_attn(E, ATTN_CFGS[0], W0a, st)
        E.emit_ln(); emit_l0_ffn(E, W0f, n_cores=n_cores, cc_idx=1)
        E.emit_ln(); E.emit_l1_ffn(W1f)
        E.emit_ln(); emit_attn(E, ATTN_CFGS[1], W1a, st)
        E.emit_ln(); E.emit_l1_ffn(W1f)
        E.emit_ln(); emit_l2_ffn(E, W2f)
        E.emit_ln(); emit_attn(E, ATTN_CFGS[2], W2a, st)
        E.emit_ln(); emit_l2_ffn(E, W2f)
        emit_l3_build_M(E, W3f, M_dram)
        E.emit_ln(); emit_l3_apply(E, W3f, M_dram, True)
        E.emit_ln(); emit_attn(E, ATTN_CFGS[3], W3a, st)
        E.emit_ln(); emit_l3_apply(E, W3f, M_dram, False)
        E.store_x(x_out)
        block = ctx.enter_context(nc.Block())
        P.replay(block)
    return nc, names


FUSED = True


def kernel(**inputs):
    prep = prep_all(inputs)
    x = np.ascontiguousarray(np.asarray(inputs["L0_attn_x"], np.float32))
    xs = [np.ascontiguousarray(x[i]) for i in range(NCORES)]
    if FUSED:
        nc, names = build_full(n_cores=NCORES)
        in_maps = [dict({"x": xs[i]}, **{k: prep[k] for k in names})
                   for i in range(NCORES)]
        res = run_bass_kernel_spmd(nc, in_maps, core_ids=list(range(NCORES)))
        xs = [np.asarray(res.results[i]["xout"], np.float32) for i in range(NCORES)]
        return np.stack(xs)
    for layer in range(4):
        nc, names = build_layer(layer, n_cores=NCORES)
        in_maps = [dict({"x": xs[i]}, **{k: prep[k] for k in names})
                   for i in range(NCORES)]
        res = run_bass_kernel_spmd(nc, in_maps, core_ids=list(range(NCORES)))
        xs = [np.asarray(res.results[i]["xout"], np.float32) for i in range(NCORES)]
    return np.stack(xs)



ALL_PREPS = None


def declare_l1f_w(nc):
    return {n: nc.dram_tensor(f"l1f_{n}", s, F32, kind="ExternalInput")
            for n, s in [("WrT", [D, D]), ("WkT", [D, D]), ("WvT", [D, D]),
                         ("mu2", [128, 16])]}


def declare_l3f_w(nc):
    return {"A": nc.dram_tensor("l3f_A", [D, D], F32, kind="ExternalInput"),
            "Bi": nc.dram_tensor("l3f_Bi", [D, D], F32, kind="ExternalInput"),
            "bias": nc.dram_tensor("l3f_bias", [1, D], F32, kind="ExternalInput")}


def prep_all(inputs):
    prep = {}
    prep.update(prep_l0_ffn(inputs))
    prep.update(prep_l1_ffn(inputs))
    prep.update(prep_l2_ffn(inputs))
    prep.update(prep_l3_ffn(inputs))
    for layer in range(4):
        prep.update(prep_attn(inputs, layer))
    return prep


def build_layer(layer, n_cores=8):
    nc = bass.Bass("TRN2", target_bir_lowering=False)
    x_in = nc.dram_tensor("x", [T, D], F32, kind="ExternalInput")
    x_out = nc.dram_tensor("xout", [T, D], F32, kind="ExternalOutput")
    names = []
    with ExitStack() as ctx:
        B = Buffers(nc, ctx)
        P = Prog(nc)
        E = Emit(nc, P, B, ctx)
        E.wcache = nc.dram_tensor("wcache", [1024, 128, 512], BF16, kind="Internal")
        E.wcache = nc.dram_tensor("wcache", [1024, 128, 512], BF16, kind="Internal")
        st = nc.dram_tensor("attn_st", [16, 32, 128], BF16, kind="Internal")
        extra = [-float(np.pi)] if layer == 3 else []
        E.init_consts(extra_consts=extra)
        E.load_x(x_in)
        if layer == 0:
            E.cc_in = nc.dram_tensor("cc_in", [2, 8, 1], F32, kind="Internal")
            E.cc_out = nc.dram_tensor("cc_out", [2, 8, 1], F32, kind="Internal")
            Wf = declare_l0f_w(nc)
            Wa = declare_attn_w(nc, 0)
            names += ["l0f_gu", "l0f_dn", "l0f_gw", "l0f_sim",
                      "a0_Wqh", "a0_Wkh", "a0_Wv", "a0_Wo", "a0_bqk", "a0_bv", "a0_bo"]
            E.emit_ln(); emit_l0_ffn(E, Wf, n_cores=n_cores, cc_idx=0)
            E.emit_ln(); emit_attn(E, ATTN_CFGS[0], Wa, st)
            E.emit_ln(); emit_l0_ffn(E, Wf, n_cores=n_cores, cc_idx=1)
        elif layer == 1:
            Wf = declare_l1f_w(nc)
            Wa = declare_attn_w(nc, 1)
            names += ["l1f_WrT", "l1f_WkT", "l1f_WvT", "l1f_mu2",
                      "a1_Wqh", "a1_Wkh", "a1_Wv", "a1_Wo", "a1_bqk", "a1_bv", "a1_bo"]
            E.emit_ln(); E.emit_l1_ffn(Wf)
            E.emit_ln(); emit_attn(E, ATTN_CFGS[1], Wa, st)
            E.emit_ln(); E.emit_l1_ffn(Wf)
        elif layer == 2:
            Wf = declare_l2f_w(nc)
            Wa = declare_attn_w(nc, 2)
            names += ["l2f_w1", "l2f_w2", "l2f_w3", "l2f_b12", "l2f_b3", "l2f_gw", "l2f_gb",
                      "a2_Wqh", "a2_Wkh", "a2_Wv", "a2_Wo", "a2_bqk", "a2_bv", "a2_bo"]
            E.emit_ln(); emit_l2_ffn(E, Wf)
            E.emit_ln(); emit_attn(E, ATTN_CFGS[2], Wa, st)
            E.emit_ln(); emit_l2_ffn(E, Wf)
        else:
            Wf = declare_l3f_w(nc)
            Wa = declare_attn_w(nc, 3)
            names += ["l3f_A", "l3f_Bi", "l3f_bias",
                      "a3_Wqh", "a3_Wkh", "a3_Wv", "a3_Wo", "a3_qsc", "a3_ksc"]
            M_dram = nc.dram_tensor("l3f_M", [8, 128, 1024], BF16, kind="Internal")
            emit_l3_build_M(E, Wf, M_dram)
            E.emit_ln(); emit_l3_apply(E, Wf, M_dram, True)
            E.emit_ln(); emit_attn(E, ATTN_CFGS[3], Wa, st)
            E.emit_ln(); emit_l3_apply(E, Wf, M_dram, False)
        E.store_x(x_out)
        block = ctx.enter_context(nc.Block())
        P.replay(block)
    return nc, names




def build_full(n_cores=8):
    nc = bass.Bass("TRN2", target_bir_lowering=False)
    x_in = nc.dram_tensor("x", [T, D], F32, kind="ExternalInput")
    x_out = nc.dram_tensor("xout", [T, D], F32, kind="ExternalOutput")
    names = []
    with ExitStack() as ctx:
        B = Buffers(nc, ctx)
        P = Prog(nc)
        E = Emit(nc, P, B, ctx)
        E.wcache = nc.dram_tensor("wcache", [1024, 128, 512], BF16, kind="Internal")
        E.wcache = nc.dram_tensor("wcache", [1024, 128, 512], BF16, kind="Internal")
        st = nc.dram_tensor("attn_st", [16, 32, 128], BF16, kind="Internal")
        E.cc_in = nc.dram_tensor("cc_in", [2, 8, 1], F32, kind="Internal")
        E.cc_out = nc.dram_tensor("cc_out", [2, 8, 1], F32, kind="Internal")
        M_dram = nc.dram_tensor("l3f_M", [8, 128, 1024], BF16, kind="Internal")
        E.init_consts(extra_consts=[-float(np.pi)])
        E.load_x(x_in)
        W0f = declare_l0f_w(nc)
        W0a = declare_attn_w(nc, 0)
        W1f = declare_l1f_w(nc)
        W1a = declare_attn_w(nc, 1)
        W2f = declare_l2f_w(nc)
        W2a = declare_attn_w(nc, 2)
        W3f = declare_l3f_w(nc)
        W3a = declare_attn_w(nc, 3)
        names += ["l0f_gu", "l0f_dn", "l0f_gw", "l0f_sim",
                  "a0_Wqh", "a0_Wkh", "a0_Wv", "a0_Wo", "a0_bqk", "a0_bv", "a0_bo",
                  "l1f_WrT", "l1f_WkT", "l1f_WvT", "l1f_mu2",
                  "a1_Wqh", "a1_Wkh", "a1_Wv", "a1_Wo", "a1_bqk", "a1_bv", "a1_bo",
                  "l2f_w1", "l2f_w2", "l2f_w3", "l2f_b12", "l2f_b3", "l2f_gw", "l2f_gb",
                  "a2_Wqh", "a2_Wkh", "a2_Wv", "a2_Wo", "a2_bqk", "a2_bv", "a2_bo",
                  "l3f_A", "l3f_Bi", "l3f_bias",
                  "a3_Wqh", "a3_Wkh", "a3_Wv", "a3_Wo", "a3_qsc", "a3_ksc"]
        E.emit_ln(); emit_l0_ffn(E, W0f, n_cores=n_cores, cc_idx=0)
        E.emit_ln(); emit_attn(E, ATTN_CFGS[0], W0a, st)
        E.emit_ln(); emit_l0_ffn(E, W0f, n_cores=n_cores, cc_idx=1)
        E.emit_ln(); E.emit_l1_ffn(W1f)
        E.emit_ln(); emit_attn(E, ATTN_CFGS[1], W1a, st)
        E.emit_ln(); E.emit_l1_ffn(W1f)
        E.emit_ln(); emit_l2_ffn(E, W2f)
        E.emit_ln(); emit_attn(E, ATTN_CFGS[2], W2a, st)
        E.emit_ln(); emit_l2_ffn(E, W2f)
        emit_l3_build_M(E, W3f, M_dram)
        E.emit_ln(); emit_l3_apply(E, W3f, M_dram, True)
        E.emit_ln(); emit_attn(E, ATTN_CFGS[3], W3a, st)
        E.emit_ln(); emit_l3_apply(E, W3f, M_dram, False)
        E.store_x(x_out)
        block = ctx.enter_context(nc.Block())
        P.replay(block)
    return nc, names


FUSED = True


def kernel(**inputs):
    prep = prep_all(inputs)
    x = np.ascontiguousarray(np.asarray(inputs["L0_attn_x"], np.float32))
    xs = [np.ascontiguousarray(x[i]) for i in range(NCORES)]
    if FUSED:
        nc, names = build_full(n_cores=NCORES)
        in_maps = [dict({"x": xs[i]}, **{k: prep[k] for k in names})
                   for i in range(NCORES)]
        res = run_bass_kernel_spmd(nc, in_maps, core_ids=list(range(NCORES)))
        xs = [np.asarray(res.results[i]["xout"], np.float32) for i in range(NCORES)]
        return np.stack(xs)
    for layer in range(4):
        nc, names = build_layer(layer, n_cores=NCORES)
        in_maps = [dict({"x": xs[i]}, **{k: prep[k] for k in names})
                   for i in range(NCORES)]
        res = run_bass_kernel_spmd(nc, in_maps, core_ids=list(range(NCORES)))
        xs = [np.asarray(res.results[i]["xout"], np.float32) for i in range(NCORES)]
    return np.stack(xs)



ALL_PREPS = None


def declare_l1f_w(nc):
    return {n: nc.dram_tensor(f"l1f_{n}", s, F32, kind="ExternalInput")
            for n, s in [("WrT", [D, D]), ("WkT", [D, D]), ("WvT", [D, D]),
                         ("mu2", [128, 16])]}


def declare_l3f_w(nc):
    return {"A": nc.dram_tensor("l3f_A", [D, D], F32, kind="ExternalInput"),
            "Bi": nc.dram_tensor("l3f_Bi", [D, D], F32, kind="ExternalInput"),
            "bias": nc.dram_tensor("l3f_bias", [1, D], F32, kind="ExternalInput")}


def prep_all(inputs):
    prep = {}
    prep.update(prep_l0_ffn(inputs))
    prep.update(prep_l1_ffn(inputs))
    prep.update(prep_l2_ffn(inputs))
    prep.update(prep_l3_ffn(inputs))
    for layer in range(4):
        prep.update(prep_attn(inputs, layer))
    return prep


def build_layer(layer, n_cores=8):
    nc = bass.Bass("TRN2", target_bir_lowering=False)
    x_in = nc.dram_tensor("x", [T, D], F32, kind="ExternalInput")
    x_out = nc.dram_tensor("xout", [T, D], F32, kind="ExternalOutput")
    names = []
    with ExitStack() as ctx:
        B = Buffers(nc, ctx)
        P = Prog(nc)
        E = Emit(nc, P, B, ctx)
        E.wcache = nc.dram_tensor("wcache", [1024, 128, 512], BF16, kind="Internal")
        st = nc.dram_tensor("attn_st", [16, 32, 128], BF16, kind="Internal")
        extra = [-float(np.pi)] if layer == 3 else []
        E.init_consts(extra_consts=extra)
        E.load_x(x_in)
        if layer == 0:
            E.cc_in = nc.dram_tensor("cc_in", [2, 8, 1], F32, kind="Internal")
            E.cc_out = nc.dram_tensor("cc_out", [2, 8, 1], F32, kind="Internal")
            Wf = declare_l0f_w(nc)
            Wa = declare_attn_w(nc, 0)
            names += ["l0f_gu", "l0f_dn", "l0f_gw", "l0f_sim",
                      "a0_Wqh", "a0_Wkh", "a0_Wv", "a0_Wo", "a0_bqk", "a0_bv", "a0_bo"]
            E.emit_ln(); emit_l0_ffn(E, Wf, n_cores=n_cores, cc_idx=0)
            E.emit_ln(); emit_attn(E, ATTN_CFGS[0], Wa, st)
            E.emit_ln(); emit_l0_ffn(E, Wf, n_cores=n_cores, cc_idx=1)
        elif layer == 1:
            Wf = declare_l1f_w(nc)
            Wa = declare_attn_w(nc, 1)
            names += ["l1f_WrT", "l1f_WkT", "l1f_WvT", "l1f_mu2",
                      "a1_Wqh", "a1_Wkh", "a1_Wv", "a1_Wo", "a1_bqk", "a1_bv", "a1_bo"]
            E.emit_ln(); E.emit_l1_ffn(Wf)
            E.emit_ln(); emit_attn(E, ATTN_CFGS[1], Wa, st)
            E.emit_ln(); E.emit_l1_ffn(Wf)
        elif layer == 2:
            Wf = declare_l2f_w(nc)
            Wa = declare_attn_w(nc, 2)
            names += ["l2f_w1", "l2f_w2", "l2f_w3", "l2f_b12", "l2f_b3", "l2f_gw", "l2f_gb",
                      "a2_Wqh", "a2_Wkh", "a2_Wv", "a2_Wo", "a2_bqk", "a2_bv", "a2_bo"]
            E.emit_ln(); emit_l2_ffn(E, Wf)
            E.emit_ln(); emit_attn(E, ATTN_CFGS[2], Wa, st)
            E.emit_ln(); emit_l2_ffn(E, Wf)
        else:
            Wf = declare_l3f_w(nc)
            Wa = declare_attn_w(nc, 3)
            names += ["l3f_A", "l3f_Bi", "l3f_bias",
                      "a3_Wqh", "a3_Wkh", "a3_Wv", "a3_Wo", "a3_qsc", "a3_ksc"]
            M_dram = nc.dram_tensor("l3f_M", [8, 128, 1024], BF16, kind="Internal")
            emit_l3_build_M(E, Wf, M_dram)
            E.emit_ln(); emit_l3_apply(E, Wf, M_dram, True)
            E.emit_ln(); emit_attn(E, ATTN_CFGS[3], Wa, st)
            E.emit_ln(); emit_l3_apply(E, Wf, M_dram, False)
        E.store_x(x_out)
        block = ctx.enter_context(nc.Block())
        P.replay(block)
    return nc, names




def build_full(n_cores=8):
    nc = bass.Bass("TRN2", target_bir_lowering=False)
    x_in = nc.dram_tensor("x", [T, D], F32, kind="ExternalInput")
    x_out = nc.dram_tensor("xout", [T, D], F32, kind="ExternalOutput")
    names = []
    with ExitStack() as ctx:
        B = Buffers(nc, ctx)
        P = Prog(nc)
        E = Emit(nc, P, B, ctx)
        E.wcache = nc.dram_tensor("wcache", [1024, 128, 512], BF16, kind="Internal")
        st = nc.dram_tensor("attn_st", [16, 32, 128], BF16, kind="Internal")
        E.cc_in = nc.dram_tensor("cc_in", [2, 8, 1], F32, kind="Internal")
        E.cc_out = nc.dram_tensor("cc_out", [2, 8, 1], F32, kind="Internal")
        M_dram = nc.dram_tensor("l3f_M", [8, 128, 1024], BF16, kind="Internal")
        E.init_consts(extra_consts=[-float(np.pi)])
        E.load_x(x_in)
        W0f = declare_l0f_w(nc)
        W0a = declare_attn_w(nc, 0)
        W1f = declare_l1f_w(nc)
        W1a = declare_attn_w(nc, 1)
        W2f = declare_l2f_w(nc)
        W2a = declare_attn_w(nc, 2)
        W3f = declare_l3f_w(nc)
        W3a = declare_attn_w(nc, 3)
        names += ["l0f_gu", "l0f_dn", "l0f_gw", "l0f_sim",
                  "a0_Wqh", "a0_Wkh", "a0_Wv", "a0_Wo", "a0_bqk", "a0_bv", "a0_bo",
                  "l1f_WrT", "l1f_WkT", "l1f_WvT", "l1f_mu2",
                  "a1_Wqh", "a1_Wkh", "a1_Wv", "a1_Wo", "a1_bqk", "a1_bv", "a1_bo",
                  "l2f_w1", "l2f_w2", "l2f_w3", "l2f_b12", "l2f_b3", "l2f_gw", "l2f_gb",
                  "a2_Wqh", "a2_Wkh", "a2_Wv", "a2_Wo", "a2_bqk", "a2_bv", "a2_bo",
                  "l3f_A", "l3f_Bi", "l3f_bias",
                  "a3_Wqh", "a3_Wkh", "a3_Wv", "a3_Wo", "a3_qsc", "a3_ksc"]
        E.emit_ln(); emit_l0_ffn(E, W0f, n_cores=n_cores, cc_idx=0)
        E.emit_ln(); emit_attn(E, ATTN_CFGS[0], W0a, st)
        E.emit_ln(); emit_l0_ffn(E, W0f, n_cores=n_cores, cc_idx=1)
        E.emit_ln(); E.emit_l1_ffn(W1f)
        E.emit_ln(); emit_attn(E, ATTN_CFGS[1], W1a, st)
        E.emit_ln(); E.emit_l1_ffn(W1f)
        E.emit_ln(); emit_l2_ffn(E, W2f)
        E.emit_ln(); emit_attn(E, ATTN_CFGS[2], W2a, st)
        E.emit_ln(); emit_l2_ffn(E, W2f)
        emit_l3_build_M(E, W3f, M_dram)
        E.emit_ln(); emit_l3_apply(E, W3f, M_dram, True)
        E.emit_ln(); emit_attn(E, ATTN_CFGS[3], W3a, st)
        E.emit_ln(); emit_l3_apply(E, W3f, M_dram, False)
        E.store_x(x_out)
        block = ctx.enter_context(nc.Block())
        P.replay(block)
    return nc, names


FUSED = True


def kernel(**inputs):
    prep = prep_all(inputs)
    x = np.ascontiguousarray(np.asarray(inputs["L0_attn_x"], np.float32))
    xs = [np.ascontiguousarray(x[i]) for i in range(NCORES)]
    if FUSED:
        nc, names = build_full(n_cores=NCORES)
        in_maps = [dict({"x": xs[i]}, **{k: prep[k] for k in names})
                   for i in range(NCORES)]
        res = run_bass_kernel_spmd(nc, in_maps, core_ids=list(range(NCORES)))
        xs = [np.asarray(res.results[i]["xout"], np.float32) for i in range(NCORES)]
        return np.stack(xs)
    for layer in range(4):
        nc, names = build_layer(layer, n_cores=NCORES)
        in_maps = [dict({"x": xs[i]}, **{k: prep[k] for k in names})
                   for i in range(NCORES)]
        res = run_bass_kernel_spmd(nc, in_maps, core_ids=list(range(NCORES)))
        xs = [np.asarray(res.results[i]["xout"], np.float32) for i in range(NCORES)]
    return np.stack(xs)
